# version 14
# baseline (speedup 1.0000x reference)
import contextlib

import numpy as np
import ml_dtypes

import concourse.bass as bass
import concourse.mybir as mybir
from concourse.bass_utils import run_bass_kernel_spmd

F32 = mybir.dt.float32
BF16 = mybir.dt.bfloat16
I32 = mybir.dt.int32
AF = mybir.ActivationFunctionType
ALU = mybir.AluOpType

D = 1024
DC = 8
E16 = 16
TT = 512
NCH = 4
ENGINES = ("sync", "tensor", "vector", "scalar", "gpsimd")


class Sem:
    __slots__ = ("h", "n", "name")

    def __init__(self, h, name):
        self.h = h
        self.n = 0
        self.name = name


class Builder:
    def __init__(self, nc):
        self.nc = nc
        self.ops = {e: [] for e in ENGINES}
        self.waited = {}

    def rec(self, eng, fn, waits=(), incs=()):
        w = [(s.h, int(v)) for (s, v) in waits]
        i = [(s.h, int(a)) for (s, a) in incs]
        assert fn is not None or not i
        for s, a in incs:
            s.n += a
        self.ops[eng].append((fn, w, i))

    def flush(self):
        blk_ops = self.ops
        self.ops = {e: [] for e in ENGINES}
        waited = self.waited
        with self.nc.Block() as block:
            for eng_name in ENGINES:
                entries = blk_ops[eng_name]
                if not entries:
                    continue

                def body(e, entries=entries, eng_name=eng_name):
                    for fn, waits, incs in entries:
                        for (h, v) in waits:
                            key = (eng_name, h.num)
                            if waited.get(key, -1) < v:
                                e.wait_ge(h, v)
                                waited[key] = v
                        if fn is None:
                            continue
                        ins = fn(e)
                        for (h, a) in incs:
                            if a == 16:
                                ins.then_inc(h, a)
                            else:
                                e.maybe_drain_then_inc((h, a), fusable=True)

                getattr(block, eng_name)(body)


def build_program(nt):
    S = nt * TT
    nc = bass.Bass("TRN2", target_bir_lowering=False, detect_race_conditions=False)

    x_d = nc.dram_tensor("x", [S, D], F32, kind="ExternalInput").ap()
    y_d = nc.dram_tensor("y", [S, D], F32, kind="ExternalOutput").ap()
    wq_d = nc.dram_tensor("wq", [D, D], F32, kind="ExternalInput").ap()
    wk_d = nc.dram_tensor("wk", [D, D], F32, kind="ExternalInput").ap()
    wv_d = nc.dram_tensor("wv", [D, D], F32, kind="ExternalInput").ap()
    wg_d = nc.dram_tensor("wg", [D, D], F32, kind="ExternalInput").ap()
    wo_d = nc.dram_tensor("wo", [D, D], F32, kind="ExternalInput").ap()
    bq_d = nc.dram_tensor("bq", [D], F32, kind="ExternalInput").ap()
    bk_d = nc.dram_tensor("bk", [D], F32, kind="ExternalInput").ap()
    bv_d = nc.dram_tensor("bv", [D], F32, kind="ExternalInput").ap()
    bg_d = nc.dram_tensor("bg", [D], F32, kind="ExternalInput").ap()
    bo_d = nc.dram_tensor("bo", [D], F32, kind="ExternalInput").ap()
    es_d = nc.dram_tensor("es", [E16, D], F32, kind="ExternalInput").ap()
    keys_d = nc.dram_tensor("keys", [E16, D, 128], F32, kind="ExternalInput").ap()
    vals_d = nc.dram_tensor("vals", [E16, 128, D], F32, kind="ExternalInput").ap()
    x1_d = nc.dram_tensor("x1s", [S, D], F32, kind="Internal").ap()
    ident_d = nc.inline_tensor(np.eye(128, dtype=ml_dtypes.bfloat16), name="ident_dr")

    B = Builder(nc)
    sIN = Sem(nc.alloc_semaphore("sIN"), "sIN")
    sOUT = Sem(nc.alloc_semaphore("sOUT"), "sOUT")
    sT = Sem(nc.alloc_semaphore("sT"), "sT")
    sV = Sem(nc.alloc_semaphore("sV"), "sV")
    sS = Sem(nc.alloc_semaphore("sS"), "sS")
    sG = Sem(nc.alloc_semaphore("sG"), "sG")
    sCFG = Sem(nc.alloc_semaphore("sCFG"), "sCFG")
    sWA = Sem(nc.alloc_semaphore("sWA"), "sWA")
    sWB = Sem(nc.alloc_semaphore("sWB"), "sWB")
    sWC = Sem(nc.alloc_semaphore("sWC"), "sWC")
    sWD = Sem(nc.alloc_semaphore("sWD"), "sWD")
    sWE = Sem(nc.alloc_semaphore("sWE"), "sWE")
    sWF = Sem(nc.alloc_semaphore("sWF"), "sWF")

    em = contextlib.ExitStack()
    E = em.enter_context
    ident = E(nc.sbuf_tensor("ident", [128, 128], BF16))
    ones = E(nc.sbuf_tensor("ones", [65, 128], BF16))
    brow_bo = E(nc.sbuf_tensor("brow_bo", [1, D], BF16))
    bcols = E(nc.sbuf_tensor("bcols", [128, 32], F32))
    keys_sb = E(nc.sbuf_tensor("keys_sb", [128, DC, 2048], BF16))
    vals_sb = E(nc.sbuf_tensor("vals_sb", [128, E16, D], BF16))
    es_sb = E(nc.sbuf_tensor("es_sb", [128, DC, E16], BF16))
    sa = E(nc.sbuf_tensor("sa", [128, NCH, D], F32))
    sb = E(nc.sbuf_tensor("sb", [128, NCH, D], F32))
    ln_tm = E(nc.sbuf_tensor("ln_tm", [128, 2, D], BF16))
    ln_fm = E(nc.sbuf_tensor("ln_fm", [128, DC, TT], BF16))
    sel_bf = E(nc.sbuf_tensor("sel_bf", [E16, TT], BF16))
    sel_row = E(nc.sbuf_tensor("sel_row", [65, 6 * TT], BF16))
    selb = E(nc.sbuf_tensor("selb", [128, 2, TT], BF16))
    stile = E(nc.sbuf_tensor("stile", [128, E16, TT], BF16))
    scr1 = E(nc.sbuf_tensor("scr1", [128, D], F32))
    bnb0 = E(nc.sbuf_tensor("bnb0", [128, 8, 6], F32))
    st40 = E(nc.sbuf_tensor("st40", [128, 28], F32))
    bnb1 = E(nc.sbuf_tensor("bnb1", [128, 8, 6], F32))
    st41 = E(nc.sbuf_tensor("st41", [128, 28], F32))
    m1a = E(nc.sbuf_tensor("m1a", [128, NCH * nt], F32))
    r1a = E(nc.sbuf_tensor("r1a", [128, NCH * nt], F32))
    qt = E(nc.sbuf_tensor("qt", [128, DC, TT], BF16))
    th = E(nc.sbuf_tensor("th", [128, 2, TT], BF16))
    kvbd = E(nc.sbuf_tensor("kvbd", [128, DC, 128], BF16))
    pT1 = E(nc.psum_tensor("pT1", [128, 1024], BF16))
    pT2 = E(nc.psum_tensor("pT2", [128, 1024], BF16))
    pA = E(nc.psum_tensor("pA", [128, 512], F32))
    pB = E(nc.psum_tensor("pB", [128, 512], F32))
    pC = E(nc.psum_tensor("pC", [128, 512], F32))
    pD = E(nc.psum_tensor("pD", [128, 512], F32))

    ev = {}

    def mark(name, sem):
        ev[name] = (sem, sem.n)

    def w(name):
        return ev[name]

    def rec_startup():
        B.rec("scalar", lambda e: e.preload_act_table(AF.Exp))
        for c in range(NCH):
            B.rec("sync", lambda e, c=c: e.dma_start(out=sa[:, c, :], in_=x_d[c * 128:(c + 1) * 128, :]),
                  incs=[(sIN, 16)])
            mark(f"xin_0_{c}", sIN)
        B.rec("sync", lambda e: e.dma_start(out=ident[:], in_=ident_d.ap()), incs=[(sCFG, 16)])
        mark("ident_in", sCFG)
        B.rec("sync", lambda e: e.dma_start(out=bcols[:, 0:8], in_=bq_d.rearrange("(c p) -> p c", p=128), allow_slow_non_contiguous=True),
              incs=[(sCFG, 16)])
        B.rec("sync", lambda e: e.dma_start(out=bcols[:, 16:24], in_=bg_d.rearrange("(c p) -> p c", p=128), allow_slow_non_contiguous=True),
              incs=[(sCFG, 16)])
        B.rec("sync", lambda e: e.dma_start(out=scr1[0:1, 0:D], in_=bo_d.unsqueeze(0)), incs=[(sCFG, 16)])
        mark("bo_in", sCFG)
        B.rec("vector", lambda e: e.memset(ones[:, :], 1.0))
        B.rec("vector", lambda e: e.tensor_scalar_mul(bcols[:, 8:16], bcols[:, 0:8], -1.0),
              waits=[w("bo_in")])
        B.rec("vector", lambda e: e.tensor_scalar_mul(bcols[:, 16:24], bcols[:, 16:24], 0.5))
        B.rec("vector", lambda e: e.tensor_scalar_mul(scr1[0:1, 0:D], scr1[0:1, 0:D], 2.0))
        B.rec("vector", lambda e: e.tensor_copy(brow_bo[0:1, 0:D], scr1[0:1, 0:D]), incs=[(sV, 1)])
        mark("cfg_done", sV)
        mark("ones_done", sV)
        mark("bcols_done", sV)
        for dc in range(DC):
            B.rec("gpsimd", lambda e, dc=dc: e.dma_start(
                out=keys_sb[:, dc, :],
                in_=keys_d[:, dc * 128:(dc + 1) * 128, :].transpose([1, 0, 2])),
                incs=[(sWA, 16)])
        mark("keys_done", sWA)
        for dc in range(DC):
            B.rec("gpsimd", lambda e, dc=dc: e.dma_start(
                out=es_sb[:, dc, :],
                in_=es_d[:, dc * 128:(dc + 1) * 128].transpose([1, 0]),
                allow_slow_non_contiguous=True),
                incs=[(sWA, 16)])
        mark("es_done", sWA)
        for e_ in range(E16):
            B.rec("gpsimd", lambda e, e_=e_: e.dma_start(
                out=vals_sb[:, e_, :], in_=vals_d[e_]), incs=[(sWB, 16)])
        mark("vals_done", sWB)
        for name, wd, wsb in (("wk", wk_d, wk_sb), ("wv", wv_d, wv_sb)):
            for dc in range(DC):
                B.rec("gpsimd", lambda e, wd=wd, dc=dc, wsb=wsb: e.dma_start(
                    out=wsb[:, dc, :], in_=wd[dc * 128:(dc + 1) * 128, :]),
                    incs=[(sWC, 16)])
        B.rec("gpsimd", lambda e: e.dma_start(out=brows_kv[0:1, 0:D], in_=bk_d.unsqueeze(0)),
              incs=[(sWC, 16)])
        B.rec("gpsimd", lambda e: e.dma_start(out=brows_kv[0:1, D:2 * D], in_=bv_d.unsqueeze(0)),
              incs=[(sWC, 16)])
        mark("wkv_done", sWC)

    def rec_bn(src, c, bufs, waits=()):
        bnb, g = bufs
        B.rec("vector", lambda e, c=c: e.bn_stats(
            bnb[:, 2 * c, :], src[:, c, 0:512]), waits=list(waits))
        B.rec("vector", lambda e, c=c: e.bn_stats(
            bnb[:, 2 * c + 1, :], src[:, c, 512:1024]))

    def rec_ln_finish(t, which, bufs):
        bnb, g = bufs
        V = "vector"

        def RD(fn):
            B.rec(V, fn)
            B.rec(V, lambda e: e.drain())

        B.rec(V, lambda e: e.drain())
        for c in range(NCH):
            B.rec(V, lambda e, c=c: e.bn_aggr(g[:, 2 * c:2 * c + 2], bnb[:, 2 * c:2 * c + 2, :]))
        B.rec(V, lambda e: e.drain())
        mvv = g.ap()[:, 1:8:2]
        us = g.ap()[:, 12:16]
        Bs = g.ap()[:, 16:20]
        Cs = g.ap()[:, 20:24]
        ys = g.ap()[:, 24:28]
        rs = g.ap()[:, 8:12]
        RD(lambda e: e.tensor_scalar_add(us, mvv, 1e-5))
        RD(lambda e: e.tensor_scalar(Bs.bitcast(I32), us.bitcast(I32), 1, None, ALU.arith_shift_right))
        RD(lambda e: e.tensor_scalar(Cs.bitcast(I32), Bs.bitcast(I32), -1, 0x5F3759DF, ALU.mult, ALU.add))
        for it in range(3):
            yin = Cs if it % 2 == 0 else ys
            yout = ys if it % 2 == 0 else Cs
            last = it == 2
            RD(lambda e, yin=yin: e.tensor_tensor(Bs, yin, yin, ALU.mult))
            RD(lambda e: e.tensor_tensor(rs, Bs, us, ALU.mult))
            RD(lambda e: e.tensor_scalar(Bs, rs, -0.5, 1.5, ALU.mult, ALU.add))
            RD(lambda e, yin=yin, yout=(rs if last else yout): e.tensor_tensor(yout, yin, Bs, ALU.mult))
        if which == "ln1":
            B.rec(V, lambda e, t=t: e.tensor_copy(m1a[:, NCH * t:NCH * (t + 1)], g.ap()[:, 0:8:2]))
            B.rec(V, lambda e, t=t: e.tensor_copy(r1a[:, NCH * t:NCH * (t + 1)], rs))
            B.rec(V, lambda e: e.drain())

    def rec_ln_finish_pair(t, which, bufs, p):
        bnb, g = bufs
        V = "vector"

        def RD(fn):
            B.rec(V, fn)
            B.rec(V, lambda e: e.drain())

        B.rec(V, lambda e: e.drain())
        for c in (2 * p, 2 * p + 1):
            B.rec(V, lambda e, c=c: e.bn_aggr(g[:, 2 * c:2 * c + 2], bnb[:, 2 * c:2 * c + 2, :]))
        B.rec(V, lambda e: e.drain())
        o = 2 * p
        mvv = g.ap()[:, 1 + 4 * p:8:2][:, 0:2]
        us = g.ap()[:, 12 + o:14 + o]
        Bs = g.ap()[:, 16 + o:18 + o]
        Cs = g.ap()[:, 20 + o:22 + o]
        ys = g.ap()[:, 24 + o:26 + o]
        rs = g.ap()[:, 8 + o:10 + o]
        RD(lambda e: e.tensor_scalar_add(us, mvv, 1e-5))
        RD(lambda e: e.tensor_scalar(Bs.bitcast(I32), us.bitcast(I32), 1, None, ALU.arith_shift_right))
        RD(lambda e: e.tensor_scalar(Cs.bitcast(I32), Bs.bitcast(I32), -1, 0x5F3759DF, ALU.mult, ALU.add))
        for it in range(3):
            yin = Cs if it % 2 == 0 else ys
            yout = ys if it % 2 == 0 else Cs
            last = it == 2
            RD(lambda e, yin=yin: e.tensor_tensor(Bs, yin, yin, ALU.mult))
            RD(lambda e: e.tensor_tensor(rs, Bs, us, ALU.mult))
            RD(lambda e: e.tensor_scalar(Bs, rs, -0.5, 1.5, ALU.mult, ALU.add))
            RD(lambda e, yin=yin, yout=(rs if last else yout): e.tensor_tensor(yout, yin, Bs, ALU.mult))
        if which == "ln1":
            B.rec(V, lambda e, t=t, p=p: e.tensor_copy(
                m1a[:, NCH * t + 2 * p:NCH * t + 2 * p + 2], g.ap()[:, 4 * p:4 * p + 4:2]))
            B.rec(V, lambda e, t=t, p=p: e.tensor_copy(
                r1a[:, NCH * t + 2 * p:NCH * t + 2 * p + 2], rs))
            B.rec(V, lambda e: e.drain())

    def rec_ln_apply_T_pair(t, src, which, g, p):
        for c in (2 * p, 2 * p + 1):
            mc, rc = g.ap()[:, 2 * c:2 * c + 1], g.ap()[:, 8 + c:9 + c]
            awaits = []
            if ev.get(f"ln_tm_free_{c % 2}"):
                awaits.append(ev[f"ln_tm_free_{c % 2}"])
            B.rec("vector", lambda e, c=c, mc=mc, rc=rc: e.tensor_scalar(
                ln_tm[:, c % 2, :], src[:, c, :], mc, rc, ALU.subtract, ALU.mult),
                waits=awaits, incs=[(sV, 1)])
            mark(f"{which}_tm_{t}_{c}", sV)
        for c in (2 * p, 2 * p + 1):
            pT = pT1 if c % 2 == 0 else pT2
            waits = [w(f"{which}_tm_{t}_{c}"), w("ident_in")]
            if ev.get(f"Tevac_prev_{c % 2}"):
                waits.append(ev[f"Tevac_prev_{c % 2}"])
            if ev.get("pT_busy"):
                waits.append(ev["pT_busy"])
            for j in range(DC):
                B.rec("tensor", lambda e, c=c, j=j, pT=pT: e.transpose(
                    pT[:, j * 128:(j + 1) * 128], ln_tm[:, c % 2, j * 128:(j + 1) * 128], ident[:]),
                    waits=waits if j == 0 else [], incs=[(sT, 1)] if j == DC - 1 else [])
            mark(f"{which}_T_{t}_{c}", sT)
            ev[f"ln_tm_free_{c % 2}"] = (sT, sT.n)
            B.rec("scalar", lambda e, c=c, pT=pT: e.copy(
                ln_fm.ap()[:, :, c * 128:(c + 1) * 128],
                pT.ap().rearrange("p (j k) -> p j k", j=DC)),
                waits=[w(f"{which}_T_{t}_{c}")], incs=[(sS, 1)])
            mark(f"{which}_fm_{t}_{c}", sS)
            ev[f"Tevac_prev_{c % 2}"] = (sS, sS.n)
        if p == 1:
            mark(f"{which}_fm_{t}", sS)

    def rec_ln_apply_transpose(t, src, which, mcol=None, rcol=None, g=None):
        for c in range(NCH):
            if mcol is None:
                mc, rc = g.ap()[:, 2 * c:2 * c + 1], g.ap()[:, 8 + c:9 + c]
            else:
                mc, rc = mcol(c), rcol(c)
            awaits = []
            if ev.get(f"ln_tm_free_{c % 2}"):
                awaits.append(ev[f"ln_tm_free_{c % 2}"])
            B.rec("vector", lambda e, c=c, mc=mc, rc=rc: e.tensor_scalar(
                ln_tm[:, c % 2, :], src[:, c, :], mc, rc, ALU.subtract, ALU.mult),
                waits=awaits, incs=[(sV, 1)])
            mark(f"{which}_tm_{t}_{c}", sV)
            pT = pT1 if c % 2 == 0 else pT2
            waits = [w(f"{which}_tm_{t}_{c}"), w("ident_in")]
            if ev.get(f"Tevac_prev_{c % 2}"):
                waits.append(ev[f"Tevac_prev_{c % 2}"])
            if ev.get("pT_busy"):
                waits.append(ev["pT_busy"])
            for j in range(DC):
                B.rec("tensor", lambda e, c=c, j=j, pT=pT: e.transpose(
                    pT[:, j * 128:(j + 1) * 128], ln_tm[:, c % 2, j * 128:(j + 1) * 128], ident[:]),
                    waits=waits if j == 0 else [], incs=[(sT, 1)] if j == DC - 1 else [])
            mark(f"{which}_T_{t}_{c}", sT)
            ev[f"ln_tm_free_{c % 2}"] = (sT, sT.n)
            B.rec("scalar", lambda e, c=c, pT=pT: e.copy(
                ln_fm.ap()[:, :, c * 128:(c + 1) * 128],
                pT.ap().rearrange("p (j k) -> p j k", j=DC)),
                waits=[w(f"{which}_T_{t}_{c}")], incs=[(sS, 1)])
            mark(f"{which}_fm_{t}_{c}", sS)
            ev[f"Tevac_prev_{c % 2}"] = (sS, sS.n)
        mark(f"{which}_fm_{t}", sS)

    def rec_ffn(t, which, lnname, resid_src, out_dst, pass_no, post_evac=None):
        waits = [w(f"{lnname}_fm_{t}"), w("es_done"), w("ident_in")]
        if ev.get("pa_free"):
            waits.append(ev["pa_free"])
        for dc in range(DC):
            B.rec("tensor", lambda e, dc=dc: e.matmul(
                pA[0:16, :], es_sb[:, dc, :], ln_fm[:, dc, :],
                start=(dc == 0), stop=(dc == DC - 1)),
                waits=waits if dc == 0 else [], incs=[(sT, 1)] if dc == DC - 1 else [])
        mark(f"sel_mm_{which}_{t}", sT)
        B.rec("scalar", lambda e: e.activation(scr1[0:16, 0:TT], pA[0:16, :], AF.Tanh, scale=0.5),
              waits=[w(f"sel_mm_{which}_{t}")], incs=[(sS, 1)])
        mark(f"sel_th_{which}_{t}", sS)
        B.rec("vector", lambda e: e.tensor_scalar(
            sel_bf[:, :], scr1[0:16, 0:TT], 0.5, 0.5, ALU.mult, ALU.add),
            waits=[w(f"sel_th_{which}_{t}")], incs=[(sV, 1)])
        mark(f"sel_{which}_{t}", sV)
        srw = [w(f"sel_{which}_{t}")]
        if ev.get("sel_row_free"):
            srw.append(ev["sel_row_free"])
        B.rec("gpsimd", lambda e: e.dma_start(out=sel_row[0:1, 0:6 * TT], in_=sel_bf[0:6, :]),
              waits=srw, incs=[(sOUT, 16)])
        B.rec("gpsimd", lambda e: e.dma_start(out=sel_row[32:33, 0:6 * TT], in_=sel_bf[6:12, :]),
              incs=[(sOUT, 16)])
        B.rec("gpsimd", lambda e: e.dma_start(out=sel_row[64:65, 0:4 * TT], in_=sel_bf[12:16, :]),
              incs=[(sOUT, 16)])
        mark(f"selrow_{which}_{t}", sOUT)
        nb = 2 if pass_no == 1 else 4
        banks = (pA, pB) if pass_no == 1 else (pA, pB, pE, pF)
        for e_ in range(E16):
            par = e_ % 2
            pSB = (pT1 if par == 0 else pT2).ap().bitcast(F32)[:, 0:512]
            sbs = selb.ap()[:, par, :]
            waits = [w(f"selrow_{which}_{t}"), w("ones_done")]
            if ev.get(f"selb_evac_{par}"):
                waits.append(ev[f"selb_evac_{par}"])
            sp = 32 * (e_ // 6) if e_ < 12 else 64
            so = (e_ % 6) * TT if e_ < 12 else (e_ - 12) * TT
            B.rec("tensor", lambda e, pSB=pSB, sp=sp, so=so: e.matmul(
                pSB, ones[sp:sp + 1, :], sel_row[sp:sp + 1, so:so + TT],
                start=True, stop=True),
                waits=waits, incs=[(sT, 1)])
            mark(f"selb_mm_{which}_{t}_{e_}", sT)
            ewaits = [w(f"selb_mm_{which}_{t}_{e_}")]
            if e_ >= 2:
                ewaits.append(w(f"stile_{which}_{t}_{e_ - 2}"))
            B.rec("scalar", lambda e, pSB=pSB, sbs=sbs: e.copy(sbs, pSB),
                  waits=ewaits, incs=[(sS, 1)])
            mark(f"selb_{which}_{t}_{e_}", sS)
            ev[f"selb_evac_{par}"] = (sS, sS.n)
            pSC = banks[e_ % nb]
            swaits = []
            if e_ < 2:
                swaits.append(w(f"sel_th_{which}_{t}"))
            if e_ >= nb:
                swaits.append(w(f"stile_{which}_{t}_{e_ - nb}"))
            if pass_no == 1 and t == 0 and e_ == 0:
                swaits.append(w("keys_done"))

            for dc in range(DC):
                B.rec("tensor", lambda e, e_=e_, dc=dc, pSC=pSC: e.matmul(
                    pSC[:], keys_sb[:, dc, e_ * 128:(e_ + 1) * 128], ln_fm[:, dc, :],
                    start=(dc == 0), stop=(dc == DC - 1)),
                    waits=swaits if dc == 0 else [], incs=[(sT, 1)] if dc == DC - 1 else [])
            mark(f"sc_mm_{which}_{t}_{e_}", sT)
            B.rec("vector", lambda e, e_=e_, pSC=pSC, sbs=sbs: e.scalar_tensor_tensor(
                stile[:, e_, :], pSC[:], 0.0, sbs, ALU.max, ALU.mult),
                waits=[w(f"sc_mm_{which}_{t}_{e_}"), w(f"selb_{which}_{t}_{e_}")],
                incs=[(sV, 1)])
            mark(f"stile_{which}_{t}_{e_}", sV)
        ev["pT_busy"] = (sS, ev[f"selb_{which}_{t}_{E16 - 1}"][1])
        ev["sel_row_free"] = (sT, ev[f"selb_mm_{which}_{t}_{E16 - 1}"][1])
        ev["pa_free"] = (sV, ev[f"stile_{which}_{t}_{E16 - 1}"][1])
        for c in range(NCH):
            waits = [w(f"stile_{which}_{t}_{E16 - 1}")]
            if pass_no == 1 and t == 0 and c == 0:
                waits.append(w("vals_done"))
            if ev.get("pv_free"):
                waits.append(ev["pv_free"])
            for e_ in range(E16):
                for gg in range(2):
                    pV = pC if gg == 0 else pD
                    B.rec("tensor", lambda e, c=c, e_=e_, gg=gg, pV=pV: e.matmul(
                        pV[:], stile[:, e_, c * 128:(c + 1) * 128],
                        vals_sb[:, e_, gg * 512:(gg + 1) * 512],
                        start=(e_ == 0), stop=(e_ == E16 - 1)),
                        waits=waits if (e_ == 0 and gg == 0) else [],
                        incs=[(sT, 1)] if (e_ == E16 - 1 and gg == 1) else [])
            mark(f"ffnv_{which}_{t}_{c}", sT)
            ewaits = [w(f"ffnv_{which}_{t}_{c}")]
            for gg in range(2):
                pV = pC if gg == 0 else pD
                B.rec("vector", lambda e, c=c, gg=gg, pV=pV: e.scalar_tensor_tensor(
                    out_dst[:, c, gg * 512:(gg + 1) * 512], pV[:], 0.5,
                    resid_src[:, c, gg * 512:(gg + 1) * 512], ALU.mult, ALU.add),
                    waits=ewaits,
                    incs=[(sV, 1)] if gg == 1 else [])
            mark(f"{which}_out_{t}_{c}", sV)
            ev["pv_free"] = (sV, sV.n)
            if post_evac is not None:
                post_evac(c)

    def rec_pass1_tile(t):
        if t == 0:
            for c in range(NCH):
                rec_bn(sa, c, (bnb0, st40), waits=[w(f"xin_0_{c}")])
            rec_ln_finish(0, "ln0", (bnb0, st40))
        rec_ln_apply_transpose(t, sa, "ln0", g=st40)
        def p1_post_evac(c):
            rec_bn(sb, c, (bnb1, st41))
            if c == 1:
                rec_ln_finish_pair(t, "ln1", (bnb1, st41), 0)
            if c == 2:
                rec_ln_apply_T_pair(t, sb, "ln1", st41, 0)
            if c == 3:
                rec_ln_finish_pair(t, "ln1", (bnb1, st41), 1)

        rec_ffn(t, "ffn1", "ln0", sa, sb, 1, post_evac=p1_post_evac)
        for c in range(NCH):
            B.rec("gpsimd", lambda e, t=t, c=c: e.dma_start(
                out=x1_d[(t * NCH + c) * 128:(t * NCH + c + 1) * 128, :], in_=sb[:, c, :]),
                waits=[w(f"ffn1_out_{t}_{c}")], incs=[(sOUT, 16)])
            mark(f"x1st_{t}_{c}", sOUT)
        if t > 0:
            B.rec("vector", None, waits=[w(f"x1st_{t - 1}_{NCH - 1}")])
        for c in range(NCH):
            if c == 2:
                rec_ln_apply_T_pair(t, sb, "ln1", st41, 1)
            waits = [w(f"ln1_fm_{t}_{c}"), w("wkv_done"),
                     w("ones_done"), ev["pa_free"], ev["pv_free"]]
            lnc = ln_fm.ap()[:, :, c * 128:(c + 1) * 128]
            for dc in range(DC):
                for g_, pX, wsb in ((0, pA, wk_sb), (1, pB, wk_sb), (2, pC, wv_sb), (3, pD, wv_sb)):
                    gg = g_ % 2
                    B.rec("tensor", lambda e, dc=dc, pX=pX, wsb=wsb, gg=gg, lnc=lnc: e.matmul(
                        pX[:], lnc[:, dc, :], wsb[:, dc, gg * 512:(gg + 1) * 512],
                        start=(dc == 0), stop=False),
                        waits=waits if (dc == 0 and g_ == 0) else [])
            for g_, pX, ro in ((0, pA, 0), (1, pB, 512), (2, pC, D), (3, pD, D + 512)):
                B.rec("tensor", lambda e, pX=pX, ro=ro: e.matmul(
                    pX[:], ones[0:1, :], brows_kv[0:1, ro:ro + 512], start=False, stop=True),
                    incs=[(sT, 1)] if g_ == 3 else [])
            mark(f"kvproj_{t}_{c}", sT)
            par = c % 2
            for gg, pX in ((0, pC), (1, pD)):
                B.rec("vector", lambda e, gg=gg, pX=pX, par=par: e.tensor_copy(
                    vtil[:, par, gg * 512:(gg + 1) * 512], pX[:]),
                    waits=[w(f"kvproj_{t}_{c}")] if gg == 0 else [])
            for gg, pX in ((0, pA), (1, pB)):
                B.rec("vector", lambda e, gg=gg, pX=pX, par=par: e.tensor_scalar_max(
                    ktil[:, par, gg * 512:(gg + 1) * 512], pX[:], 0.0))
                kw = []
                if t * NCH + c >= 2:
                    pt, pc = divmod(t * NCH + c - 2, NCH)
                    kw = [w(f"kexp_{pt}_{pc}")]
                B.rec("vector", lambda e, gg=gg, pX=pX, par=par: e.tensor_scalar_min(
                    scr2k[:, par, gg * 512:(gg + 1) * 512], pX[:], 0.0),
                    waits=kw if gg == 0 else [],
                    incs=[(sV, 1)] if gg == 1 else [])
            mark(f"kv_free_{t}_{c}", sV)
            ev["pa_free"] = (sV, sV.n)
            ev["pv_free"] = (sV, sV.n)
            B.rec("scalar", lambda e, par=par: e.activation(
                scr2k[:, par, :], scr2k[:, par, :], AF.Exp),
                waits=[w(f"kv_free_{t}_{c}")], incs=[(sS, 1)])
            mark(f"kexp_{t}_{c}", sS)
            B.rec("vector", lambda e, par=par: e.tensor_tensor(
                ktil[:, par, :], ktil[:, par, :], scr2k[:, par, :], ALU.add),
                waits=[w(f"kexp_{t}_{c}")], incs=[(sV, 1)])
            mark(f"kv_in_{t}_{c}", sV)

            def rec_kvmm(tc):
                tt_, cc_ = divmod(tc, NCH)
                par2 = cc_ % 2
                first = tc == 0
                last = tc == nt * NCH - 1
                for h in range(E16):
                    B.rec("tensor", lambda e, h=h, first=first, last=last, par2=par2: e.matmul(
                        pkv[0:64, h * 64:(h + 1) * 64], ktil[:, par2, h * 64:(h + 1) * 64],
                        vtil[:, par2, h * 64:(h + 1) * 64],
                        start=(first and h % 8 == 0), stop=(last and h % 8 == 7),
                        skip_group_check=True),
                        waits=[w(f"kv_in_{tt_}_{cc_}")] if h == 0 else [],
                        incs=[(sT, 1)] if h == E16 - 1 else [])
                mark(f"kvacc_{tt_}_{cc_}", sT)

            tc = t * NCH + c
            if tc >= 1:
                rec_kvmm(tc - 1)
            if tc == nt * NCH - 1:
                rec_kvmm(tc)
        if t + 1 < nt:
            for c in range(NCH):
                B.rec("sync", lambda e, t=t, c=c: e.dma_start(
                    out=sa[:, c, :], in_=x_d[((t + 1) * NCH + c) * 128:((t + 1) * NCH + c + 1) * 128, :]),
                    waits=[w(f"ffn1_out_{t}_{c}")], incs=[(sIN, 16)])
                mark(f"xin_{t + 1}_{c}", sIN)
            for c in range(NCH):
                rec_bn(sa, c, (bnb0, st40), waits=[w(f"xin_{t + 1}_{c}")])
            rec_ln_finish(t + 1, "ln0", (bnb0, st40))

    def rec_kv_finalize():
        B.rec("vector", lambda e: e.tensor_copy(kvbf[:, :], pkv[:, :]),
              waits=[w(f"kvacc_{nt - 1}_{NCH - 1}")])
        B.rec("vector", lambda e: e.memset(kvbd[:, :, :], 0.0), incs=[(sV, 1)])
        mark("kvbf", sV)
        for cc in range(DC):
            B.rec("gpsimd", lambda e, cc=cc: e.dma_start(
                out=kvbd[0:64, cc, 0:64], in_=kvbf[0:64, cc * 128:cc * 128 + 64]),
                waits=[w("kvbf")], incs=[(sOUT, 16)])
            B.rec("gpsimd", lambda e, cc=cc: e.dma_start(
                out=kvbd[64:128, cc, 64:128], in_=kvbf[0:64, cc * 128 + 64:cc * 128 + 128]),
                incs=[(sOUT, 16)])
        mark("kvbd", sOUT)

    def rec_phase2_weights():
        kvp = w(f"kvproj_{nt - 1}_{NCH - 1}")
        kva = w(f"kvacc_{nt - 1}_{NCH - 1}")
        for i, (wd, wsb, sw) in enumerate(((wq_d, wq_sb, sWD), (wg_d, wg_sb, sWE), (wo_d, wo_sb, sWF))):
            for dc in range(DC):
                waits = []
                if dc == 0:
                    waits = [kvp] if i < 2 else [kva, w("kvbd")]
                B.rec("gpsimd", lambda e, wd=wd, dc=dc, wsb=wsb: e.dma_start(
                    out=wsb[:, dc, :], in_=wd[dc * 128:(dc + 1) * 128, :]),
                    waits=waits, incs=[(sw, 16)])
            mark(f"p2w_{i}", sw)

    def rec_pass2_tile(t):
        q2 = stile.ap()[:, 0:4, :].rearrange("p a b -> p (a b)").bitcast(F32)
        for c in range(NCH):
            if t == 0:
                waits = [w(f"x1st_{nt - 1}_{NCH - 1}")]
            else:
                waits = [w(f"yst_{t - 1}_{c}")]
            B.rec("sync", lambda e, t=t, c=c: e.dma_start(
                out=sa[:, c, :], in_=x1_d[(t * NCH + c) * 128:(t * NCH + c + 1) * 128, :]),
                waits=waits, incs=[(sIN, 16)])
            mark(f"x1in_{t}_{c}", sIN)
        B.rec("vector", None, waits=[w(f"x1in_{t}_{NCH - 1}")])
        rec_ln_apply_transpose(t, sa, "ln1b",
                               mcol=lambda c: m1a.ap()[:, NCH * t + c:NCH * t + c + 1],
                               rcol=lambda c: r1a.ap()[:, NCH * t + c:NCH * t + c + 1])
        for fc in range(DC):
            par = fc % 2
            pQ = pA if par == 0 else pB
            pG = pC if par == 0 else pD
            qwaits = [w(f"ln1b_fm_{t}"), w("p2w_0"), w("bcols_done")]
            if ev.get(f"pq_free_{par}"):
                qwaits.append(ev[f"pq_free_{par}"])
            for dc in range(DC):
                B.rec("tensor", lambda e, fc=fc, dc=dc, pQ=pQ: e.matmul(
                    pQ[:], wq_sb[:, dc, fc * 128:(fc + 1) * 128], ln_fm[:, dc, :],
                    start=(dc == 0), stop=(dc == DC - 1)),
                    waits=qwaits if dc == 0 else [],
                    incs=[(sT, 1)] if dc == DC - 1 else [])
            mark(f"qmm_{t}_{fc}", sT)
            gwaits = [w("p2w_1")]
            if ev.get(f"pg_free_{par}"):
                gwaits.append(ev[f"pg_free_{par}"])
            for dc in range(DC):
                B.rec("tensor", lambda e, fc=fc, dc=dc, pG=pG: e.matmul(
                    pG[:], wg_sb[:, dc, fc * 128:(fc + 1) * 128], ln_fm[:, dc, :],
                    start=(dc == 0), stop=(dc == DC - 1)),
                    waits=gwaits if dc == 0 else [],
                    incs=[(sT, 1)] if dc == DC - 1 else [])
            mark(f"gmm_{t}_{fc}", sT)
            sh = par * 512
            B.rec("vector", lambda e, fc=fc, pQ=pQ, sh=sh: e.tensor_scalar(
                scr1[:, sh:sh + 512], pQ[:], bcols.ap()[:, fc:fc + 1], 0.0,
                ALU.add, ALU.max),
                waits=[w(f"qmm_{t}_{fc}")])
            B.rec("vector", lambda e, fc=fc, pQ=pQ, sh=sh: e.tensor_scalar(
                q2[:, sh:sh + 512], pQ[:], bcols.ap()[:, fc:fc + 1], 0.0,
                ALU.add, ALU.min),
                incs=[(sV, 1)])
            mark(f"qmin_{t}_{fc}", sV)
            B.rec("scalar", lambda e, sh=sh: e.activation(
                q2[:, sh:sh + 512], q2[:, sh:sh + 512], AF.Exp),
                waits=[w(f"qmin_{t}_{fc}")], incs=[(sS, 1)])
            mark(f"qexp_{t}_{fc}", sS)
            B.rec("vector", lambda e, fc=fc, sh=sh: e.tensor_add(
                qt[:, fc, :], scr1[:, sh:sh + 512], q2[:, sh:sh + 512]),
                waits=[w(f"qexp_{t}_{fc}")], incs=[(sV, 1)])
            mark(f"qt_{t}_{fc}", sV)
            twaits = [w(f"gmm_{t}_{fc}")]
            if ev.get(f"th_free_{par}"):
                twaits.append(ev[f"th_free_{par}"])
            B.rec("scalar", lambda e, fc=fc, pG=pG, par=par: e.activation(
                th[:, par, :], pG[:], AF.Tanh, scale=0.5,
                bias=bcols.ap()[:, 16 + fc:17 + fc]),
                waits=twaits, incs=[(sS, 1)])
            mark(f"th_{t}_{fc}", sS)
            ev[f"pg_free_{par}"] = (sS, sS.n)
            B.rec("tensor", lambda e, fc=fc, pQ=pQ: e.matmul(
                pQ[:], kvbd[:, fc, :], qt[:, fc, :], start=True, stop=True),
                waits=[w(f"qt_{t}_{fc}"), w("kvbd")], incs=[(sT, 1)])
            mark(f"attn_{t}_{fc}", sT)
            B.rec("vector", lambda e, fc=fc, pQ=pQ, par=par: e.scalar_tensor_tensor(
                qt[:, fc, :], th[:, par, :], 1.0, pQ[:], ALU.add, ALU.mult),
                waits=[w(f"attn_{t}_{fc}"), w(f"th_{t}_{fc}")], incs=[(sV, 1)])
            mark(f"gt_{t}_{fc}", sV)
            ev[f"pq_free_{par}"] = (sV, sV.n)
            ev[f"th_free_{par}"] = (sV, sV.n)
        ev["pa_free"] = (sV, sV.n)
        for c in range(NCH):
            waits = [w(f"gt_{t}_{DC - 1}"), w("p2w_2")]
            if ev.get("pv_free"):
                waits.append(ev["pv_free"])
            for fc in range(DC):
                for gg, pX in ((0, pC), (1, pD)):
                    B.rec("tensor", lambda e, c=c, fc=fc, gg=gg, pX=pX: e.matmul(
                        pX[:], qt[:, fc, c * 128:(c + 1) * 128],
                        wo_sb[:, fc, gg * 512:(gg + 1) * 512],
                        start=(fc == 0), stop=False),
                        waits=waits if (fc == 0 and gg == 0) else [])
            for gg, pX in ((0, pC), (1, pD)):
                B.rec("tensor", lambda e, gg=gg, pX=pX: e.matmul(
                    pX[:], ones[0:1, :], brow_bo[0:1, gg * 512:(gg + 1) * 512],
                    start=False, stop=True),
                    incs=[(sT, 1)] if gg == 1 else [])
            mark(f"womm_{t}_{c}", sT)
            for gg, pX in ((0, pC), (1, pD)):
                B.rec("vector", lambda e, c=c, gg=gg, pX=pX: e.scalar_tensor_tensor(
                    sb[:, c, gg * 512:(gg + 1) * 512], pX[:], 0.5,
                    sa[:, c, gg * 512:(gg + 1) * 512], ALU.mult, ALU.add),
                    waits=[w(f"womm_{t}_{c}")],
                    incs=[(sV, 1)] if gg == 1 else [])
            mark(f"x2_{t}_{c}", sV)
            ev["pv_free"] = (sV, sV.n)
            rec_bn(sb, c, (bnb0, st40))
            if c == 1:
                rec_ln_finish_pair(t, "ln2", (bnb0, st40), 0)
            if c == 3:
                rec_ln_finish_pair(t, "ln2", (bnb0, st40), 1)
        rec_ln_apply_T_pair(t, sb, "ln2", st40, 0)
        rec_ln_apply_T_pair(t, sb, "ln2", st40, 1)
        rec_ffn(t, "ffn2", "ln2", sb, sa, 2)
        for c in range(NCH):
            B.rec("gpsimd", lambda e, t=t, c=c: e.dma_start(
                out=y_d[(t * NCH + c) * 128:(t * NCH + c + 1) * 128, :], in_=sa[:, c, :]),
                waits=[w(f"ffn2_out_{t}_{c}")], incs=[(sOUT, 16)])
            mark(f"yst_{t}_{c}", sOUT)

    with contextlib.ExitStack() as p1:
        wk_sb = p1.enter_context(nc.sbuf_tensor("wk_sb", [128, DC, D], BF16))
        wv_sb = p1.enter_context(nc.sbuf_tensor("wv_sb", [128, DC, D], BF16))
        brows_kv = p1.enter_context(nc.sbuf_tensor("brows_kv", [1, 2 * D], BF16))
        scr2k = p1.enter_context(nc.sbuf_tensor("scr2k", [128, 2, D], F32))
        ktil = p1.enter_context(nc.sbuf_tensor("ktil", [128, 2, D], BF16))
        vtil = p1.enter_context(nc.sbuf_tensor("vtil", [128, 2, D], BF16))
        kvbf = p1.enter_context(nc.sbuf_tensor("kvbf", [64, 1024], BF16))
        pkv = p1.enter_context(nc.psum_tensor("pkv", [64, 1024], F32))
        rec_startup()
        for t in range(nt):
            rec_pass1_tile(t)
        rec_kv_finalize()
        B.flush()
    with contextlib.ExitStack() as p2:
        pE = p2.enter_context(nc.psum_tensor("pE", [128, 512], F32))
        pF = p2.enter_context(nc.psum_tensor("pF", [128, 512], F32))
        wq_sb = p2.enter_context(nc.sbuf_tensor("wq_sb", [128, DC, D], BF16))
        wg_sb = p2.enter_context(nc.sbuf_tensor("wg_sb", [128, DC, D], BF16))
        wo_sb = p2.enter_context(nc.sbuf_tensor("wo_sb", [128, DC, D], BF16))
        rec_phase2_weights()
        for t in range(nt):
            rec_pass2_tile(t)
        B.rec("gpsimd", lambda e: e.sem_inc(sG.h, 1),
              waits=[(sOUT, sOUT.n), (sIN, sIN.n)])
        B.flush()
    em.close()
    return nc


_PROG_CACHE = {}


def _get_prog(nt):
    if nt not in _PROG_CACHE:
        _PROG_CACHE[nt] = build_program(nt)
    return _PROG_CACHE[nt]


def kernel(L0_attn_x, L0_attn_Wq, L0_attn_bq, L0_attn_Wk, L0_attn_bk,
           L0_attn_Wv, L0_attn_bv, L0_attn_Wg, L0_attn_bg, L0_attn_Wo,
           L0_attn_bo, L0_ffn_expert_sel, L0_ffn_keys, L0_ffn_values):
    x = np.ascontiguousarray(np.asarray(L0_attn_x, dtype=np.float32))
    Bz, S, Dm = x.shape
    assert Dm == D and S % TT == 0
    nt = S // TT
    nc = _get_prog(nt)
    common = {
        "wq": np.ascontiguousarray(L0_attn_Wq, dtype=np.float32),
        "wk": np.ascontiguousarray(L0_attn_Wk, dtype=np.float32),
        "wv": np.ascontiguousarray(L0_attn_Wv, dtype=np.float32),
        "wg": np.ascontiguousarray(L0_attn_Wg, dtype=np.float32),
        "wo": np.ascontiguousarray(L0_attn_Wo, dtype=np.float32),
        "bq": np.ascontiguousarray(L0_attn_bq, dtype=np.float32),
        "bk": np.ascontiguousarray(L0_attn_bk, dtype=np.float32),
        "bv": np.ascontiguousarray(L0_attn_bv, dtype=np.float32),
        "bg": np.ascontiguousarray(L0_attn_bg, dtype=np.float32),
        "bo": np.ascontiguousarray(L0_attn_bo, dtype=np.float32),
        "es": np.ascontiguousarray(L0_ffn_expert_sel, dtype=np.float32),
        "keys": np.ascontiguousarray(L0_ffn_keys, dtype=np.float32),
        "vals": np.ascontiguousarray(L0_ffn_values, dtype=np.float32),
    }
    in_maps = [dict(common, x=np.ascontiguousarray(x[b])) for b in range(Bz)]
    res = run_bass_kernel_spmd(nc, in_maps, core_ids=list(range(Bz)))
    out = np.stack([np.asarray(r["y"], dtype=np.float32) for r in res.results], axis=0)
    return out



# version 16
# speedup vs baseline: 1.0109x; 1.0109x over previous
import contextlib

import numpy as np
import ml_dtypes

import concourse.bass as bass
import concourse.mybir as mybir
from concourse.bass_utils import run_bass_kernel_spmd

F32 = mybir.dt.float32
BF16 = mybir.dt.bfloat16
I32 = mybir.dt.int32
AF = mybir.ActivationFunctionType
ALU = mybir.AluOpType

D = 1024
DC = 8
E16 = 16
TT = 512
NCH = 4
ENGINES = ("sync", "tensor", "vector", "scalar", "gpsimd")


class Sem:
    __slots__ = ("h", "n", "name")

    def __init__(self, h, name):
        self.h = h
        self.n = 0
        self.name = name


class Builder:
    def __init__(self, nc):
        self.nc = nc
        self.ops = {e: [] for e in ENGINES}
        self.waited = {}

    def rec(self, eng, fn, waits=(), incs=()):
        w = [(s.h, int(v)) for (s, v) in waits]
        i = [(s.h, int(a)) for (s, a) in incs]
        assert fn is not None or not i
        for s, a in incs:
            s.n += a
        self.ops[eng].append((fn, w, i))

    def flush(self):
        blk_ops = self.ops
        self.ops = {e: [] for e in ENGINES}
        waited = self.waited
        with self.nc.Block() as block:
            for eng_name in ENGINES:
                entries = blk_ops[eng_name]
                if not entries:
                    continue

                def body(e, entries=entries, eng_name=eng_name):
                    for fn, waits, incs in entries:
                        for (h, v) in waits:
                            key = (eng_name, h.num)
                            if waited.get(key, -1) < v:
                                e.wait_ge(h, v)
                                waited[key] = v
                        if fn is None:
                            continue
                        ins = fn(e)
                        for (h, a) in incs:
                            if a == 16:
                                ins.then_inc(h, a)
                            else:
                                e.maybe_drain_then_inc((h, a), fusable=True)

                getattr(block, eng_name)(body)


def build_program(nt):
    S = nt * TT
    nc = bass.Bass("TRN2", target_bir_lowering=False, detect_race_conditions=False)

    x_d = nc.dram_tensor("x", [S, D], F32, kind="ExternalInput").ap()
    y_d = nc.dram_tensor("y", [S, D], F32, kind="ExternalOutput").ap()
    wq_d = nc.dram_tensor("wq", [D, D], F32, kind="ExternalInput").ap()
    wk_d = nc.dram_tensor("wk", [D, D], F32, kind="ExternalInput").ap()
    wv_d = nc.dram_tensor("wv", [D, D], F32, kind="ExternalInput").ap()
    wg_d = nc.dram_tensor("wg", [D, D], F32, kind="ExternalInput").ap()
    wo_d = nc.dram_tensor("wo", [D, D], F32, kind="ExternalInput").ap()
    bq_d = nc.dram_tensor("bq", [D], F32, kind="ExternalInput").ap()
    bk_d = nc.dram_tensor("bk", [D], F32, kind="ExternalInput").ap()
    bv_d = nc.dram_tensor("bv", [D], F32, kind="ExternalInput").ap()
    bg_d = nc.dram_tensor("bg", [D], F32, kind="ExternalInput").ap()
    bo_d = nc.dram_tensor("bo", [D], F32, kind="ExternalInput").ap()
    es_d = nc.dram_tensor("es", [E16, D], F32, kind="ExternalInput").ap()
    keys_d = nc.dram_tensor("keys", [E16, D, 128], F32, kind="ExternalInput").ap()
    vals_d = nc.dram_tensor("vals", [E16, 128, D], F32, kind="ExternalInput").ap()
    x1_d = nc.dram_tensor("x1s", [S, D], F32, kind="Internal").ap()
    ident_d = nc.inline_tensor(np.eye(128, dtype=ml_dtypes.bfloat16), name="ident_dr")

    B = Builder(nc)
    sIN = Sem(nc.alloc_semaphore("sIN"), "sIN")
    sOUT = Sem(nc.alloc_semaphore("sOUT"), "sOUT")
    sT = Sem(nc.alloc_semaphore("sT"), "sT")
    sV = Sem(nc.alloc_semaphore("sV"), "sV")
    sS = Sem(nc.alloc_semaphore("sS"), "sS")
    sG = Sem(nc.alloc_semaphore("sG"), "sG")
    sCFG = Sem(nc.alloc_semaphore("sCFG"), "sCFG")
    sWA = Sem(nc.alloc_semaphore("sWA"), "sWA")
    sWB = Sem(nc.alloc_semaphore("sWB"), "sWB")
    sWC = Sem(nc.alloc_semaphore("sWC"), "sWC")
    sWD = Sem(nc.alloc_semaphore("sWD"), "sWD")
    sWE = Sem(nc.alloc_semaphore("sWE"), "sWE")
    sWF = Sem(nc.alloc_semaphore("sWF"), "sWF")

    em = contextlib.ExitStack()
    E = em.enter_context
    ident = E(nc.sbuf_tensor("ident", [128, 128], BF16))
    ones = E(nc.sbuf_tensor("ones", [65, 128], BF16))
    brow_bo = E(nc.sbuf_tensor("brow_bo", [1, D], BF16))
    bcols = E(nc.sbuf_tensor("bcols", [128, 32], F32))
    keys_sb = E(nc.sbuf_tensor("keys_sb", [128, DC, 2048], BF16))
    vals_sb = E(nc.sbuf_tensor("vals_sb", [128, E16, D], BF16))
    es_sb = E(nc.sbuf_tensor("es_sb", [128, DC, E16], BF16))
    sa = E(nc.sbuf_tensor("sa", [128, NCH, D], F32))
    sb = E(nc.sbuf_tensor("sb", [128, NCH, D], F32))
    ln_tm = E(nc.sbuf_tensor("ln_tm", [128, 2, D], BF16))
    ln_fm = E(nc.sbuf_tensor("ln_fm", [128, DC, TT], BF16))
    sel_bf = E(nc.sbuf_tensor("sel_bf", [E16, TT], BF16))
    sel_row = E(nc.sbuf_tensor("sel_row", [65, 6 * TT], BF16))
    selb = E(nc.sbuf_tensor("selb", [128, 3, TT], BF16))
    stile = E(nc.sbuf_tensor("stile", [128, E16, TT], BF16))
    scr1 = E(nc.sbuf_tensor("scr1", [16, 512], F32))
    bnb0 = E(nc.sbuf_tensor("bnb0", [128, 8, 6], F32))
    st40 = E(nc.sbuf_tensor("st40", [128, 28], F32))
    bnb1 = E(nc.sbuf_tensor("bnb1", [128, 8, 6], F32))
    st41 = E(nc.sbuf_tensor("st41", [128, 28], F32))
    m1a = E(nc.sbuf_tensor("m1a", [128, NCH * nt], F32))
    r1a = E(nc.sbuf_tensor("r1a", [128, NCH * nt], F32))
    qt = E(nc.sbuf_tensor("qt", [128, DC, TT], BF16))
    th = E(nc.sbuf_tensor("th", [128, 2, TT], BF16))
    kvbd = E(nc.sbuf_tensor("kvbd", [128, DC, 128], BF16))
    pT1 = E(nc.psum_tensor("pT1", [128, 1024], BF16))
    pT2 = E(nc.psum_tensor("pT2", [128, 1024], BF16))
    pA = E(nc.psum_tensor("pA", [128, 512], F32))
    pB = E(nc.psum_tensor("pB", [128, 512], F32))
    pC = E(nc.psum_tensor("pC", [128, 512], F32))
    pD = E(nc.psum_tensor("pD", [128, 512], F32))

    ev = {}

    def mark(name, sem):
        ev[name] = (sem, sem.n)

    def w(name):
        return ev[name]

    def rec_startup():
        B.rec("scalar", lambda e: e.preload_act_table(AF.Exp))
        for c in range(NCH):
            B.rec("sync", lambda e, c=c: e.dma_start(out=sa[:, c, :], in_=x_d[c * 128:(c + 1) * 128, :]),
                  incs=[(sIN, 16)])
            mark(f"xin_0_{c}", sIN)
        B.rec("sync", lambda e: e.dma_start(out=ident[:], in_=ident_d.ap()), incs=[(sCFG, 16)])
        mark("ident_in", sCFG)
        B.rec("sync", lambda e: e.dma_start(out=bcols[:, 0:8], in_=bq_d.rearrange("(c p) -> p c", p=128), allow_slow_non_contiguous=True),
              incs=[(sCFG, 16)])
        B.rec("sync", lambda e: e.dma_start(out=bcols[:, 16:24], in_=bg_d.rearrange("(c p) -> p c", p=128), allow_slow_non_contiguous=True),
              incs=[(sCFG, 16)])
        B.rec("sync", lambda e: e.dma_start(out=stile.ap()[0:1, 0:4, :].rearrange('p a b -> p (a b)').bitcast(F32)[0:1, 0:D], in_=bo_d.unsqueeze(0)), incs=[(sCFG, 16)])
        mark("bo_in", sCFG)
        B.rec("vector", lambda e: e.memset(ones[:, :], 1.0))
        B.rec("vector", lambda e: e.tensor_scalar_mul(bcols[:, 8:16], bcols[:, 0:8], -1.0),
              waits=[w("bo_in")])
        B.rec("vector", lambda e: e.tensor_scalar_mul(bcols[:, 16:24], bcols[:, 16:24], 0.5))
        B.rec("vector", lambda e: e.tensor_scalar_mul(stile.ap()[0:1, 0:4, :].rearrange('p a b -> p (a b)').bitcast(F32)[0:1, 0:D], stile.ap()[0:1, 0:4, :].rearrange('p a b -> p (a b)').bitcast(F32)[0:1, 0:D], 2.0))
        B.rec("vector", lambda e: e.tensor_copy(brow_bo[0:1, 0:D], stile.ap()[0:1, 0:4, :].rearrange('p a b -> p (a b)').bitcast(F32)[0:1, 0:D]), incs=[(sV, 1)])
        mark("cfg_done", sV)
        mark("ones_done", sV)
        mark("bcols_done", sV)
        for dc in range(DC):
            B.rec("gpsimd", lambda e, dc=dc: e.dma_start(
                out=keys_sb[:, dc, :],
                in_=keys_d[:, dc * 128:(dc + 1) * 128, :].transpose([1, 0, 2])),
                incs=[(sWA, 16)])
        mark("keys_done", sWA)
        for dc in range(DC):
            B.rec("gpsimd", lambda e, dc=dc: e.dma_start(
                out=es_sb[:, dc, :],
                in_=es_d[:, dc * 128:(dc + 1) * 128].transpose([1, 0]),
                allow_slow_non_contiguous=True),
                incs=[(sWA, 16)])
        mark("es_done", sWA)
        for e_ in range(E16):
            B.rec("gpsimd", lambda e, e_=e_: e.dma_start(
                out=vals_sb[:, e_, :], in_=vals_d[e_]), incs=[(sWB, 16)])
        mark("vals_done", sWB)
        for name, wd, wsb in (("wk", wk_d, wk_sb), ("wv", wv_d, wv_sb)):
            for dc in range(DC):
                B.rec("gpsimd", lambda e, wd=wd, dc=dc, wsb=wsb: e.dma_start(
                    out=wsb[:, dc, :], in_=wd[dc * 128:(dc + 1) * 128, :]),
                    incs=[(sWC, 16)])
        B.rec("gpsimd", lambda e: e.dma_start(out=brows_kv[0:1, 0:D], in_=bk_d.unsqueeze(0)),
              incs=[(sWC, 16)])
        B.rec("gpsimd", lambda e: e.dma_start(out=brows_kv[0:1, D:2 * D], in_=bv_d.unsqueeze(0)),
              incs=[(sWC, 16)])
        mark("wkv_done", sWC)

    def rec_bn(src, c, bufs, waits=()):
        bnb, g = bufs
        B.rec("vector", lambda e, c=c: e.bn_stats(
            bnb[:, 2 * c, :], src[:, c, 0:512]), waits=list(waits))
        B.rec("vector", lambda e, c=c: e.bn_stats(
            bnb[:, 2 * c + 1, :], src[:, c, 512:1024]))

    def rec_ln_finish(t, which, bufs):
        bnb, g = bufs
        V = "vector"

        def RD(fn):
            B.rec(V, fn)
            B.rec(V, lambda e: e.drain())

        B.rec(V, lambda e: e.drain())
        for c in range(NCH):
            B.rec(V, lambda e, c=c: e.bn_aggr(g[:, 2 * c:2 * c + 2], bnb[:, 2 * c:2 * c + 2, :]))
        B.rec(V, lambda e: e.drain())
        mvv = g.ap()[:, 1:8:2]
        us = g.ap()[:, 12:16]
        Bs = g.ap()[:, 16:20]
        Cs = g.ap()[:, 20:24]
        ys = g.ap()[:, 24:28]
        rs = g.ap()[:, 8:12]
        RD(lambda e: e.tensor_scalar_add(us, mvv, 1e-5))
        RD(lambda e: e.tensor_scalar(Bs.bitcast(I32), us.bitcast(I32), 1, None, ALU.arith_shift_right))
        RD(lambda e: e.tensor_scalar(Cs.bitcast(I32), Bs.bitcast(I32), -1, 0x5F3759DF, ALU.mult, ALU.add))
        for it in range(3):
            yin = Cs if it % 2 == 0 else ys
            yout = ys if it % 2 == 0 else Cs
            last = it == 2
            RD(lambda e, yin=yin: e.tensor_tensor(Bs, yin, yin, ALU.mult))
            RD(lambda e: e.tensor_tensor(rs, Bs, us, ALU.mult))
            RD(lambda e: e.tensor_scalar(Bs, rs, -0.5, 1.5, ALU.mult, ALU.add))
            RD(lambda e, yin=yin, yout=(rs if last else yout): e.tensor_tensor(yout, yin, Bs, ALU.mult))
        if which == "ln1":
            B.rec(V, lambda e, t=t: e.tensor_copy(m1a[:, NCH * t:NCH * (t + 1)], g.ap()[:, 0:8:2]))
            B.rec(V, lambda e, t=t: e.tensor_copy(r1a[:, NCH * t:NCH * (t + 1)], rs))
            B.rec(V, lambda e: e.drain())

    def rec_ln_finish_pair(t, which, bufs, p):
        bnb, g = bufs
        V = "vector"

        def RD(fn):
            B.rec(V, fn)
            B.rec(V, lambda e: e.drain())

        B.rec(V, lambda e: e.drain())
        for c in (2 * p, 2 * p + 1):
            B.rec(V, lambda e, c=c: e.bn_aggr(g[:, 2 * c:2 * c + 2], bnb[:, 2 * c:2 * c + 2, :]))
        B.rec(V, lambda e: e.drain())
        o = 2 * p
        mvv = g.ap()[:, 1 + 4 * p:8:2][:, 0:2]
        us = g.ap()[:, 12 + o:14 + o]
        Bs = g.ap()[:, 16 + o:18 + o]
        Cs = g.ap()[:, 20 + o:22 + o]
        ys = g.ap()[:, 24 + o:26 + o]
        rs = g.ap()[:, 8 + o:10 + o]
        RD(lambda e: e.tensor_scalar_add(us, mvv, 1e-5))
        RD(lambda e: e.tensor_scalar(Bs.bitcast(I32), us.bitcast(I32), 1, None, ALU.arith_shift_right))
        RD(lambda e: e.tensor_scalar(Cs.bitcast(I32), Bs.bitcast(I32), -1, 0x5F3759DF, ALU.mult, ALU.add))
        for it in range(3):
            yin = Cs if it % 2 == 0 else ys
            yout = ys if it % 2 == 0 else Cs
            last = it == 2
            RD(lambda e, yin=yin: e.tensor_tensor(Bs, yin, yin, ALU.mult))
            RD(lambda e: e.tensor_tensor(rs, Bs, us, ALU.mult))
            RD(lambda e: e.tensor_scalar(Bs, rs, -0.5, 1.5, ALU.mult, ALU.add))
            RD(lambda e, yin=yin, yout=(rs if last else yout): e.tensor_tensor(yout, yin, Bs, ALU.mult))
        if which == "ln1":
            B.rec(V, lambda e, t=t, p=p: e.tensor_copy(
                m1a[:, NCH * t + 2 * p:NCH * t + 2 * p + 2], g.ap()[:, 4 * p:4 * p + 4:2]))
            B.rec(V, lambda e, t=t, p=p: e.tensor_copy(
                r1a[:, NCH * t + 2 * p:NCH * t + 2 * p + 2], rs))
            B.rec(V, lambda e: e.drain())

    def rec_ln_apply_T_pair(t, src, which, g, p):
        for c in (2 * p, 2 * p + 1):
            mc, rc = g.ap()[:, 2 * c:2 * c + 1], g.ap()[:, 8 + c:9 + c]
            awaits = []
            if ev.get(f"ln_tm_free_{c % 2}"):
                awaits.append(ev[f"ln_tm_free_{c % 2}"])
            B.rec("vector", lambda e, c=c, mc=mc, rc=rc: e.tensor_scalar(
                ln_tm[:, c % 2, :], src[:, c, :], mc, rc, ALU.subtract, ALU.mult),
                waits=awaits, incs=[(sV, 1)])
            mark(f"{which}_tm_{t}_{c}", sV)
        for c in (2 * p, 2 * p + 1):
            pT = pT1 if c % 2 == 0 else pT2
            waits = [w(f"{which}_tm_{t}_{c}"), w("ident_in")]
            if ev.get(f"Tevac_prev_{c % 2}"):
                waits.append(ev[f"Tevac_prev_{c % 2}"])
            if ev.get("pT_busy"):
                waits.append(ev["pT_busy"])
            for j in range(DC):
                B.rec("tensor", lambda e, c=c, j=j, pT=pT: e.transpose(
                    pT[:, j * 128:(j + 1) * 128], ln_tm[:, c % 2, j * 128:(j + 1) * 128], ident[:]),
                    waits=waits if j == 0 else [], incs=[(sT, 1)] if j == DC - 1 else [])
            mark(f"{which}_T_{t}_{c}", sT)
            ev[f"ln_tm_free_{c % 2}"] = (sT, sT.n)
            B.rec("scalar", lambda e, c=c, pT=pT: e.copy(
                ln_fm.ap()[:, :, c * 128:(c + 1) * 128],
                pT.ap().rearrange("p (j k) -> p j k", j=DC)),
                waits=[w(f"{which}_T_{t}_{c}")], incs=[(sS, 1)])
            mark(f"{which}_fm_{t}_{c}", sS)
            ev[f"Tevac_prev_{c % 2}"] = (sS, sS.n)
        if p == 1:
            mark(f"{which}_fm_{t}", sS)

    def rec_ln_apply_transpose(t, src, which, mcol=None, rcol=None, g=None):
        for c in range(NCH):
            if mcol is None:
                mc, rc = g.ap()[:, 2 * c:2 * c + 1], g.ap()[:, 8 + c:9 + c]
            else:
                mc, rc = mcol(c), rcol(c)
            awaits = []
            if ev.get(f"ln_tm_free_{c % 2}"):
                awaits.append(ev[f"ln_tm_free_{c % 2}"])
            B.rec("vector", lambda e, c=c, mc=mc, rc=rc: e.tensor_scalar(
                ln_tm[:, c % 2, :], src[:, c, :], mc, rc, ALU.subtract, ALU.mult),
                waits=awaits, incs=[(sV, 1)])
            mark(f"{which}_tm_{t}_{c}", sV)
            pT = pT1 if c % 2 == 0 else pT2
            waits = [w(f"{which}_tm_{t}_{c}"), w("ident_in")]
            if ev.get(f"Tevac_prev_{c % 2}"):
                waits.append(ev[f"Tevac_prev_{c % 2}"])
            if ev.get("pT_busy"):
                waits.append(ev["pT_busy"])
            for j in range(DC):
                B.rec("tensor", lambda e, c=c, j=j, pT=pT: e.transpose(
                    pT[:, j * 128:(j + 1) * 128], ln_tm[:, c % 2, j * 128:(j + 1) * 128], ident[:]),
                    waits=waits if j == 0 else [], incs=[(sT, 1)] if j == DC - 1 else [])
            mark(f"{which}_T_{t}_{c}", sT)
            ev[f"ln_tm_free_{c % 2}"] = (sT, sT.n)
            B.rec("scalar", lambda e, c=c, pT=pT: e.copy(
                ln_fm.ap()[:, :, c * 128:(c + 1) * 128],
                pT.ap().rearrange("p (j k) -> p j k", j=DC)),
                waits=[w(f"{which}_T_{t}_{c}")], incs=[(sS, 1)])
            mark(f"{which}_fm_{t}_{c}", sS)
            ev[f"Tevac_prev_{c % 2}"] = (sS, sS.n)
        mark(f"{which}_fm_{t}", sS)

    def rec_ffn(t, which, lnname, resid_src, out_dst, pass_no, post_evac=None):
        waits = [w(f"{lnname}_fm_{t}"), w("es_done"), w("ident_in")]
        if ev.get("pa_free"):
            waits.append(ev["pa_free"])
        for dc in range(DC):
            B.rec("tensor", lambda e, dc=dc: e.matmul(
                pA[0:16, :], es_sb[:, dc, :], ln_fm[:, dc, :],
                start=(dc == 0), stop=(dc == DC - 1)),
                waits=waits if dc == 0 else [], incs=[(sT, 1)] if dc == DC - 1 else [])
        mark(f"sel_mm_{which}_{t}", sT)
        B.rec("scalar", lambda e: e.activation(scr1[0:16, 0:TT], pA[0:16, :], AF.Tanh, scale=0.5),
              waits=[w(f"sel_mm_{which}_{t}")], incs=[(sS, 1)])
        mark(f"sel_th_{which}_{t}", sS)
        B.rec("vector", lambda e: e.tensor_scalar(
            sel_bf[:, :], scr1[0:16, 0:TT], 0.5, 0.5, ALU.mult, ALU.add),
            waits=[w(f"sel_th_{which}_{t}")], incs=[(sV, 1)])
        mark(f"sel_{which}_{t}", sV)
        srw = [w(f"sel_{which}_{t}")]
        if ev.get("sel_row_free"):
            srw.append(ev["sel_row_free"])
        B.rec("gpsimd", lambda e: e.dma_start(out=sel_row[0:1, 0:6 * TT], in_=sel_bf[0:6, :]),
              waits=srw, incs=[(sOUT, 16)])
        B.rec("gpsimd", lambda e: e.dma_start(out=sel_row[32:33, 0:6 * TT], in_=sel_bf[6:12, :]),
              incs=[(sOUT, 16)])
        B.rec("gpsimd", lambda e: e.dma_start(out=sel_row[64:65, 0:4 * TT], in_=sel_bf[12:16, :]),
              incs=[(sOUT, 16)])
        mark(f"selrow_{which}_{t}", sOUT)
        nb = 2 if pass_no == 1 else 4
        banks = (pA, pB) if pass_no == 1 else (pA, pB, pE, pF)
        for e_ in range(E16):
            par = e_ % 2
            spar = e_ % 3
            pSB = (pT1 if par == 0 else pT2).ap().bitcast(F32)[:, 0:512]
            sbs = selb.ap()[:, spar, :]
            waits = [w(f"selrow_{which}_{t}"), w("ones_done")]
            if ev.get(f"selb_evac_{par}"):
                waits.append(ev[f"selb_evac_{par}"])
            sp = 32 * (e_ // 6) if e_ < 12 else 64
            so = (e_ % 6) * TT if e_ < 12 else (e_ - 12) * TT
            B.rec("tensor", lambda e, pSB=pSB, sp=sp, so=so: e.matmul(
                pSB, ones[sp:sp + 1, :], sel_row[sp:sp + 1, so:so + TT],
                start=True, stop=True),
                waits=waits, incs=[(sT, 1)])
            mark(f"selb_mm_{which}_{t}_{e_}", sT)
            ewaits = [w(f"selb_mm_{which}_{t}_{e_}")]
            if e_ >= 3:
                ewaits.append(w(f"stile_{which}_{t}_{e_ - 3}"))
            B.rec("scalar", lambda e, pSB=pSB, sbs=sbs: e.copy(sbs, pSB),
                  waits=ewaits, incs=[(sS, 1)])
            mark(f"selb_{which}_{t}_{e_}", sS)
            ev[f"selb_evac_{par}"] = (sS, sS.n)
            pSC = banks[e_ % nb]
            swaits = []
            if e_ < 2:
                swaits.append(w(f"sel_th_{which}_{t}"))
            if e_ >= nb:
                swaits.append(w(f"stile_{which}_{t}_{e_ - nb}"))
            if pass_no == 1 and t == 0 and e_ == 0:
                swaits.append(w("keys_done"))

            for dc in range(DC):
                B.rec("tensor", lambda e, e_=e_, dc=dc, pSC=pSC: e.matmul(
                    pSC[:], keys_sb[:, dc, e_ * 128:(e_ + 1) * 128], ln_fm[:, dc, :],
                    start=(dc == 0), stop=(dc == DC - 1)),
                    waits=swaits if dc == 0 else [], incs=[(sT, 1)] if dc == DC - 1 else [])
            mark(f"sc_mm_{which}_{t}_{e_}", sT)
            B.rec("vector", lambda e, e_=e_, pSC=pSC, sbs=sbs: e.scalar_tensor_tensor(
                stile[:, e_, :], pSC[:], 0.0, sbs, ALU.max, ALU.mult),
                waits=[w(f"sc_mm_{which}_{t}_{e_}"), w(f"selb_{which}_{t}_{e_}")],
                incs=[(sV, 1)])
            mark(f"stile_{which}_{t}_{e_}", sV)
        ev["pT_busy"] = (sS, ev[f"selb_{which}_{t}_{E16 - 1}"][1])
        ev["sel_row_free"] = (sT, ev[f"selb_mm_{which}_{t}_{E16 - 1}"][1])
        ev["pa_free"] = (sV, ev[f"stile_{which}_{t}_{E16 - 1}"][1])
        for c in range(NCH):
            waits = [w(f"stile_{which}_{t}_{E16 - 1}")]
            if pass_no == 1 and t == 0 and c == 0:
                waits.append(w("vals_done"))
            if ev.get("pv_free"):
                waits.append(ev["pv_free"])
            for e_ in range(E16):
                for gg in range(2):
                    pV = pC if gg == 0 else pD
                    B.rec("tensor", lambda e, c=c, e_=e_, gg=gg, pV=pV: e.matmul(
                        pV[:], stile[:, e_, c * 128:(c + 1) * 128],
                        vals_sb[:, e_, gg * 512:(gg + 1) * 512],
                        start=(e_ == 0), stop=(e_ == E16 - 1)),
                        waits=waits if (e_ == 0 and gg == 0) else [],
                        incs=[(sT, 1)] if (e_ == E16 - 1 and gg == 1) else [])
            mark(f"ffnv_{which}_{t}_{c}", sT)
            ewaits = [w(f"ffnv_{which}_{t}_{c}")]
            for gg in range(2):
                pV = pC if gg == 0 else pD
                B.rec("vector", lambda e, c=c, gg=gg, pV=pV: e.scalar_tensor_tensor(
                    out_dst[:, c, gg * 512:(gg + 1) * 512], pV[:], 0.5,
                    resid_src[:, c, gg * 512:(gg + 1) * 512], ALU.mult, ALU.add),
                    waits=ewaits,
                    incs=[(sV, 1)] if gg == 1 else [])
            mark(f"{which}_out_{t}_{c}", sV)
            ev["pv_free"] = (sV, sV.n)
            if post_evac is not None:
                post_evac(c)

    def rec_pass1_tile(t):
        if t == 0:
            for c in range(NCH):
                rec_bn(sa, c, (bnb0, st40), waits=[w(f"xin_0_{c}")])
            rec_ln_finish(0, "ln0", (bnb0, st40))
        rec_ln_apply_transpose(t, sa, "ln0", g=st40)
        def p1_post_evac(c):
            rec_bn(sb, c, (bnb1, st41))
            if c == 1:
                rec_ln_finish_pair(t, "ln1", (bnb1, st41), 0)
            if c == 2:
                rec_ln_apply_T_pair(t, sb, "ln1", st41, 0)
            if c == 3:
                rec_ln_finish_pair(t, "ln1", (bnb1, st41), 1)

        rec_ffn(t, "ffn1", "ln0", sa, sb, 1, post_evac=p1_post_evac)
        for c in range(NCH):
            B.rec("gpsimd", lambda e, t=t, c=c: e.dma_start(
                out=x1_d[(t * NCH + c) * 128:(t * NCH + c + 1) * 128, :], in_=sb[:, c, :]),
                waits=[w(f"ffn1_out_{t}_{c}")], incs=[(sOUT, 16)])
            mark(f"x1st_{t}_{c}", sOUT)
        if t > 0:
            B.rec("vector", None, waits=[w(f"x1st_{t - 1}_{NCH - 1}")])
        for c in range(NCH):
            if c == 2:
                rec_ln_apply_T_pair(t, sb, "ln1", st41, 1)
            waits = [w(f"ln1_fm_{t}_{c}"), w("wkv_done"),
                     w("ones_done"), ev["pa_free"], ev["pv_free"]]
            lnc = ln_fm.ap()[:, :, c * 128:(c + 1) * 128]
            for dc in range(DC):
                for g_, pX, wsb in ((0, pA, wk_sb), (1, pB, wk_sb), (2, pC, wv_sb), (3, pD, wv_sb)):
                    gg = g_ % 2
                    B.rec("tensor", lambda e, dc=dc, pX=pX, wsb=wsb, gg=gg, lnc=lnc: e.matmul(
                        pX[:], lnc[:, dc, :], wsb[:, dc, gg * 512:(gg + 1) * 512],
                        start=(dc == 0), stop=False),
                        waits=waits if (dc == 0 and g_ == 0) else [])
            for g_, pX, ro in ((0, pA, 0), (1, pB, 512), (2, pC, D), (3, pD, D + 512)):
                B.rec("tensor", lambda e, pX=pX, ro=ro: e.matmul(
                    pX[:], ones[0:1, :], brows_kv[0:1, ro:ro + 512], start=False, stop=True),
                    incs=[(sT, 1)] if g_ == 3 else [])
            mark(f"kvproj_{t}_{c}", sT)
            par = c % 2
            for gg, pX in ((0, pC), (1, pD)):
                B.rec("vector", lambda e, gg=gg, pX=pX, par=par: e.tensor_copy(
                    vtil[:, par, gg * 512:(gg + 1) * 512], pX[:]),
                    waits=[w(f"kvproj_{t}_{c}")] if gg == 0 else [])
            for gg, pX in ((0, pA), (1, pB)):
                B.rec("vector", lambda e, gg=gg, pX=pX, par=par: e.tensor_scalar_max(
                    ktil[:, par, gg * 512:(gg + 1) * 512], pX[:], 0.0))
                kw = []
                if t * NCH + c >= 2:
                    pt, pc = divmod(t * NCH + c - 2, NCH)
                    kw = [w(f"kexp_{pt}_{pc}")]
                B.rec("vector", lambda e, gg=gg, pX=pX, par=par: e.tensor_scalar_min(
                    scr2k[:, par, gg * 512:(gg + 1) * 512], pX[:], 0.0),
                    waits=kw if gg == 0 else [],
                    incs=[(sV, 1)] if gg == 1 else [])
            mark(f"kv_free_{t}_{c}", sV)
            ev["pa_free"] = (sV, sV.n)
            ev["pv_free"] = (sV, sV.n)
            B.rec("scalar", lambda e, par=par: e.activation(
                scr2k[:, par, :], scr2k[:, par, :], AF.Exp),
                waits=[w(f"kv_free_{t}_{c}")], incs=[(sS, 1)])
            mark(f"kexp_{t}_{c}", sS)
            B.rec("vector", lambda e, par=par: e.tensor_tensor(
                ktil[:, par, :], ktil[:, par, :], scr2k[:, par, :], ALU.add),
                waits=[w(f"kexp_{t}_{c}")], incs=[(sV, 1)])
            mark(f"kv_in_{t}_{c}", sV)

            def rec_kvmm(tc):
                tt_, cc_ = divmod(tc, NCH)
                par2 = cc_ % 2
                first = tc == 0
                last = tc == nt * NCH - 1
                for h in range(E16):
                    B.rec("tensor", lambda e, h=h, first=first, last=last, par2=par2: e.matmul(
                        pkv[0:64, h * 64:(h + 1) * 64], ktil[:, par2, h * 64:(h + 1) * 64],
                        vtil[:, par2, h * 64:(h + 1) * 64],
                        start=(first and h % 8 == 0), stop=(last and h % 8 == 7),
                        skip_group_check=True),
                        waits=[w(f"kv_in_{tt_}_{cc_}")] if h == 0 else [],
                        incs=[(sT, 1)] if h == E16 - 1 else [])
                mark(f"kvacc_{tt_}_{cc_}", sT)

            tc = t * NCH + c
            if tc >= 1:
                rec_kvmm(tc - 1)
            if tc == nt * NCH - 1:
                rec_kvmm(tc)
        if t + 1 < nt:
            for c in range(NCH):
                B.rec("sync", lambda e, t=t, c=c: e.dma_start(
                    out=sa[:, c, :], in_=x_d[((t + 1) * NCH + c) * 128:((t + 1) * NCH + c + 1) * 128, :]),
                    waits=[w(f"ffn1_out_{t}_{c}")], incs=[(sIN, 16)])
                mark(f"xin_{t + 1}_{c}", sIN)
            for c in range(NCH):
                rec_bn(sa, c, (bnb0, st40), waits=[w(f"xin_{t + 1}_{c}")])
            rec_ln_finish(t + 1, "ln0", (bnb0, st40))

    def rec_kv_finalize():
        B.rec("vector", lambda e: e.tensor_copy(kvbf[:, :], pkv[:, :]),
              waits=[w(f"kvacc_{nt - 1}_{NCH - 1}")])
        B.rec("vector", lambda e: e.memset(kvbd[:, :, :], 0.0), incs=[(sV, 1)])
        mark("kvbf", sV)
        for cc in range(DC):
            B.rec("gpsimd", lambda e, cc=cc: e.dma_start(
                out=kvbd[0:64, cc, 0:64], in_=kvbf[0:64, cc * 128:cc * 128 + 64]),
                waits=[w("kvbf")], incs=[(sOUT, 16)])
            B.rec("gpsimd", lambda e, cc=cc: e.dma_start(
                out=kvbd[64:128, cc, 64:128], in_=kvbf[0:64, cc * 128 + 64:cc * 128 + 128]),
                incs=[(sOUT, 16)])
        mark("kvbd", sOUT)

    def rec_phase2_weights():
        kvp = w(f"kvproj_{nt - 1}_{NCH - 1}")
        kva = w(f"kvacc_{nt - 1}_{NCH - 1}")
        for i, (wd, wsb, sw) in enumerate(((wq_d, wq_sb, sWD), (wg_d, wg_sb, sWE), (wo_d, wo_sb, sWF))):
            for dc in range(DC):
                waits = []
                if dc == 0:
                    waits = [kvp] if i < 2 else [kva, w("kvbd")]
                B.rec("gpsimd", lambda e, wd=wd, dc=dc, wsb=wsb: e.dma_start(
                    out=wsb[:, dc, :], in_=wd[dc * 128:(dc + 1) * 128, :]),
                    waits=waits, incs=[(sw, 16)])
            mark(f"p2w_{i}", sw)

    def rec_pass2_tile(t):
        q2 = stile.ap()[:, 0:4, :].rearrange("p a b -> p (a b)").bitcast(F32)
        q3 = stile.ap()[:, 4:8, :].rearrange("p a b -> p (a b)").bitcast(F32)
        for c in range(NCH):
            if t == 0:
                waits = [w(f"x1st_{nt - 1}_{NCH - 1}")]
            else:
                waits = [w(f"yst_{t - 1}_{c}")]
            B.rec("sync", lambda e, t=t, c=c: e.dma_start(
                out=sa[:, c, :], in_=x1_d[(t * NCH + c) * 128:(t * NCH + c + 1) * 128, :]),
                waits=waits, incs=[(sIN, 16)])
            mark(f"x1in_{t}_{c}", sIN)
        B.rec("vector", None, waits=[w(f"x1in_{t}_{NCH - 1}")])
        rec_ln_apply_transpose(t, sa, "ln1b",
                               mcol=lambda c: m1a.ap()[:, NCH * t + c:NCH * t + c + 1],
                               rcol=lambda c: r1a.ap()[:, NCH * t + c:NCH * t + c + 1])
        for fc in range(DC):
            par = fc % 2
            pQ = pA if par == 0 else pB
            pG = pC if par == 0 else pD
            qwaits = [w(f"ln1b_fm_{t}"), w("p2w_0"), w("bcols_done")]
            if ev.get(f"pq_free_{par}"):
                qwaits.append(ev[f"pq_free_{par}"])
            for dc in range(DC):
                B.rec("tensor", lambda e, fc=fc, dc=dc, pQ=pQ: e.matmul(
                    pQ[:], wq_sb[:, dc, fc * 128:(fc + 1) * 128], ln_fm[:, dc, :],
                    start=(dc == 0), stop=(dc == DC - 1)),
                    waits=qwaits if dc == 0 else [],
                    incs=[(sT, 1)] if dc == DC - 1 else [])
            mark(f"qmm_{t}_{fc}", sT)
            gwaits = [w("p2w_1")]
            if ev.get(f"pg_free_{par}"):
                gwaits.append(ev[f"pg_free_{par}"])
            for dc in range(DC):
                B.rec("tensor", lambda e, fc=fc, dc=dc, pG=pG: e.matmul(
                    pG[:], wg_sb[:, dc, fc * 128:(fc + 1) * 128], ln_fm[:, dc, :],
                    start=(dc == 0), stop=(dc == DC - 1)),
                    waits=gwaits if dc == 0 else [],
                    incs=[(sT, 1)] if dc == DC - 1 else [])
            mark(f"gmm_{t}_{fc}", sT)
            sh = par * 512
            B.rec("vector", lambda e, fc=fc, pQ=pQ, sh=sh: e.tensor_scalar(
                q3[:, sh:sh + 512], pQ[:], bcols.ap()[:, fc:fc + 1], 0.0,
                ALU.add, ALU.max),
                waits=[w(f"qmm_{t}_{fc}")])
            B.rec("vector", lambda e, fc=fc, pQ=pQ, sh=sh: e.tensor_scalar(
                q2[:, sh:sh + 512], pQ[:], bcols.ap()[:, fc:fc + 1], 0.0,
                ALU.add, ALU.min),
                incs=[(sV, 1)])
            mark(f"qmin_{t}_{fc}", sV)
            B.rec("scalar", lambda e, sh=sh: e.activation(
                q2[:, sh:sh + 512], q2[:, sh:sh + 512], AF.Exp),
                waits=[w(f"qmin_{t}_{fc}")], incs=[(sS, 1)])
            mark(f"qexp_{t}_{fc}", sS)
            B.rec("vector", lambda e, fc=fc, sh=sh: e.tensor_add(
                qt[:, fc, :], q3[:, sh:sh + 512], q2[:, sh:sh + 512]),
                waits=[w(f"qexp_{t}_{fc}")], incs=[(sV, 1)])
            mark(f"qt_{t}_{fc}", sV)
            twaits = [w(f"gmm_{t}_{fc}")]
            if ev.get(f"th_free_{par}"):
                twaits.append(ev[f"th_free_{par}"])
            B.rec("scalar", lambda e, fc=fc, pG=pG, par=par: e.activation(
                th[:, par, :], pG[:], AF.Tanh, scale=0.5,
                bias=bcols.ap()[:, 16 + fc:17 + fc]),
                waits=twaits, incs=[(sS, 1)])
            mark(f"th_{t}_{fc}", sS)
            ev[f"pg_free_{par}"] = (sS, sS.n)
            B.rec("tensor", lambda e, fc=fc, pQ=pQ: e.matmul(
                pQ[:], kvbd[:, fc, :], qt[:, fc, :], start=True, stop=True),
                waits=[w(f"qt_{t}_{fc}"), w("kvbd")], incs=[(sT, 1)])
            mark(f"attn_{t}_{fc}", sT)
            B.rec("vector", lambda e, fc=fc, pQ=pQ, par=par: e.scalar_tensor_tensor(
                qt[:, fc, :], th[:, par, :], 1.0, pQ[:], ALU.add, ALU.mult),
                waits=[w(f"attn_{t}_{fc}"), w(f"th_{t}_{fc}")], incs=[(sV, 1)])
            mark(f"gt_{t}_{fc}", sV)
            ev[f"pq_free_{par}"] = (sV, sV.n)
            ev[f"th_free_{par}"] = (sV, sV.n)
        ev["pa_free"] = (sV, sV.n)
        for c in range(NCH):
            waits = [w(f"gt_{t}_{DC - 1}"), w("p2w_2")]
            if ev.get("pv_free"):
                waits.append(ev["pv_free"])
            for fc in range(DC):
                for gg, pX in ((0, pC), (1, pD)):
                    B.rec("tensor", lambda e, c=c, fc=fc, gg=gg, pX=pX: e.matmul(
                        pX[:], qt[:, fc, c * 128:(c + 1) * 128],
                        wo_sb[:, fc, gg * 512:(gg + 1) * 512],
                        start=(fc == 0), stop=False),
                        waits=waits if (fc == 0 and gg == 0) else [])
            for gg, pX in ((0, pC), (1, pD)):
                B.rec("tensor", lambda e, gg=gg, pX=pX: e.matmul(
                    pX[:], ones[0:1, :], brow_bo[0:1, gg * 512:(gg + 1) * 512],
                    start=False, stop=True),
                    incs=[(sT, 1)] if gg == 1 else [])
            mark(f"womm_{t}_{c}", sT)
            for gg, pX in ((0, pC), (1, pD)):
                B.rec("vector", lambda e, c=c, gg=gg, pX=pX: e.scalar_tensor_tensor(
                    sb[:, c, gg * 512:(gg + 1) * 512], pX[:], 0.5,
                    sa[:, c, gg * 512:(gg + 1) * 512], ALU.mult, ALU.add),
                    waits=[w(f"womm_{t}_{c}")],
                    incs=[(sV, 1)] if gg == 1 else [])
            mark(f"x2_{t}_{c}", sV)
            ev["pv_free"] = (sV, sV.n)
            rec_bn(sb, c, (bnb0, st40))
        rec_ln_finish(t, "ln2", (bnb0, st40))
        rec_ln_apply_transpose(t, sb, "ln2", g=st40)
        rec_ffn(t, "ffn2", "ln2", sb, sa, 2)
        for c in range(NCH):
            B.rec("gpsimd", lambda e, t=t, c=c: e.dma_start(
                out=y_d[(t * NCH + c) * 128:(t * NCH + c + 1) * 128, :], in_=sa[:, c, :]),
                waits=[w(f"ffn2_out_{t}_{c}")], incs=[(sOUT, 16)])
            mark(f"yst_{t}_{c}", sOUT)

    with contextlib.ExitStack() as p1:
        wk_sb = p1.enter_context(nc.sbuf_tensor("wk_sb", [128, DC, D], BF16))
        wv_sb = p1.enter_context(nc.sbuf_tensor("wv_sb", [128, DC, D], BF16))
        brows_kv = p1.enter_context(nc.sbuf_tensor("brows_kv", [1, 2 * D], BF16))
        scr2k = p1.enter_context(nc.sbuf_tensor("scr2k", [128, 2, D], F32))
        ktil = p1.enter_context(nc.sbuf_tensor("ktil", [128, 2, D], BF16))
        vtil = p1.enter_context(nc.sbuf_tensor("vtil", [128, 2, D], BF16))
        kvbf = p1.enter_context(nc.sbuf_tensor("kvbf", [64, 1024], BF16))
        pkv = p1.enter_context(nc.psum_tensor("pkv", [64, 1024], F32))
        rec_startup()
        for t in range(nt):
            rec_pass1_tile(t)
        rec_kv_finalize()
        B.flush()
    with contextlib.ExitStack() as p2:
        pE = p2.enter_context(nc.psum_tensor("pE", [128, 512], F32))
        pF = p2.enter_context(nc.psum_tensor("pF", [128, 512], F32))
        wq_sb = p2.enter_context(nc.sbuf_tensor("wq_sb", [128, DC, D], BF16))
        wg_sb = p2.enter_context(nc.sbuf_tensor("wg_sb", [128, DC, D], BF16))
        wo_sb = p2.enter_context(nc.sbuf_tensor("wo_sb", [128, DC, D], BF16))
        rec_phase2_weights()
        for t in range(nt):
            rec_pass2_tile(t)
        B.rec("gpsimd", lambda e: e.sem_inc(sG.h, 1),
              waits=[(sOUT, sOUT.n), (sIN, sIN.n)])
        B.flush()
    em.close()
    return nc


_PROG_CACHE = {}


def _get_prog(nt):
    if nt not in _PROG_CACHE:
        _PROG_CACHE[nt] = build_program(nt)
    return _PROG_CACHE[nt]


def kernel(L0_attn_x, L0_attn_Wq, L0_attn_bq, L0_attn_Wk, L0_attn_bk,
           L0_attn_Wv, L0_attn_bv, L0_attn_Wg, L0_attn_bg, L0_attn_Wo,
           L0_attn_bo, L0_ffn_expert_sel, L0_ffn_keys, L0_ffn_values):
    x = np.ascontiguousarray(np.asarray(L0_attn_x, dtype=np.float32))
    Bz, S, Dm = x.shape
    assert Dm == D and S % TT == 0
    nt = S // TT
    nc = _get_prog(nt)
    common = {
        "wq": np.ascontiguousarray(L0_attn_Wq, dtype=np.float32),
        "wk": np.ascontiguousarray(L0_attn_Wk, dtype=np.float32),
        "wv": np.ascontiguousarray(L0_attn_Wv, dtype=np.float32),
        "wg": np.ascontiguousarray(L0_attn_Wg, dtype=np.float32),
        "wo": np.ascontiguousarray(L0_attn_Wo, dtype=np.float32),
        "bq": np.ascontiguousarray(L0_attn_bq, dtype=np.float32),
        "bk": np.ascontiguousarray(L0_attn_bk, dtype=np.float32),
        "bv": np.ascontiguousarray(L0_attn_bv, dtype=np.float32),
        "bg": np.ascontiguousarray(L0_attn_bg, dtype=np.float32),
        "bo": np.ascontiguousarray(L0_attn_bo, dtype=np.float32),
        "es": np.ascontiguousarray(L0_ffn_expert_sel, dtype=np.float32),
        "keys": np.ascontiguousarray(L0_ffn_keys, dtype=np.float32),
        "vals": np.ascontiguousarray(L0_ffn_values, dtype=np.float32),
    }
    in_maps = [dict(common, x=np.ascontiguousarray(x[b])) for b in range(Bz)]
    res = run_bass_kernel_spmd(nc, in_maps, core_ids=list(range(Bz)))
    out = np.stack([np.asarray(r["y"], dtype=np.float32) for r in res.results], axis=0)
    return out



# version 17
# speedup vs baseline: 1.0234x; 1.0124x over previous
import contextlib

import numpy as np
import ml_dtypes

import concourse.bass as bass
import concourse.mybir as mybir
from concourse.bass_utils import run_bass_kernel_spmd

F32 = mybir.dt.float32
BF16 = mybir.dt.bfloat16
I32 = mybir.dt.int32
AF = mybir.ActivationFunctionType
ALU = mybir.AluOpType

D = 1024
DC = 8
E16 = 16
TT = 512
NCH = 4
ENGINES = ("sync", "tensor", "vector", "scalar", "gpsimd")


class Sem:
    __slots__ = ("h", "n", "name")

    def __init__(self, h, name):
        self.h = h
        self.n = 0
        self.name = name


class Builder:
    def __init__(self, nc):
        self.nc = nc
        self.ops = {e: [] for e in ENGINES}
        self.waited = {}

    def rec(self, eng, fn, waits=(), incs=()):
        w = [(s.h, int(v)) for (s, v) in waits]
        i = [(s.h, int(a)) for (s, a) in incs]
        assert fn is not None or not i
        for s, a in incs:
            s.n += a
        self.ops[eng].append((fn, w, i))

    def flush(self):
        blk_ops = self.ops
        self.ops = {e: [] for e in ENGINES}
        waited = self.waited
        with self.nc.Block() as block:
            for eng_name in ENGINES:
                entries = blk_ops[eng_name]
                if not entries:
                    continue

                def body(e, entries=entries, eng_name=eng_name):
                    for fn, waits, incs in entries:
                        for (h, v) in waits:
                            key = (eng_name, h.num)
                            if waited.get(key, -1) < v:
                                e.wait_ge(h, v)
                                waited[key] = v
                        if fn is None:
                            continue
                        ins = fn(e)
                        for (h, a) in incs:
                            if a == 16:
                                ins.then_inc(h, a)
                            else:
                                e.maybe_drain_then_inc((h, a), fusable=True)

                getattr(block, eng_name)(body)


def build_program(nt):
    S = nt * TT
    nc = bass.Bass("TRN2", target_bir_lowering=False, detect_race_conditions=False)

    x_d = nc.dram_tensor("x", [S, D], F32, kind="ExternalInput").ap()
    y_d = nc.dram_tensor("y", [S, D], F32, kind="ExternalOutput").ap()
    wq_d = nc.dram_tensor("wq", [D, D], F32, kind="ExternalInput").ap()
    wk_d = nc.dram_tensor("wk", [D, D], F32, kind="ExternalInput").ap()
    wv_d = nc.dram_tensor("wv", [D, D], F32, kind="ExternalInput").ap()
    wg_d = nc.dram_tensor("wg", [D, D], F32, kind="ExternalInput").ap()
    wo_d = nc.dram_tensor("wo", [D, D], F32, kind="ExternalInput").ap()
    bq_d = nc.dram_tensor("bq", [D], F32, kind="ExternalInput").ap()
    bk_d = nc.dram_tensor("bk", [D], F32, kind="ExternalInput").ap()
    bv_d = nc.dram_tensor("bv", [D], F32, kind="ExternalInput").ap()
    bg_d = nc.dram_tensor("bg", [D], F32, kind="ExternalInput").ap()
    bo_d = nc.dram_tensor("bo", [D], F32, kind="ExternalInput").ap()
    es_d = nc.dram_tensor("es", [E16, D], F32, kind="ExternalInput").ap()
    keys_d = nc.dram_tensor("keys", [E16, D, 128], F32, kind="ExternalInput").ap()
    vals_d = nc.dram_tensor("vals", [E16, 128, D], F32, kind="ExternalInput").ap()
    x1_d = nc.dram_tensor("x1s", [S, D], F32, kind="Internal").ap()
    ident_d = nc.inline_tensor(np.eye(128, dtype=ml_dtypes.bfloat16), name="ident_dr")

    B = Builder(nc)
    sIN = Sem(nc.alloc_semaphore("sIN"), "sIN")
    sOUT = Sem(nc.alloc_semaphore("sOUT"), "sOUT")
    sT = Sem(nc.alloc_semaphore("sT"), "sT")
    sV = Sem(nc.alloc_semaphore("sV"), "sV")
    sS = Sem(nc.alloc_semaphore("sS"), "sS")
    sG = Sem(nc.alloc_semaphore("sG"), "sG")
    sCFG = Sem(nc.alloc_semaphore("sCFG"), "sCFG")
    sWA = Sem(nc.alloc_semaphore("sWA"), "sWA")
    sWB = Sem(nc.alloc_semaphore("sWB"), "sWB")
    sWC = Sem(nc.alloc_semaphore("sWC"), "sWC")
    sWD = Sem(nc.alloc_semaphore("sWD"), "sWD")
    sWE = Sem(nc.alloc_semaphore("sWE"), "sWE")
    sWF = Sem(nc.alloc_semaphore("sWF"), "sWF")

    em = contextlib.ExitStack()
    E = em.enter_context
    ident = E(nc.sbuf_tensor("ident", [128, 128], BF16))
    ones = E(nc.sbuf_tensor("ones", [65, 128], BF16))
    brow_bo = E(nc.sbuf_tensor("brow_bo", [1, D], BF16))
    bcols = E(nc.sbuf_tensor("bcols", [128, 32], F32))
    keys_sb = E(nc.sbuf_tensor("keys_sb", [128, DC, 2048], BF16))
    vals_sb = E(nc.sbuf_tensor("vals_sb", [128, E16, D], BF16))
    es_sb = E(nc.sbuf_tensor("es_sb", [128, DC, E16], BF16))
    sa = E(nc.sbuf_tensor("sa", [128, NCH, D], F32))
    sb = E(nc.sbuf_tensor("sb", [128, NCH, D], F32))
    ln_tm = E(nc.sbuf_tensor("ln_tm", [128, 2, D], BF16))
    ln_fm = E(nc.sbuf_tensor("ln_fm", [128, DC, TT], BF16))
    sel_bf = E(nc.sbuf_tensor("sel_bf", [E16, TT], BF16))
    sel_row = E(nc.sbuf_tensor("sel_row", [65, 6 * TT], BF16))
    selb = E(nc.sbuf_tensor("selb", [128, 3, TT], BF16))
    stile = E(nc.sbuf_tensor("stile", [128, E16, TT], BF16))
    scr1 = E(nc.sbuf_tensor("scr1", [16, 512], F32))
    bnb0 = E(nc.sbuf_tensor("bnb0", [128, 8, 6], F32))
    st40 = E(nc.sbuf_tensor("st40", [128, 28], F32))
    bnb1 = E(nc.sbuf_tensor("bnb1", [128, 8, 6], F32))
    st41 = E(nc.sbuf_tensor("st41", [128, 28], F32))
    m1a = E(nc.sbuf_tensor("m1a", [128, NCH * nt], F32))
    r1a = E(nc.sbuf_tensor("r1a", [128, NCH * nt], F32))
    qt = E(nc.sbuf_tensor("qt", [128, DC, TT], BF16))
    th = E(nc.sbuf_tensor("th", [128, 2, TT], BF16))
    kvbd = E(nc.sbuf_tensor("kvbd", [128, DC, 128], BF16))
    pT1 = E(nc.psum_tensor("pT1", [128, 1024], BF16))
    pT2 = E(nc.psum_tensor("pT2", [128, 1024], BF16))
    pA = E(nc.psum_tensor("pA", [128, 512], F32))
    pB = E(nc.psum_tensor("pB", [128, 512], F32))
    pC = E(nc.psum_tensor("pC", [128, 512], F32))
    pD = E(nc.psum_tensor("pD", [128, 512], F32))

    ev = {}

    def mark(name, sem):
        ev[name] = (sem, sem.n)

    def w(name):
        return ev[name]

    def rec_startup():
        B.rec("scalar", lambda e: e.preload_act_table(AF.Exp))
        for c in range(NCH):
            B.rec("sync", lambda e, c=c: e.dma_start(out=sa[:, c, :], in_=x_d[c * 128:(c + 1) * 128, :]),
                  incs=[(sIN, 16)])
            mark(f"xin_0_{c}", sIN)
        B.rec("sync", lambda e: e.dma_start(out=ident[:], in_=ident_d.ap()), incs=[(sCFG, 16)])
        mark("ident_in", sCFG)
        B.rec("sync", lambda e: e.dma_start(out=bcols[:, 0:8], in_=bq_d.rearrange("(c p) -> p c", p=128), allow_slow_non_contiguous=True),
              incs=[(sCFG, 16)])
        B.rec("sync", lambda e: e.dma_start(out=bcols[:, 16:24], in_=bg_d.rearrange("(c p) -> p c", p=128), allow_slow_non_contiguous=True),
              incs=[(sCFG, 16)])
        B.rec("sync", lambda e: e.dma_start(out=stile.ap()[0:1, 0:4, :].rearrange('p a b -> p (a b)').bitcast(F32)[0:1, 0:D], in_=bo_d.unsqueeze(0)), incs=[(sCFG, 16)])
        mark("bo_in", sCFG)
        B.rec("vector", lambda e: e.memset(ones[:, :], 1.0))
        B.rec("vector", lambda e: e.tensor_scalar_mul(bcols[:, 8:16], bcols[:, 0:8], -1.0),
              waits=[w("bo_in")])
        B.rec("vector", lambda e: e.tensor_scalar_mul(bcols[:, 16:24], bcols[:, 16:24], 0.5))
        B.rec("vector", lambda e: e.tensor_scalar_mul(stile.ap()[0:1, 0:4, :].rearrange('p a b -> p (a b)').bitcast(F32)[0:1, 0:D], stile.ap()[0:1, 0:4, :].rearrange('p a b -> p (a b)').bitcast(F32)[0:1, 0:D], 2.0))
        B.rec("vector", lambda e: e.tensor_copy(brow_bo[0:1, 0:D], stile.ap()[0:1, 0:4, :].rearrange('p a b -> p (a b)').bitcast(F32)[0:1, 0:D]), incs=[(sV, 1)])
        mark("cfg_done", sV)
        mark("ones_done", sV)
        mark("bcols_done", sV)
        for dc in range(DC):
            B.rec("gpsimd", lambda e, dc=dc: e.dma_start(
                out=keys_sb[:, dc, :],
                in_=keys_d[:, dc * 128:(dc + 1) * 128, :].transpose([1, 0, 2])),
                incs=[(sWA, 16)])
        mark("keys_done", sWA)
        for dc in range(DC):
            B.rec("gpsimd", lambda e, dc=dc: e.dma_start(
                out=es_sb[:, dc, :],
                in_=es_d[:, dc * 128:(dc + 1) * 128].transpose([1, 0]),
                allow_slow_non_contiguous=True),
                incs=[(sWA, 16)])
        mark("es_done", sWA)
        for e_ in range(E16):
            B.rec("gpsimd", lambda e, e_=e_: e.dma_start(
                out=vals_sb[:, e_, :], in_=vals_d[e_]), incs=[(sWB, 16)])
        mark("vals_done", sWB)
        for name, wd, wsb in (("wk", wk_d, wk_sb), ("wv", wv_d, wv_sb)):
            for dc in range(DC):
                B.rec("gpsimd", lambda e, wd=wd, dc=dc, wsb=wsb: e.dma_start(
                    out=wsb[:, dc, :], in_=wd[dc * 128:(dc + 1) * 128, :]),
                    incs=[(sWC, 16)])
        B.rec("gpsimd", lambda e: e.dma_start(out=brows_kv[0:1, 0:D], in_=bk_d.unsqueeze(0)),
              incs=[(sWC, 16)])
        B.rec("gpsimd", lambda e: e.dma_start(out=brows_kv[0:1, D:2 * D], in_=bv_d.unsqueeze(0)),
              incs=[(sWC, 16)])
        mark("wkv_done", sWC)

    def rec_bn(src, c, bufs, waits=()):
        bnb, g = bufs
        B.rec("vector", lambda e, c=c: e.bn_stats(
            bnb[:, 2 * c, :], src[:, c, 0:512]), waits=list(waits))
        B.rec("vector", lambda e, c=c: e.bn_stats(
            bnb[:, 2 * c + 1, :], src[:, c, 512:1024]))

    def rec_ln_finish(t, which, bufs):
        bnb, g = bufs
        V = "vector"

        def RD(fn):
            B.rec(V, fn)
            B.rec(V, lambda e: e.drain())

        B.rec(V, lambda e: e.drain())
        for c in range(NCH):
            B.rec(V, lambda e, c=c: e.bn_aggr(g[:, 2 * c:2 * c + 2], bnb[:, 2 * c:2 * c + 2, :]))
        B.rec(V, lambda e: e.drain())
        mvv = g.ap()[:, 1:8:2]
        us = g.ap()[:, 12:16]
        Bs = g.ap()[:, 16:20]
        Cs = g.ap()[:, 20:24]
        ys = g.ap()[:, 24:28]
        rs = g.ap()[:, 8:12]
        RD(lambda e: e.tensor_scalar_add(us, mvv, 1e-5))
        RD(lambda e: e.tensor_scalar(Bs.bitcast(I32), us.bitcast(I32), 1, None, ALU.arith_shift_right))
        RD(lambda e: e.tensor_scalar(Cs.bitcast(I32), Bs.bitcast(I32), -1, 0x5F3759DF, ALU.mult, ALU.add))
        for it in range(3):
            yin = Cs if it % 2 == 0 else ys
            yout = ys if it % 2 == 0 else Cs
            last = it == 2
            RD(lambda e, yin=yin: e.tensor_tensor(Bs, yin, yin, ALU.mult))
            RD(lambda e: e.tensor_tensor(rs, Bs, us, ALU.mult))
            RD(lambda e: e.tensor_scalar(Bs, rs, -0.5, 1.5, ALU.mult, ALU.add))
            RD(lambda e, yin=yin, yout=(rs if last else yout): e.tensor_tensor(yout, yin, Bs, ALU.mult))
        if which == "ln1":
            B.rec(V, lambda e, t=t: e.tensor_copy(m1a[:, NCH * t:NCH * (t + 1)], g.ap()[:, 0:8:2]))
            B.rec(V, lambda e, t=t: e.tensor_copy(r1a[:, NCH * t:NCH * (t + 1)], rs))
            B.rec(V, lambda e: e.drain())

    def rec_ln_finish_pair(t, which, bufs, p):
        bnb, g = bufs
        V = "vector"

        def RD(fn):
            B.rec(V, fn)
            B.rec(V, lambda e: e.drain())

        B.rec(V, lambda e: e.drain())
        for c in (2 * p, 2 * p + 1):
            B.rec(V, lambda e, c=c: e.bn_aggr(g[:, 2 * c:2 * c + 2], bnb[:, 2 * c:2 * c + 2, :]))
        B.rec(V, lambda e: e.drain())
        o = 2 * p
        mvv = g.ap()[:, 1 + 4 * p:8:2][:, 0:2]
        us = g.ap()[:, 12 + o:14 + o]
        Bs = g.ap()[:, 16 + o:18 + o]
        Cs = g.ap()[:, 20 + o:22 + o]
        ys = g.ap()[:, 24 + o:26 + o]
        rs = g.ap()[:, 8 + o:10 + o]
        RD(lambda e: e.tensor_scalar_add(us, mvv, 1e-5))
        RD(lambda e: e.tensor_scalar(Bs.bitcast(I32), us.bitcast(I32), 1, None, ALU.arith_shift_right))
        RD(lambda e: e.tensor_scalar(Cs.bitcast(I32), Bs.bitcast(I32), -1, 0x5F3759DF, ALU.mult, ALU.add))
        for it in range(3):
            yin = Cs if it % 2 == 0 else ys
            yout = ys if it % 2 == 0 else Cs
            last = it == 2
            RD(lambda e, yin=yin: e.tensor_tensor(Bs, yin, yin, ALU.mult))
            RD(lambda e: e.tensor_tensor(rs, Bs, us, ALU.mult))
            RD(lambda e: e.tensor_scalar(Bs, rs, -0.5, 1.5, ALU.mult, ALU.add))
            RD(lambda e, yin=yin, yout=(rs if last else yout): e.tensor_tensor(yout, yin, Bs, ALU.mult))
        if which == "ln1":
            B.rec(V, lambda e, t=t, p=p: e.tensor_copy(
                m1a[:, NCH * t + 2 * p:NCH * t + 2 * p + 2], g.ap()[:, 4 * p:4 * p + 4:2]))
            B.rec(V, lambda e, t=t, p=p: e.tensor_copy(
                r1a[:, NCH * t + 2 * p:NCH * t + 2 * p + 2], rs))
            B.rec(V, lambda e: e.drain())

    def rec_ln_apply_T_pair(t, src, which, g, p):
        for c in (2 * p, 2 * p + 1):
            mc, rc = g.ap()[:, 2 * c:2 * c + 1], g.ap()[:, 8 + c:9 + c]
            awaits = []
            if ev.get(f"ln_tm_free_{c % 2}"):
                awaits.append(ev[f"ln_tm_free_{c % 2}"])
            B.rec("vector", lambda e, c=c, mc=mc, rc=rc: e.tensor_scalar(
                ln_tm[:, c % 2, :], src[:, c, :], mc, rc, ALU.subtract, ALU.mult),
                waits=awaits, incs=[(sV, 1)])
            mark(f"{which}_tm_{t}_{c}", sV)
        for c in (2 * p, 2 * p + 1):
            pT = pT1 if c % 2 == 0 else pT2
            waits = [w(f"{which}_tm_{t}_{c}"), w("ident_in")]
            if ev.get(f"Tevac_prev_{c % 2}"):
                waits.append(ev[f"Tevac_prev_{c % 2}"])
            if ev.get("pT_busy"):
                waits.append(ev["pT_busy"])
            for j in range(DC):
                B.rec("tensor", lambda e, c=c, j=j, pT=pT: e.transpose(
                    pT[:, j * 128:(j + 1) * 128], ln_tm[:, c % 2, j * 128:(j + 1) * 128], ident[:]),
                    waits=waits if j == 0 else [], incs=[(sT, 1)] if j == DC - 1 else [])
            mark(f"{which}_T_{t}_{c}", sT)
            ev[f"ln_tm_free_{c % 2}"] = (sT, sT.n)
            B.rec("scalar", lambda e, c=c, pT=pT: e.copy(
                ln_fm.ap()[:, :, c * 128:(c + 1) * 128],
                pT.ap().rearrange("p (j k) -> p j k", j=DC)),
                waits=[w(f"{which}_T_{t}_{c}")], incs=[(sS, 1)])
            mark(f"{which}_fm_{t}_{c}", sS)
            ev[f"Tevac_prev_{c % 2}"] = (sS, sS.n)
        if p == 1:
            mark(f"{which}_fm_{t}", sS)

    def rec_ln_apply_transpose(t, src, which, mcol=None, rcol=None, g=None):
        for c in range(NCH):
            if mcol is None:
                mc, rc = g.ap()[:, 2 * c:2 * c + 1], g.ap()[:, 8 + c:9 + c]
            else:
                mc, rc = mcol(c), rcol(c)
            awaits = []
            if ev.get(f"ln_tm_free_{c % 2}"):
                awaits.append(ev[f"ln_tm_free_{c % 2}"])
            B.rec("vector", lambda e, c=c, mc=mc, rc=rc: e.tensor_scalar(
                ln_tm[:, c % 2, :], src[:, c, :], mc, rc, ALU.subtract, ALU.mult),
                waits=awaits, incs=[(sV, 1)])
            mark(f"{which}_tm_{t}_{c}", sV)
            pT = pT1 if c % 2 == 0 else pT2
            waits = [w(f"{which}_tm_{t}_{c}"), w("ident_in")]
            if ev.get(f"Tevac_prev_{c % 2}"):
                waits.append(ev[f"Tevac_prev_{c % 2}"])
            if ev.get("pT_busy"):
                waits.append(ev["pT_busy"])
            for j in range(DC):
                B.rec("tensor", lambda e, c=c, j=j, pT=pT: e.transpose(
                    pT[:, j * 128:(j + 1) * 128], ln_tm[:, c % 2, j * 128:(j + 1) * 128], ident[:]),
                    waits=waits if j == 0 else [], incs=[(sT, 1)] if j == DC - 1 else [])
            mark(f"{which}_T_{t}_{c}", sT)
            ev[f"ln_tm_free_{c % 2}"] = (sT, sT.n)
            B.rec("scalar", lambda e, c=c, pT=pT: e.copy(
                ln_fm.ap()[:, :, c * 128:(c + 1) * 128],
                pT.ap().rearrange("p (j k) -> p j k", j=DC)),
                waits=[w(f"{which}_T_{t}_{c}")], incs=[(sS, 1)])
            mark(f"{which}_fm_{t}_{c}", sS)
            ev[f"Tevac_prev_{c % 2}"] = (sS, sS.n)
        mark(f"{which}_fm_{t}", sS)

    def rec_ffn(t, which, lnname, resid_src, out_dst, pass_no, post_evac=None):
        waits = [w(f"{lnname}_fm_{t}"), w("es_done"), w("ident_in")]
        if ev.get("pa_free"):
            waits.append(ev["pa_free"])
        for dc in range(DC):
            B.rec("tensor", lambda e, dc=dc: e.matmul(
                pA[0:16, :], es_sb[:, dc, :], ln_fm[:, dc, :],
                start=(dc == 0), stop=(dc == DC - 1)),
                waits=waits if dc == 0 else [], incs=[(sT, 1)] if dc == DC - 1 else [])
        mark(f"sel_mm_{which}_{t}", sT)
        B.rec("scalar", lambda e: e.activation(scr1[0:16, 0:TT], pA[0:16, :], AF.Tanh, scale=0.5),
              waits=[w(f"sel_mm_{which}_{t}")], incs=[(sS, 1)])
        mark(f"sel_th_{which}_{t}", sS)
        B.rec("vector", lambda e: e.tensor_scalar(
            sel_bf[:, :], scr1[0:16, 0:TT], 0.5, 0.5, ALU.mult, ALU.add),
            waits=[w(f"sel_th_{which}_{t}")], incs=[(sV, 1)])
        mark(f"sel_{which}_{t}", sV)
        srw = [w(f"sel_{which}_{t}")]
        if ev.get("sel_row_free"):
            srw.append(ev["sel_row_free"])
        B.rec("gpsimd", lambda e: e.dma_start(out=sel_row[0:1, 0:6 * TT], in_=sel_bf[0:6, :]),
              waits=srw, incs=[(sOUT, 16)])
        B.rec("gpsimd", lambda e: e.dma_start(out=sel_row[32:33, 0:6 * TT], in_=sel_bf[6:12, :]),
              incs=[(sOUT, 16)])
        B.rec("gpsimd", lambda e: e.dma_start(out=sel_row[64:65, 0:4 * TT], in_=sel_bf[12:16, :]),
              incs=[(sOUT, 16)])
        mark(f"selrow_{which}_{t}", sOUT)
        nb = 2 if pass_no == 1 else 4
        banks = (pA, pB) if pass_no == 1 else (pA, pB, pE, pF)
        for e_ in range(E16):
            par = e_ % 2
            spar = e_ % 3
            pSB = (pT1 if par == 0 else pT2).ap().bitcast(F32)[:, 0:512]
            sbs = selb.ap()[:, spar, :]
            waits = [w(f"selrow_{which}_{t}"), w("ones_done")]
            if ev.get(f"selb_evac_{par}"):
                waits.append(ev[f"selb_evac_{par}"])
            sp = 32 * (e_ // 6) if e_ < 12 else 64
            so = (e_ % 6) * TT if e_ < 12 else (e_ - 12) * TT
            B.rec("tensor", lambda e, pSB=pSB, sp=sp, so=so: e.matmul(
                pSB, ones[sp:sp + 1, :], sel_row[sp:sp + 1, so:so + TT],
                start=True, stop=True),
                waits=waits, incs=[(sT, 1)])
            mark(f"selb_mm_{which}_{t}_{e_}", sT)
            ewaits = [w(f"selb_mm_{which}_{t}_{e_}")]
            if e_ >= 3:
                ewaits.append(w(f"stile_{which}_{t}_{e_ - 3}"))
            B.rec("scalar", lambda e, pSB=pSB, sbs=sbs: e.copy(sbs, pSB),
                  waits=ewaits, incs=[(sS, 1)])
            mark(f"selb_{which}_{t}_{e_}", sS)
            ev[f"selb_evac_{par}"] = (sS, sS.n)
            pSC = banks[e_ % nb]
            swaits = []
            if e_ < 2:
                swaits.append(w(f"sel_th_{which}_{t}"))
            if e_ >= nb:
                swaits.append(w(f"stile_{which}_{t}_{e_ - nb}"))
            if pass_no == 1 and t == 0 and e_ == 0:
                swaits.append(w("keys_done"))

            for dc in range(DC):
                B.rec("tensor", lambda e, e_=e_, dc=dc, pSC=pSC: e.matmul(
                    pSC[:], keys_sb[:, dc, e_ * 128:(e_ + 1) * 128], ln_fm[:, dc, :],
                    start=(dc == 0), stop=(dc == DC - 1)),
                    waits=swaits if dc == 0 else [], incs=[(sT, 1)] if dc == DC - 1 else [])
            mark(f"sc_mm_{which}_{t}_{e_}", sT)
            B.rec("vector", lambda e, e_=e_, pSC=pSC, sbs=sbs: e.scalar_tensor_tensor(
                stile[:, e_, :], pSC[:], 0.0, sbs, ALU.max, ALU.mult),
                waits=[w(f"sc_mm_{which}_{t}_{e_}"), w(f"selb_{which}_{t}_{e_}")],
                incs=[(sV, 1)])
            mark(f"stile_{which}_{t}_{e_}", sV)
        ev["pT_busy"] = (sS, ev[f"selb_{which}_{t}_{E16 - 1}"][1])
        ev["sel_row_free"] = (sT, ev[f"selb_mm_{which}_{t}_{E16 - 1}"][1])
        ev["pa_free"] = (sV, ev[f"stile_{which}_{t}_{E16 - 1}"][1])
        for c in range(NCH):
            if pass_no == 1 or c % 2 == 0:
                vb = (pC, pD)
            else:
                vb = (pE, pF)
            waits = [w(f"stile_{which}_{t}_{E16 - 1}")]
            if pass_no == 1 and t == 0 and c == 0:
                waits.append(w("vals_done"))
            if pass_no == 1 or c < 2:
                if ev.get("pv_free"):
                    waits.append(ev["pv_free"])
            else:
                waits.append(ev[f"pvv_free_{c % 2}"])
            for e_ in range(E16):
                for gg in range(2):
                    pV = vb[gg]
                    B.rec("tensor", lambda e, c=c, e_=e_, gg=gg, pV=pV: e.matmul(
                        pV[:], stile[:, e_, c * 128:(c + 1) * 128],
                        vals_sb[:, e_, gg * 512:(gg + 1) * 512],
                        start=(e_ == 0), stop=(e_ == E16 - 1)),
                        waits=waits if (e_ == 0 and gg == 0) else [],
                        incs=[(sT, 1)] if (e_ == E16 - 1 and gg == 1) else [])
            mark(f"ffnv_{which}_{t}_{c}", sT)
            ewaits = [w(f"ffnv_{which}_{t}_{c}")]
            for gg in range(2):
                pV = vb[gg]
                B.rec("vector", lambda e, c=c, gg=gg, pV=pV: e.scalar_tensor_tensor(
                    out_dst[:, c, gg * 512:(gg + 1) * 512], pV[:], 0.5,
                    resid_src[:, c, gg * 512:(gg + 1) * 512], ALU.mult, ALU.add),
                    waits=ewaits,
                    incs=[(sV, 1)] if gg == 1 else [])
            mark(f"{which}_out_{t}_{c}", sV)
            ev["pv_free"] = (sV, sV.n)
            ev[f"pvv_free_{c % 2}"] = (sV, sV.n)
            if post_evac is not None:
                post_evac(c)

    def rec_pass1_tile(t):
        if t == 0:
            for c in range(NCH):
                rec_bn(sa, c, (bnb0, st40), waits=[w(f"xin_0_{c}")])
            rec_ln_finish(0, "ln0", (bnb0, st40))
        rec_ln_apply_transpose(t, sa, "ln0", g=st40)
        def p1_post_evac(c):
            rec_bn(sb, c, (bnb1, st41))
            if c == 1:
                rec_ln_finish_pair(t, "ln1", (bnb1, st41), 0)
            if c == 2:
                rec_ln_apply_T_pair(t, sb, "ln1", st41, 0)
            if c == 3:
                rec_ln_finish_pair(t, "ln1", (bnb1, st41), 1)

        rec_ffn(t, "ffn1", "ln0", sa, sb, 1, post_evac=p1_post_evac)
        for c in range(NCH):
            B.rec("gpsimd", lambda e, t=t, c=c: e.dma_start(
                out=x1_d[(t * NCH + c) * 128:(t * NCH + c + 1) * 128, :], in_=sb[:, c, :]),
                waits=[w(f"ffn1_out_{t}_{c}")], incs=[(sOUT, 16)])
            mark(f"x1st_{t}_{c}", sOUT)
        if t > 0:
            B.rec("vector", None, waits=[w(f"x1st_{t - 1}_{NCH - 1}")])
        for c in range(NCH):
            if c == 2:
                rec_ln_apply_T_pair(t, sb, "ln1", st41, 1)
            waits = [w(f"ln1_fm_{t}_{c}"), w("wkv_done"),
                     w("ones_done"), ev["pa_free"], ev["pv_free"]]
            lnc = ln_fm.ap()[:, :, c * 128:(c + 1) * 128]
            for dc in range(DC):
                for g_, pX, wsb in ((0, pA, wk_sb), (1, pB, wk_sb), (2, pC, wv_sb), (3, pD, wv_sb)):
                    gg = g_ % 2
                    B.rec("tensor", lambda e, dc=dc, pX=pX, wsb=wsb, gg=gg, lnc=lnc: e.matmul(
                        pX[:], lnc[:, dc, :], wsb[:, dc, gg * 512:(gg + 1) * 512],
                        start=(dc == 0), stop=False),
                        waits=waits if (dc == 0 and g_ == 0) else [])
            for g_, pX, ro in ((0, pA, 0), (1, pB, 512), (2, pC, D), (3, pD, D + 512)):
                B.rec("tensor", lambda e, pX=pX, ro=ro: e.matmul(
                    pX[:], ones[0:1, :], brows_kv[0:1, ro:ro + 512], start=False, stop=True),
                    incs=[(sT, 1)] if g_ == 3 else [])
            mark(f"kvproj_{t}_{c}", sT)
            par = c % 2
            for gg, pX in ((0, pC), (1, pD)):
                B.rec("vector", lambda e, gg=gg, pX=pX, par=par: e.tensor_copy(
                    vtil[:, par, gg * 512:(gg + 1) * 512], pX[:]),
                    waits=[w(f"kvproj_{t}_{c}")] if gg == 0 else [])
            for gg, pX in ((0, pA), (1, pB)):
                B.rec("vector", lambda e, gg=gg, pX=pX, par=par: e.tensor_scalar_max(
                    ktil[:, par, gg * 512:(gg + 1) * 512], pX[:], 0.0))
                kw = []
                if t * NCH + c >= 2:
                    pt, pc = divmod(t * NCH + c - 2, NCH)
                    kw = [w(f"kexp_{pt}_{pc}")]
                B.rec("vector", lambda e, gg=gg, pX=pX, par=par: e.tensor_scalar_min(
                    scr2k[:, par, gg * 512:(gg + 1) * 512], pX[:], 0.0),
                    waits=kw if gg == 0 else [],
                    incs=[(sV, 1)] if gg == 1 else [])
            mark(f"kv_free_{t}_{c}", sV)
            ev["pa_free"] = (sV, sV.n)
            ev["pv_free"] = (sV, sV.n)
            B.rec("scalar", lambda e, par=par: e.activation(
                scr2k[:, par, :], scr2k[:, par, :], AF.Exp),
                waits=[w(f"kv_free_{t}_{c}")], incs=[(sS, 1)])
            mark(f"kexp_{t}_{c}", sS)
            B.rec("vector", lambda e, par=par: e.tensor_tensor(
                ktil[:, par, :], ktil[:, par, :], scr2k[:, par, :], ALU.add),
                waits=[w(f"kexp_{t}_{c}")], incs=[(sV, 1)])
            mark(f"kv_in_{t}_{c}", sV)

            def rec_kvmm(tc):
                tt_, cc_ = divmod(tc, NCH)
                par2 = cc_ % 2
                first = tc == 0
                last = tc == nt * NCH - 1
                for h in range(E16):
                    B.rec("tensor", lambda e, h=h, first=first, last=last, par2=par2: e.matmul(
                        pkv[0:64, h * 64:(h + 1) * 64], ktil[:, par2, h * 64:(h + 1) * 64],
                        vtil[:, par2, h * 64:(h + 1) * 64],
                        start=(first and h % 8 == 0), stop=(last and h % 8 == 7),
                        skip_group_check=True),
                        waits=[w(f"kv_in_{tt_}_{cc_}")] if h == 0 else [],
                        incs=[(sT, 1)] if h == E16 - 1 else [])
                mark(f"kvacc_{tt_}_{cc_}", sT)

            tc = t * NCH + c
            if tc >= 1:
                rec_kvmm(tc - 1)
            if tc == nt * NCH - 1:
                rec_kvmm(tc)
        if t + 1 < nt:
            for c in range(NCH):
                B.rec("sync", lambda e, t=t, c=c: e.dma_start(
                    out=sa[:, c, :], in_=x_d[((t + 1) * NCH + c) * 128:((t + 1) * NCH + c + 1) * 128, :]),
                    waits=[w(f"ffn1_out_{t}_{c}")], incs=[(sIN, 16)])
                mark(f"xin_{t + 1}_{c}", sIN)
            for c in range(NCH):
                rec_bn(sa, c, (bnb0, st40), waits=[w(f"xin_{t + 1}_{c}")])
            rec_ln_finish(t + 1, "ln0", (bnb0, st40))

    def rec_kv_finalize():
        B.rec("vector", lambda e: e.tensor_copy(kvbf[:, :], pkv[:, :]),
              waits=[w(f"kvacc_{nt - 1}_{NCH - 1}")])
        B.rec("vector", lambda e: e.memset(kvbd[:, :, :], 0.0), incs=[(sV, 1)])
        mark("kvbf", sV)
        for cc in range(DC):
            B.rec("gpsimd", lambda e, cc=cc: e.dma_start(
                out=kvbd[0:64, cc, 0:64], in_=kvbf[0:64, cc * 128:cc * 128 + 64]),
                waits=[w("kvbf")], incs=[(sOUT, 16)])
            B.rec("gpsimd", lambda e, cc=cc: e.dma_start(
                out=kvbd[64:128, cc, 64:128], in_=kvbf[0:64, cc * 128 + 64:cc * 128 + 128]),
                incs=[(sOUT, 16)])
        mark("kvbd", sOUT)

    def rec_phase2_weights():
        kvp = w(f"kvproj_{nt - 1}_{NCH - 1}")
        kva = w(f"kvacc_{nt - 1}_{NCH - 1}")
        for i, (wd, wsb, sw) in enumerate(((wq_d, wq_sb, sWD), (wg_d, wg_sb, sWE), (wo_d, wo_sb, sWF))):
            for dc in range(DC):
                waits = []
                if dc == 0:
                    waits = [kvp] if i < 2 else [kva, w("kvbd")]
                B.rec("gpsimd", lambda e, wd=wd, dc=dc, wsb=wsb: e.dma_start(
                    out=wsb[:, dc, :], in_=wd[dc * 128:(dc + 1) * 128, :]),
                    waits=waits, incs=[(sw, 16)])
            mark(f"p2w_{i}", sw)

    def rec_pass2_tile(t):
        q2 = stile.ap()[:, 0:4, :].rearrange("p a b -> p (a b)").bitcast(F32)
        q3 = stile.ap()[:, 4:8, :].rearrange("p a b -> p (a b)").bitcast(F32)
        for c in range(NCH):
            if t == 0:
                waits = [w(f"x1st_{nt - 1}_{NCH - 1}")]
            else:
                waits = [w(f"yst_{t - 1}_{c}")]
            B.rec("sync", lambda e, t=t, c=c: e.dma_start(
                out=sa[:, c, :], in_=x1_d[(t * NCH + c) * 128:(t * NCH + c + 1) * 128, :]),
                waits=waits, incs=[(sIN, 16)])
            mark(f"x1in_{t}_{c}", sIN)
        B.rec("vector", None, waits=[w(f"x1in_{t}_{NCH - 1}")])
        rec_ln_apply_transpose(t, sa, "ln1b",
                               mcol=lambda c: m1a.ap()[:, NCH * t + c:NCH * t + c + 1],
                               rcol=lambda c: r1a.ap()[:, NCH * t + c:NCH * t + c + 1])
        for fc in range(DC):
            par = fc % 2
            pQ = pA if par == 0 else pB
            pG = pC if par == 0 else pD
            qwaits = [w(f"ln1b_fm_{t}"), w("p2w_0"), w("bcols_done")]
            if ev.get(f"pq_free_{par}"):
                qwaits.append(ev[f"pq_free_{par}"])
            for dc in range(DC):
                B.rec("tensor", lambda e, fc=fc, dc=dc, pQ=pQ: e.matmul(
                    pQ[:], wq_sb[:, dc, fc * 128:(fc + 1) * 128], ln_fm[:, dc, :],
                    start=(dc == 0), stop=(dc == DC - 1)),
                    waits=qwaits if dc == 0 else [],
                    incs=[(sT, 1)] if dc == DC - 1 else [])
            mark(f"qmm_{t}_{fc}", sT)
            gwaits = [w("p2w_1")]
            if ev.get(f"pg_free_{par}"):
                gwaits.append(ev[f"pg_free_{par}"])
            for dc in range(DC):
                B.rec("tensor", lambda e, fc=fc, dc=dc, pG=pG: e.matmul(
                    pG[:], wg_sb[:, dc, fc * 128:(fc + 1) * 128], ln_fm[:, dc, :],
                    start=(dc == 0), stop=(dc == DC - 1)),
                    waits=gwaits if dc == 0 else [],
                    incs=[(sT, 1)] if dc == DC - 1 else [])
            mark(f"gmm_{t}_{fc}", sT)
            sh = par * 512
            B.rec("vector", lambda e, fc=fc, pQ=pQ, sh=sh: e.tensor_scalar(
                q3[:, sh:sh + 512], pQ[:], bcols.ap()[:, fc:fc + 1], 0.0,
                ALU.add, ALU.max),
                waits=[w(f"qmm_{t}_{fc}")])
            B.rec("vector", lambda e, fc=fc, pQ=pQ, sh=sh: e.tensor_scalar(
                q2[:, sh:sh + 512], pQ[:], bcols.ap()[:, fc:fc + 1], 0.0,
                ALU.add, ALU.min),
                incs=[(sV, 1)])
            mark(f"qmin_{t}_{fc}", sV)
            B.rec("scalar", lambda e, sh=sh: e.activation(
                q2[:, sh:sh + 512], q2[:, sh:sh + 512], AF.Exp),
                waits=[w(f"qmin_{t}_{fc}")], incs=[(sS, 1)])
            mark(f"qexp_{t}_{fc}", sS)
            B.rec("vector", lambda e, fc=fc, sh=sh: e.tensor_add(
                qt[:, fc, :], q3[:, sh:sh + 512], q2[:, sh:sh + 512]),
                waits=[w(f"qexp_{t}_{fc}")], incs=[(sV, 1)])
            mark(f"qt_{t}_{fc}", sV)
            twaits = [w(f"gmm_{t}_{fc}")]
            if ev.get(f"th_free_{par}"):
                twaits.append(ev[f"th_free_{par}"])
            B.rec("scalar", lambda e, fc=fc, pG=pG, par=par: e.activation(
                th[:, par, :], pG[:], AF.Tanh, scale=0.5,
                bias=bcols.ap()[:, 16 + fc:17 + fc]),
                waits=twaits, incs=[(sS, 1)])
            mark(f"th_{t}_{fc}", sS)
            ev[f"pg_free_{par}"] = (sS, sS.n)
            B.rec("tensor", lambda e, fc=fc, pQ=pQ: e.matmul(
                pQ[:], kvbd[:, fc, :], qt[:, fc, :], start=True, stop=True),
                waits=[w(f"qt_{t}_{fc}"), w("kvbd")], incs=[(sT, 1)])
            mark(f"attn_{t}_{fc}", sT)
            B.rec("vector", lambda e, fc=fc, pQ=pQ, par=par: e.scalar_tensor_tensor(
                qt[:, fc, :], th[:, par, :], 1.0, pQ[:], ALU.add, ALU.mult),
                waits=[w(f"attn_{t}_{fc}"), w(f"th_{t}_{fc}")], incs=[(sV, 1)])
            mark(f"gt_{t}_{fc}", sV)
            ev[f"pq_free_{par}"] = (sV, sV.n)
            ev[f"th_free_{par}"] = (sV, sV.n)
        ev["pa_free"] = (sV, sV.n)
        for c in range(NCH):
            waits = [w(f"gt_{t}_{DC - 1}"), w("p2w_2")]
            if ev.get("pv_free"):
                waits.append(ev["pv_free"])
            for fc in range(DC):
                for gg, pX in ((0, pC), (1, pD)):
                    B.rec("tensor", lambda e, c=c, fc=fc, gg=gg, pX=pX: e.matmul(
                        pX[:], qt[:, fc, c * 128:(c + 1) * 128],
                        wo_sb[:, fc, gg * 512:(gg + 1) * 512],
                        start=(fc == 0), stop=False),
                        waits=waits if (fc == 0 and gg == 0) else [])
            for gg, pX in ((0, pC), (1, pD)):
                B.rec("tensor", lambda e, gg=gg, pX=pX: e.matmul(
                    pX[:], ones[0:1, :], brow_bo[0:1, gg * 512:(gg + 1) * 512],
                    start=False, stop=True),
                    incs=[(sT, 1)] if gg == 1 else [])
            mark(f"womm_{t}_{c}", sT)
            for gg, pX in ((0, pC), (1, pD)):
                B.rec("vector", lambda e, c=c, gg=gg, pX=pX: e.scalar_tensor_tensor(
                    sb[:, c, gg * 512:(gg + 1) * 512], pX[:], 0.5,
                    sa[:, c, gg * 512:(gg + 1) * 512], ALU.mult, ALU.add),
                    waits=[w(f"womm_{t}_{c}")],
                    incs=[(sV, 1)] if gg == 1 else [])
            mark(f"x2_{t}_{c}", sV)
            ev["pv_free"] = (sV, sV.n)
            rec_bn(sb, c, (bnb0, st40))
        rec_ln_finish(t, "ln2", (bnb0, st40))
        rec_ln_apply_transpose(t, sb, "ln2", g=st40)
        rec_ffn(t, "ffn2", "ln2", sb, sa, 2)
        for c in range(NCH):
            B.rec("gpsimd", lambda e, t=t, c=c: e.dma_start(
                out=y_d[(t * NCH + c) * 128:(t * NCH + c + 1) * 128, :], in_=sa[:, c, :]),
                waits=[w(f"ffn2_out_{t}_{c}")], incs=[(sOUT, 16)])
            mark(f"yst_{t}_{c}", sOUT)

    with contextlib.ExitStack() as p1:
        wk_sb = p1.enter_context(nc.sbuf_tensor("wk_sb", [128, DC, D], BF16))
        wv_sb = p1.enter_context(nc.sbuf_tensor("wv_sb", [128, DC, D], BF16))
        brows_kv = p1.enter_context(nc.sbuf_tensor("brows_kv", [1, 2 * D], BF16))
        scr2k = p1.enter_context(nc.sbuf_tensor("scr2k", [128, 2, D], F32))
        ktil = p1.enter_context(nc.sbuf_tensor("ktil", [128, 2, D], BF16))
        vtil = p1.enter_context(nc.sbuf_tensor("vtil", [128, 2, D], BF16))
        kvbf = p1.enter_context(nc.sbuf_tensor("kvbf", [64, 1024], BF16))
        pkv = p1.enter_context(nc.psum_tensor("pkv", [64, 1024], F32))
        rec_startup()
        for t in range(nt):
            rec_pass1_tile(t)
        rec_kv_finalize()
        B.flush()
    with contextlib.ExitStack() as p2:
        pE = p2.enter_context(nc.psum_tensor("pE", [128, 512], F32))
        pF = p2.enter_context(nc.psum_tensor("pF", [128, 512], F32))
        wq_sb = p2.enter_context(nc.sbuf_tensor("wq_sb", [128, DC, D], BF16))
        wg_sb = p2.enter_context(nc.sbuf_tensor("wg_sb", [128, DC, D], BF16))
        wo_sb = p2.enter_context(nc.sbuf_tensor("wo_sb", [128, DC, D], BF16))
        rec_phase2_weights()
        for t in range(nt):
            rec_pass2_tile(t)
        B.rec("gpsimd", lambda e: e.sem_inc(sG.h, 1),
              waits=[(sOUT, sOUT.n), (sIN, sIN.n)])
        B.flush()
    em.close()
    return nc


_PROG_CACHE = {}


def _get_prog(nt):
    if nt not in _PROG_CACHE:
        _PROG_CACHE[nt] = build_program(nt)
    return _PROG_CACHE[nt]


def kernel(L0_attn_x, L0_attn_Wq, L0_attn_bq, L0_attn_Wk, L0_attn_bk,
           L0_attn_Wv, L0_attn_bv, L0_attn_Wg, L0_attn_bg, L0_attn_Wo,
           L0_attn_bo, L0_ffn_expert_sel, L0_ffn_keys, L0_ffn_values):
    x = np.ascontiguousarray(np.asarray(L0_attn_x, dtype=np.float32))
    Bz, S, Dm = x.shape
    assert Dm == D and S % TT == 0
    nt = S // TT
    nc = _get_prog(nt)
    common = {
        "wq": np.ascontiguousarray(L0_attn_Wq, dtype=np.float32),
        "wk": np.ascontiguousarray(L0_attn_Wk, dtype=np.float32),
        "wv": np.ascontiguousarray(L0_attn_Wv, dtype=np.float32),
        "wg": np.ascontiguousarray(L0_attn_Wg, dtype=np.float32),
        "wo": np.ascontiguousarray(L0_attn_Wo, dtype=np.float32),
        "bq": np.ascontiguousarray(L0_attn_bq, dtype=np.float32),
        "bk": np.ascontiguousarray(L0_attn_bk, dtype=np.float32),
        "bv": np.ascontiguousarray(L0_attn_bv, dtype=np.float32),
        "bg": np.ascontiguousarray(L0_attn_bg, dtype=np.float32),
        "bo": np.ascontiguousarray(L0_attn_bo, dtype=np.float32),
        "es": np.ascontiguousarray(L0_ffn_expert_sel, dtype=np.float32),
        "keys": np.ascontiguousarray(L0_ffn_keys, dtype=np.float32),
        "vals": np.ascontiguousarray(L0_ffn_values, dtype=np.float32),
    }
    in_maps = [dict(common, x=np.ascontiguousarray(x[b])) for b in range(Bz)]
    res = run_bass_kernel_spmd(nc, in_maps, core_ids=list(range(Bz)))
    out = np.stack([np.asarray(r["y"], dtype=np.float32) for r in res.results], axis=0)
    return out



# version 18
# speedup vs baseline: 1.0416x; 1.0178x over previous
import contextlib

import numpy as np
import ml_dtypes

import concourse.bass as bass
import concourse.mybir as mybir
from concourse.bass_utils import run_bass_kernel_spmd

F32 = mybir.dt.float32
BF16 = mybir.dt.bfloat16
I32 = mybir.dt.int32
AF = mybir.ActivationFunctionType
ALU = mybir.AluOpType

D = 1024
DC = 8
E16 = 16
TT = 512
NCH = 4
ENGINES = ("sync", "tensor", "vector", "scalar", "gpsimd")


class Sem:
    __slots__ = ("h", "n", "name")

    def __init__(self, h, name):
        self.h = h
        self.n = 0
        self.name = name


class Builder:
    def __init__(self, nc):
        self.nc = nc
        self.ops = {e: [] for e in ENGINES}
        self.waited = {}

    def rec(self, eng, fn, waits=(), incs=()):
        w = [(s.h, int(v)) for (s, v) in waits]
        i = [(s.h, int(a)) for (s, a) in incs]
        assert fn is not None or not i
        for s, a in incs:
            s.n += a
        self.ops[eng].append((fn, w, i))

    def flush(self):
        blk_ops = self.ops
        self.ops = {e: [] for e in ENGINES}
        waited = self.waited
        with self.nc.Block() as block:
            for eng_name in ENGINES:
                entries = blk_ops[eng_name]
                if not entries:
                    continue

                def body(e, entries=entries, eng_name=eng_name):
                    for fn, waits, incs in entries:
                        for (h, v) in waits:
                            key = (eng_name, h.num)
                            if waited.get(key, -1) < v:
                                e.wait_ge(h, v)
                                waited[key] = v
                        if fn is None:
                            continue
                        ins = fn(e)
                        for (h, a) in incs:
                            if a == 16:
                                ins.then_inc(h, a)
                            else:
                                e.maybe_drain_then_inc((h, a), fusable=True)

                getattr(block, eng_name)(body)


def build_program(nt):
    S = nt * TT
    nc = bass.Bass("TRN2", target_bir_lowering=False, detect_race_conditions=False)

    x_d = nc.dram_tensor("x", [S, D], F32, kind="ExternalInput").ap()
    y_d = nc.dram_tensor("y", [S, D], F32, kind="ExternalOutput").ap()
    wq_d = nc.dram_tensor("wq", [D, D], F32, kind="ExternalInput").ap()
    wk_d = nc.dram_tensor("wk", [D, D], F32, kind="ExternalInput").ap()
    wv_d = nc.dram_tensor("wv", [D, D], F32, kind="ExternalInput").ap()
    wg_d = nc.dram_tensor("wg", [D, D], F32, kind="ExternalInput").ap()
    wo_d = nc.dram_tensor("wo", [D, D], F32, kind="ExternalInput").ap()
    bq_d = nc.dram_tensor("bq", [D], F32, kind="ExternalInput").ap()
    bk_d = nc.dram_tensor("bk", [D], F32, kind="ExternalInput").ap()
    bv_d = nc.dram_tensor("bv", [D], F32, kind="ExternalInput").ap()
    bg_d = nc.dram_tensor("bg", [D], F32, kind="ExternalInput").ap()
    bo_d = nc.dram_tensor("bo", [D], F32, kind="ExternalInput").ap()
    es_d = nc.dram_tensor("es", [E16, D], F32, kind="ExternalInput").ap()
    keys_d = nc.dram_tensor("keys", [E16, D, 128], F32, kind="ExternalInput").ap()
    vals_d = nc.dram_tensor("vals", [E16, 128, D], F32, kind="ExternalInput").ap()
    x1_d = nc.dram_tensor("x1s", [S, D], F32, kind="Internal").ap()
    ident_d = nc.inline_tensor(np.eye(128, dtype=ml_dtypes.bfloat16), name="ident_dr")

    B = Builder(nc)
    sIN = Sem(nc.alloc_semaphore("sIN"), "sIN")
    sOUT = Sem(nc.alloc_semaphore("sOUT"), "sOUT")
    sT = Sem(nc.alloc_semaphore("sT"), "sT")
    sV = Sem(nc.alloc_semaphore("sV"), "sV")
    sS = Sem(nc.alloc_semaphore("sS"), "sS")
    sG = Sem(nc.alloc_semaphore("sG"), "sG")
    sCFG = Sem(nc.alloc_semaphore("sCFG"), "sCFG")
    sWA = Sem(nc.alloc_semaphore("sWA"), "sWA")
    sWB = Sem(nc.alloc_semaphore("sWB"), "sWB")
    sWC = Sem(nc.alloc_semaphore("sWC"), "sWC")
    sWD = Sem(nc.alloc_semaphore("sWD"), "sWD")
    sWE = Sem(nc.alloc_semaphore("sWE"), "sWE")
    sWF = Sem(nc.alloc_semaphore("sWF"), "sWF")

    em = contextlib.ExitStack()
    E = em.enter_context
    ident = E(nc.sbuf_tensor("ident", [128, 128], BF16))
    ones = E(nc.sbuf_tensor("ones", [65, 128], BF16))
    brow_bo = E(nc.sbuf_tensor("brow_bo", [1, D], BF16))
    bcols = E(nc.sbuf_tensor("bcols", [128, 32], F32))
    keys_sb = E(nc.sbuf_tensor("keys_sb", [128, DC, 2048], BF16))
    vals_sb = E(nc.sbuf_tensor("vals_sb", [128, E16, D], BF16))
    es_sb = E(nc.sbuf_tensor("es_sb", [128, DC, E16], BF16))
    sa = E(nc.sbuf_tensor("sa", [128, NCH, D], F32))
    sb = E(nc.sbuf_tensor("sb", [128, NCH, D], F32))
    ln_tm = E(nc.sbuf_tensor("ln_tm", [128, 2, D], BF16))
    ln_fm = E(nc.sbuf_tensor("ln_fm", [128, DC, TT], BF16))
    sel_bf = E(nc.sbuf_tensor("sel_bf", [E16, TT], BF16))
    sel_row = E(nc.sbuf_tensor("sel_row", [65, 6 * TT], BF16))
    selb = E(nc.sbuf_tensor("selb", [128, 3, TT], BF16))
    stile = E(nc.sbuf_tensor("stile", [128, E16, TT], BF16))
    scr1 = E(nc.sbuf_tensor("scr1", [16, 512], F32))
    bnb0 = E(nc.sbuf_tensor("bnb0", [128, 8, 6], F32))
    st40 = E(nc.sbuf_tensor("st40", [128, 28], F32))
    bnb1 = E(nc.sbuf_tensor("bnb1", [128, 8, 6], F32))
    st41 = E(nc.sbuf_tensor("st41", [128, 28], F32))
    m1a = E(nc.sbuf_tensor("m1a", [128, NCH * nt], F32))
    r1a = E(nc.sbuf_tensor("r1a", [128, NCH * nt], F32))
    qt = E(nc.sbuf_tensor("qt", [128, DC, TT], BF16))
    th = E(nc.sbuf_tensor("th", [128, 2, TT], BF16))
    kvbd = E(nc.sbuf_tensor("kvbd", [128, DC, 128], BF16))
    pT1 = E(nc.psum_tensor("pT1", [128, 1024], BF16))
    pT2 = E(nc.psum_tensor("pT2", [128, 1024], BF16))
    pA = E(nc.psum_tensor("pA", [128, 512], F32))
    pB = E(nc.psum_tensor("pB", [128, 512], F32))
    pC = E(nc.psum_tensor("pC", [128, 512], F32))
    pD = E(nc.psum_tensor("pD", [128, 512], F32))

    ev = {}

    def mark(name, sem):
        ev[name] = (sem, sem.n)

    def w(name):
        return ev[name]

    def rec_startup():
        B.rec("scalar", lambda e: e.preload_act_table(AF.Exp))
        for c in range(NCH):
            B.rec("sync", lambda e, c=c: e.dma_start(out=sa[:, c, :], in_=x_d[c * 128:(c + 1) * 128, :]),
                  incs=[(sIN, 16)])
            mark(f"xin_0_{c}", sIN)
        B.rec("sync", lambda e: e.dma_start(out=ident[:], in_=ident_d.ap()), incs=[(sCFG, 16)])
        mark("ident_in", sCFG)
        B.rec("sync", lambda e: e.dma_start(out=bcols[:, 0:8], in_=bq_d.rearrange("(c p) -> p c", p=128), allow_slow_non_contiguous=True),
              incs=[(sCFG, 16)])
        B.rec("sync", lambda e: e.dma_start(out=bcols[:, 16:24], in_=bg_d.rearrange("(c p) -> p c", p=128), allow_slow_non_contiguous=True),
              incs=[(sCFG, 16)])
        B.rec("sync", lambda e: e.dma_start(out=stile.ap()[0:1, 0:4, :].rearrange('p a b -> p (a b)').bitcast(F32)[0:1, 0:D], in_=bo_d.unsqueeze(0)), incs=[(sCFG, 16)])
        mark("bo_in", sCFG)
        B.rec("vector", lambda e: e.memset(ones[:, :], 1.0))
        B.rec("vector", lambda e: e.tensor_scalar_mul(bcols[:, 8:16], bcols[:, 0:8], -1.0),
              waits=[w("bo_in")])
        B.rec("vector", lambda e: e.tensor_scalar_mul(bcols[:, 16:24], bcols[:, 16:24], 0.5))
        B.rec("vector", lambda e: e.tensor_scalar_mul(stile.ap()[0:1, 0:4, :].rearrange('p a b -> p (a b)').bitcast(F32)[0:1, 0:D], stile.ap()[0:1, 0:4, :].rearrange('p a b -> p (a b)').bitcast(F32)[0:1, 0:D], 2.0))
        B.rec("vector", lambda e: e.tensor_copy(brow_bo[0:1, 0:D], stile.ap()[0:1, 0:4, :].rearrange('p a b -> p (a b)').bitcast(F32)[0:1, 0:D]), incs=[(sV, 1)])
        mark("cfg_done", sV)
        mark("ones_done", sV)
        mark("bcols_done", sV)
        for dc in range(DC):
            B.rec("gpsimd", lambda e, dc=dc: e.dma_start(
                out=keys_sb[:, dc, :],
                in_=keys_d[:, dc * 128:(dc + 1) * 128, :].transpose([1, 0, 2])),
                incs=[(sWA, 16)])
        mark("keys_done", sWA)
        for dc in range(DC):
            B.rec("gpsimd", lambda e, dc=dc: e.dma_start(
                out=es_sb[:, dc, :],
                in_=es_d[:, dc * 128:(dc + 1) * 128].transpose([1, 0]),
                allow_slow_non_contiguous=True),
                incs=[(sWA, 16)])
        mark("es_done", sWA)
        for e_ in range(E16):
            B.rec("gpsimd", lambda e, e_=e_: e.dma_start(
                out=vals_sb[:, e_, :], in_=vals_d[e_]), incs=[(sWB, 16)])
        mark("vals_done", sWB)
        for name, wd, wsb in (("wk", wk_d, wk_sb), ("wv", wv_d, wv_sb)):
            for dc in range(DC):
                B.rec("gpsimd", lambda e, wd=wd, dc=dc, wsb=wsb: e.dma_start(
                    out=wsb[:, dc, :], in_=wd[dc * 128:(dc + 1) * 128, :]),
                    incs=[(sWC, 16)])
        B.rec("gpsimd", lambda e: e.dma_start(out=brows_kv[0:1, 0:D], in_=bk_d.unsqueeze(0)),
              incs=[(sWC, 16)])
        B.rec("gpsimd", lambda e: e.dma_start(out=brows_kv[0:1, D:2 * D], in_=bv_d.unsqueeze(0)),
              incs=[(sWC, 16)])
        mark("wkv_done", sWC)

    def rec_bn(src, c, bufs, waits=()):
        bnb, g = bufs
        B.rec("vector", lambda e, c=c: e.bn_stats(
            bnb[:, 2 * c, :], src[:, c, 0:512]), waits=list(waits))
        B.rec("vector", lambda e, c=c: e.bn_stats(
            bnb[:, 2 * c + 1, :], src[:, c, 512:1024]))

    def rec_ln_finish(t, which, bufs):
        bnb, g = bufs
        V = "vector"

        def RD(fn):
            B.rec(V, fn)
            B.rec(V, lambda e: e.drain())

        B.rec(V, lambda e: e.drain())
        for c in range(NCH):
            B.rec(V, lambda e, c=c: e.bn_aggr(g[:, 2 * c:2 * c + 2], bnb[:, 2 * c:2 * c + 2, :]))
        B.rec(V, lambda e: e.drain())
        mvv = g.ap()[:, 1:8:2]
        us = g.ap()[:, 12:16]
        Bs = g.ap()[:, 16:20]
        Cs = g.ap()[:, 20:24]
        ys = g.ap()[:, 24:28]
        rs = g.ap()[:, 8:12]
        RD(lambda e: e.tensor_scalar_add(us, mvv, 1e-5))
        RD(lambda e: e.tensor_scalar(Bs.bitcast(I32), us.bitcast(I32), 1, None, ALU.arith_shift_right))
        RD(lambda e: e.tensor_scalar(Cs.bitcast(I32), Bs.bitcast(I32), -1, 0x5F3759DF, ALU.mult, ALU.add))
        for it in range(3):
            yin = Cs if it % 2 == 0 else ys
            yout = ys if it % 2 == 0 else Cs
            last = it == 2
            RD(lambda e, yin=yin: e.tensor_tensor(Bs, yin, yin, ALU.mult))
            RD(lambda e: e.tensor_tensor(rs, Bs, us, ALU.mult))
            RD(lambda e: e.tensor_scalar(Bs, rs, -0.5, 1.5, ALU.mult, ALU.add))
            RD(lambda e, yin=yin, yout=(rs if last else yout): e.tensor_tensor(yout, yin, Bs, ALU.mult))
        if which == "ln1":
            B.rec(V, lambda e, t=t: e.tensor_copy(m1a[:, NCH * t:NCH * (t + 1)], g.ap()[:, 0:8:2]))
            B.rec(V, lambda e, t=t: e.tensor_copy(r1a[:, NCH * t:NCH * (t + 1)], rs))
            B.rec(V, lambda e: e.drain())

    def rec_ln_finish_pair(t, which, bufs, p):
        bnb, g = bufs
        V = "vector"

        def RD(fn):
            B.rec(V, fn)
            B.rec(V, lambda e: e.drain())

        B.rec(V, lambda e: e.drain())
        for c in (2 * p, 2 * p + 1):
            B.rec(V, lambda e, c=c: e.bn_aggr(g[:, 2 * c:2 * c + 2], bnb[:, 2 * c:2 * c + 2, :]))
        B.rec(V, lambda e: e.drain())
        o = 2 * p
        mvv = g.ap()[:, 1 + 4 * p:8:2][:, 0:2]
        us = g.ap()[:, 12 + o:14 + o]
        Bs = g.ap()[:, 16 + o:18 + o]
        Cs = g.ap()[:, 20 + o:22 + o]
        ys = g.ap()[:, 24 + o:26 + o]
        rs = g.ap()[:, 8 + o:10 + o]
        RD(lambda e: e.tensor_scalar_add(us, mvv, 1e-5))
        RD(lambda e: e.tensor_scalar(Bs.bitcast(I32), us.bitcast(I32), 1, None, ALU.arith_shift_right))
        RD(lambda e: e.tensor_scalar(Cs.bitcast(I32), Bs.bitcast(I32), -1, 0x5F3759DF, ALU.mult, ALU.add))
        for it in range(3):
            yin = Cs if it % 2 == 0 else ys
            yout = ys if it % 2 == 0 else Cs
            last = it == 2
            RD(lambda e, yin=yin: e.tensor_tensor(Bs, yin, yin, ALU.mult))
            RD(lambda e: e.tensor_tensor(rs, Bs, us, ALU.mult))
            RD(lambda e: e.tensor_scalar(Bs, rs, -0.5, 1.5, ALU.mult, ALU.add))
            RD(lambda e, yin=yin, yout=(rs if last else yout): e.tensor_tensor(yout, yin, Bs, ALU.mult))
        if which == "ln1":
            B.rec(V, lambda e, t=t, p=p: e.tensor_copy(
                m1a[:, NCH * t + 2 * p:NCH * t + 2 * p + 2], g.ap()[:, 4 * p:4 * p + 4:2]))
            B.rec(V, lambda e, t=t, p=p: e.tensor_copy(
                r1a[:, NCH * t + 2 * p:NCH * t + 2 * p + 2], rs))
            B.rec(V, lambda e: e.drain())

    def rec_ln_apply_T_pair(t, src, which, g, p):
        for c in (2 * p, 2 * p + 1):
            mc, rc = g.ap()[:, 2 * c:2 * c + 1], g.ap()[:, 8 + c:9 + c]
            awaits = []
            if ev.get(f"ln_tm_free_{c % 2}"):
                awaits.append(ev[f"ln_tm_free_{c % 2}"])
            B.rec("vector", lambda e, c=c, mc=mc, rc=rc: e.tensor_scalar(
                ln_tm[:, c % 2, :], src[:, c, :], mc, rc, ALU.subtract, ALU.mult),
                waits=awaits, incs=[(sV, 1)])
            mark(f"{which}_tm_{t}_{c}", sV)
        for c in (2 * p, 2 * p + 1):
            pT = pT1 if c % 2 == 0 else pT2
            waits = [w(f"{which}_tm_{t}_{c}"), w("ident_in")]
            if ev.get(f"Tevac_prev_{c % 2}"):
                waits.append(ev[f"Tevac_prev_{c % 2}"])
            if ev.get("pT_busy"):
                waits.append(ev["pT_busy"])
            for j in range(DC):
                B.rec("tensor", lambda e, c=c, j=j, pT=pT: e.transpose(
                    pT[:, j * 128:(j + 1) * 128], ln_tm[:, c % 2, j * 128:(j + 1) * 128], ident[:]),
                    waits=waits if j == 0 else [], incs=[(sT, 1)] if j == DC - 1 else [])
            mark(f"{which}_T_{t}_{c}", sT)
            ev[f"ln_tm_free_{c % 2}"] = (sT, sT.n)
            B.rec("scalar", lambda e, c=c, pT=pT: e.copy(
                ln_fm.ap()[:, :, c * 128:(c + 1) * 128],
                pT.ap().rearrange("p (j k) -> p j k", j=DC)),
                waits=[w(f"{which}_T_{t}_{c}")], incs=[(sS, 1)])
            mark(f"{which}_fm_{t}_{c}", sS)
            ev[f"Tevac_prev_{c % 2}"] = (sS, sS.n)
        if p == 1:
            mark(f"{which}_fm_{t}", sS)

    def rec_ln_apply_transpose(t, src, which, mcol=None, rcol=None, g=None):
        for c in range(NCH):
            if mcol is None:
                mc, rc = g.ap()[:, 2 * c:2 * c + 1], g.ap()[:, 8 + c:9 + c]
            else:
                mc, rc = mcol(c), rcol(c)
            awaits = []
            if ev.get(f"ln_tm_free_{c % 2}"):
                awaits.append(ev[f"ln_tm_free_{c % 2}"])
            B.rec("vector", lambda e, c=c, mc=mc, rc=rc: e.tensor_scalar(
                ln_tm[:, c % 2, :], src[:, c, :], mc, rc, ALU.subtract, ALU.mult),
                waits=awaits, incs=[(sV, 1)])
            mark(f"{which}_tm_{t}_{c}", sV)
            pT = pT1 if c % 2 == 0 else pT2
            waits = [w(f"{which}_tm_{t}_{c}"), w("ident_in")]
            if ev.get(f"Tevac_prev_{c % 2}"):
                waits.append(ev[f"Tevac_prev_{c % 2}"])
            if ev.get("pT_busy"):
                waits.append(ev["pT_busy"])
            for j in range(DC):
                B.rec("tensor", lambda e, c=c, j=j, pT=pT: e.transpose(
                    pT[:, j * 128:(j + 1) * 128], ln_tm[:, c % 2, j * 128:(j + 1) * 128], ident[:]),
                    waits=waits if j == 0 else [], incs=[(sT, 1)] if j == DC - 1 else [])
            mark(f"{which}_T_{t}_{c}", sT)
            ev[f"ln_tm_free_{c % 2}"] = (sT, sT.n)
            B.rec("scalar", lambda e, c=c, pT=pT: e.copy(
                ln_fm.ap()[:, :, c * 128:(c + 1) * 128],
                pT.ap().rearrange("p (j k) -> p j k", j=DC)),
                waits=[w(f"{which}_T_{t}_{c}")], incs=[(sS, 1)])
            mark(f"{which}_fm_{t}_{c}", sS)
            ev[f"Tevac_prev_{c % 2}"] = (sS, sS.n)
        mark(f"{which}_fm_{t}", sS)

    def rec_ffn(t, which, lnname, resid_src, out_dst, pass_no, post_evac=None):
        waits = [w(f"{lnname}_fm_{t}"), w("es_done"), w("ident_in")]
        if ev.get("pa_free"):
            waits.append(ev["pa_free"])
        for dc in range(DC):
            B.rec("tensor", lambda e, dc=dc: e.matmul(
                pA[0:16, :], es_sb[:, dc, :], ln_fm[:, dc, :],
                start=(dc == 0), stop=(dc == DC - 1)),
                waits=waits if dc == 0 else [], incs=[(sT, 1)] if dc == DC - 1 else [])
        mark(f"sel_mm_{which}_{t}", sT)
        B.rec("scalar", lambda e: e.activation(scr1[0:16, 0:TT], pA[0:16, :], AF.Tanh, scale=0.5),
              waits=[w(f"sel_mm_{which}_{t}")], incs=[(sS, 1)])
        mark(f"sel_th_{which}_{t}", sS)
        B.rec("vector", lambda e: e.tensor_scalar(
            sel_bf[:, :], scr1[0:16, 0:TT], 0.5, 0.5, ALU.mult, ALU.add),
            waits=[w(f"sel_th_{which}_{t}")], incs=[(sV, 1)])
        mark(f"sel_{which}_{t}", sV)
        srw = [w(f"sel_{which}_{t}")]
        if ev.get("sel_row_free"):
            srw.append(ev["sel_row_free"])
        B.rec("gpsimd", lambda e: e.dma_start(out=sel_row[0:1, 0:6 * TT], in_=sel_bf[0:6, :]),
              waits=srw, incs=[(sOUT, 16)])
        B.rec("gpsimd", lambda e: e.dma_start(out=sel_row[32:33, 0:6 * TT], in_=sel_bf[6:12, :]),
              incs=[(sOUT, 16)])
        B.rec("gpsimd", lambda e: e.dma_start(out=sel_row[64:65, 0:4 * TT], in_=sel_bf[12:16, :]),
              incs=[(sOUT, 16)])
        mark(f"selrow_{which}_{t}", sOUT)
        nb = 2 if pass_no == 1 else 4
        banks = (pA, pB) if pass_no == 1 else (pA, pB, pE, pF)
        for e_ in range(E16):
            par = e_ % 2
            spar = e_ % 3
            pSB = (pT1 if par == 0 else pT2).ap().bitcast(F32)[:, 0:512]
            sbs = selb.ap()[:, spar, :]
            waits = [w(f"selrow_{which}_{t}"), w("ones_done")]
            if ev.get(f"selb_evac_{par}"):
                waits.append(ev[f"selb_evac_{par}"])
            sp = 32 * (e_ // 6) if e_ < 12 else 64
            so = (e_ % 6) * TT if e_ < 12 else (e_ - 12) * TT
            B.rec("tensor", lambda e, pSB=pSB, sp=sp, so=so: e.matmul(
                pSB, ones[sp:sp + 1, :], sel_row[sp:sp + 1, so:so + TT],
                start=True, stop=True),
                waits=waits, incs=[(sT, 1)])
            mark(f"selb_mm_{which}_{t}_{e_}", sT)
            ewaits = [w(f"selb_mm_{which}_{t}_{e_}")]
            if e_ >= 3:
                ewaits.append(w(f"stile_{which}_{t}_{e_ - 3}"))
            B.rec("scalar", lambda e, pSB=pSB, sbs=sbs: e.copy(sbs, pSB),
                  waits=ewaits, incs=[(sS, 1)])
            mark(f"selb_{which}_{t}_{e_}", sS)
            ev[f"selb_evac_{par}"] = (sS, sS.n)
            pSC = banks[e_ % nb]
            swaits = []
            if e_ < 2:
                swaits.append(w(f"sel_th_{which}_{t}"))
            if e_ >= nb:
                swaits.append(w(f"stile_{which}_{t}_{e_ - nb}"))
            if pass_no == 1 and t == 0 and e_ == 0:
                swaits.append(w("keys_done"))

            for dc in range(DC):
                B.rec("tensor", lambda e, e_=e_, dc=dc, pSC=pSC: e.matmul(
                    pSC[:], keys_sb[:, dc, e_ * 128:(e_ + 1) * 128], ln_fm[:, dc, :],
                    start=(dc == 0), stop=(dc == DC - 1)),
                    waits=swaits if dc == 0 else [], incs=[(sT, 1)] if dc == DC - 1 else [])
            mark(f"sc_mm_{which}_{t}_{e_}", sT)
            B.rec("vector", lambda e, e_=e_, pSC=pSC, sbs=sbs: e.scalar_tensor_tensor(
                stile[:, e_, :], pSC[:], 0.0, sbs, ALU.max, ALU.mult),
                waits=[w(f"sc_mm_{which}_{t}_{e_}"), w(f"selb_{which}_{t}_{e_}")],
                incs=[(sV, 1)])
            mark(f"stile_{which}_{t}_{e_}", sV)
        ev["pT_busy"] = (sS, ev[f"selb_{which}_{t}_{E16 - 1}"][1])
        ev["sel_row_free"] = (sT, ev[f"selb_mm_{which}_{t}_{E16 - 1}"][1])
        ev["pa_free"] = (sV, ev[f"stile_{which}_{t}_{E16 - 1}"][1])
        for c in range(NCH):
            if pass_no == 1 or c % 2 == 0:
                vb = (pC, pD)
            else:
                vb = (pE, pF)
            waits = [w(f"stile_{which}_{t}_{E16 - 1}")]
            if pass_no == 1 and t == 0 and c == 0:
                waits.append(w("vals_done"))
            if pass_no == 1 or c < 2:
                if ev.get("pv_free"):
                    waits.append(ev["pv_free"])
            else:
                waits.append(ev[f"pvv_free_{c % 2}"])
            for e_ in range(E16):
                for gg in range(2):
                    pV = vb[gg]
                    B.rec("tensor", lambda e, c=c, e_=e_, gg=gg, pV=pV: e.matmul(
                        pV[:], stile[:, e_, c * 128:(c + 1) * 128],
                        vals_sb[:, e_, gg * 512:(gg + 1) * 512],
                        start=(e_ == 0), stop=(e_ == E16 - 1)),
                        waits=waits if (e_ == 0 and gg == 0) else [],
                        incs=[(sT, 1)] if (e_ == E16 - 1 and gg == 1) else [])
            mark(f"ffnv_{which}_{t}_{c}", sT)
            ewaits = [w(f"ffnv_{which}_{t}_{c}")]
            for gg in range(2):
                pV = vb[gg]
                B.rec("vector", lambda e, c=c, gg=gg, pV=pV: e.scalar_tensor_tensor(
                    out_dst[:, c, gg * 512:(gg + 1) * 512], pV[:], 0.5,
                    resid_src[:, c, gg * 512:(gg + 1) * 512], ALU.mult, ALU.add),
                    waits=ewaits,
                    incs=[(sV, 1)] if gg == 1 else [])
            mark(f"{which}_out_{t}_{c}", sV)
            ev["pv_free"] = (sV, sV.n)
            ev[f"pvv_free_{c % 2}"] = (sV, sV.n)
            if post_evac is not None:
                post_evac(c)

    def rec_pass1_tile(t):
        if t == 0:
            for c in range(NCH):
                rec_bn(sa, c, (bnb0, st40), waits=[w(f"xin_0_{c}")])
            rec_ln_finish(0, "ln0", (bnb0, st40))
        rec_ln_apply_transpose(t, sa, "ln0", g=st40)
        def p1_post_evac(c):
            rec_bn(sb, c, (bnb1, st41))
            if c == 1:
                rec_ln_finish_pair(t, "ln1", (bnb1, st41), 0)
            if c == 2:
                rec_ln_apply_T_pair(t, sb, "ln1", st41, 0)
            if c == 3:
                rec_ln_finish_pair(t, "ln1", (bnb1, st41), 1)

        rec_ffn(t, "ffn1", "ln0", sa, sb, 1, post_evac=p1_post_evac)
        for c in range(NCH):
            B.rec("gpsimd", lambda e, t=t, c=c: e.dma_start(
                out=x1_d[(t * NCH + c) * 128:(t * NCH + c + 1) * 128, :], in_=sb[:, c, :]),
                waits=[w(f"ffn1_out_{t}_{c}")], incs=[(sOUT, 16)])
            mark(f"x1st_{t}_{c}", sOUT)
        if t > 0:
            B.rec("vector", None, waits=[w(f"x1st_{t - 1}_{NCH - 1}")])
        for c in range(NCH):
            if c == 2:
                rec_ln_apply_T_pair(t, sb, "ln1", st41, 1)
            waits = [w(f"ln1_fm_{t}_{c}"), w("wkv_done"),
                     w("ones_done"), ev["pa_free"], ev["pv_free"]]
            lnc = ln_fm.ap()[:, :, c * 128:(c + 1) * 128]
            for dc in range(DC):
                for g_, pX, wsb in ((0, pA, wk_sb), (1, pB, wk_sb), (2, pC, wv_sb), (3, pD, wv_sb)):
                    gg = g_ % 2
                    B.rec("tensor", lambda e, dc=dc, pX=pX, wsb=wsb, gg=gg, lnc=lnc: e.matmul(
                        pX[:], lnc[:, dc, :], wsb[:, dc, gg * 512:(gg + 1) * 512],
                        start=(dc == 0), stop=False),
                        waits=waits if (dc == 0 and g_ == 0) else [])
            for g_, pX, ro in ((0, pA, 0), (1, pB, 512), (2, pC, D), (3, pD, D + 512)):
                B.rec("tensor", lambda e, pX=pX, ro=ro: e.matmul(
                    pX[:], ones[0:1, :], brows_kv[0:1, ro:ro + 512], start=False, stop=True),
                    incs=[(sT, 1)] if g_ == 3 else [])
            mark(f"kvproj_{t}_{c}", sT)
            par = c % 2
            for gg, pX in ((0, pC), (1, pD)):
                B.rec("vector", lambda e, gg=gg, pX=pX, par=par: e.tensor_copy(
                    vtil[:, par, gg * 512:(gg + 1) * 512], pX[:]),
                    waits=[w(f"kvproj_{t}_{c}")] if gg == 0 else [])
            for gg, pX in ((0, pA), (1, pB)):
                B.rec("vector", lambda e, gg=gg, pX=pX, par=par: e.tensor_scalar_max(
                    ktil[:, par, gg * 512:(gg + 1) * 512], pX[:], 0.0))
                kw = []
                if t * NCH + c >= 2:
                    pt, pc = divmod(t * NCH + c - 2, NCH)
                    kw = [w(f"kexp_{pt}_{pc}")]
                B.rec("vector", lambda e, gg=gg, pX=pX, par=par: e.tensor_scalar_min(
                    scr2k[:, par, gg * 512:(gg + 1) * 512], pX[:], 0.0),
                    waits=kw if gg == 0 else [],
                    incs=[(sV, 1)] if gg == 1 else [])
            mark(f"kv_free_{t}_{c}", sV)
            ev["pa_free"] = (sV, sV.n)
            ev["pv_free"] = (sV, sV.n)
            B.rec("scalar", lambda e, par=par: e.activation(
                scr2k[:, par, :], scr2k[:, par, :], AF.Exp),
                waits=[w(f"kv_free_{t}_{c}")], incs=[(sS, 1)])
            mark(f"kexp_{t}_{c}", sS)
            B.rec("vector", lambda e, par=par: e.tensor_tensor(
                ktil[:, par, :], ktil[:, par, :], scr2k[:, par, :], ALU.add),
                waits=[w(f"kexp_{t}_{c}")], incs=[(sV, 1)])
            mark(f"kv_in_{t}_{c}", sV)

            def rec_kvmm(tc):
                tt_, cc_ = divmod(tc, NCH)
                par2 = cc_ % 2
                first = tc == 0
                last = tc == nt * NCH - 1
                for h in range(E16):
                    B.rec("tensor", lambda e, h=h, first=first, last=last, par2=par2: e.matmul(
                        pkv[0:64, h * 64:(h + 1) * 64], ktil[:, par2, h * 64:(h + 1) * 64],
                        vtil[:, par2, h * 64:(h + 1) * 64],
                        start=(first and h % 8 == 0), stop=(last and h % 8 == 7),
                        skip_group_check=True),
                        waits=[w(f"kv_in_{tt_}_{cc_}")] if h == 0 else [],
                        incs=[(sT, 1)] if h == E16 - 1 else [])
                mark(f"kvacc_{tt_}_{cc_}", sT)

            tc = t * NCH + c
            if tc >= 1:
                rec_kvmm(tc - 1)
            if tc == nt * NCH - 1:
                rec_kvmm(tc)
        if t + 1 < nt:
            for c in range(NCH):
                B.rec("sync", lambda e, t=t, c=c: e.dma_start(
                    out=sa[:, c, :], in_=x_d[((t + 1) * NCH + c) * 128:((t + 1) * NCH + c + 1) * 128, :]),
                    waits=[w(f"ffn1_out_{t}_{c}")], incs=[(sIN, 16)])
                mark(f"xin_{t + 1}_{c}", sIN)
            for c in range(NCH):
                rec_bn(sa, c, (bnb0, st40), waits=[w(f"xin_{t + 1}_{c}")])
            rec_ln_finish(t + 1, "ln0", (bnb0, st40))

    def rec_kv_finalize():
        B.rec("vector", lambda e: e.tensor_copy(kvbf[:, :], pkv[:, :]),
              waits=[w(f"kvacc_{nt - 1}_{NCH - 1}")])
        B.rec("vector", lambda e: e.memset(kvbd[:, :, :], 0.0), incs=[(sV, 1)])
        mark("kvbf", sV)
        for cc in range(DC):
            B.rec("gpsimd", lambda e, cc=cc: e.dma_start(
                out=kvbd[0:64, cc, 0:64], in_=kvbf[0:64, cc * 128:cc * 128 + 64]),
                waits=[w("kvbf")], incs=[(sOUT, 16)])
            B.rec("gpsimd", lambda e, cc=cc: e.dma_start(
                out=kvbd[64:128, cc, 64:128], in_=kvbf[0:64, cc * 128 + 64:cc * 128 + 128]),
                incs=[(sOUT, 16)])
        mark("kvbd", sOUT)

    def rec_phase2_weights():
        kvp = w(f"kvproj_{nt - 1}_{NCH - 1}")
        kva = w(f"kvacc_{nt - 1}_{NCH - 1}")
        for i, (wd, wsb, sw) in enumerate(((wq_d, wq_sb, sWD), (wg_d, wg_sb, sWE), (wo_d, wo_sb, sWF))):
            for dc in range(DC):
                waits = []
                if dc == 0:
                    waits = [kvp] if i < 2 else [kva, w("kvbd")]
                B.rec("gpsimd", lambda e, wd=wd, dc=dc, wsb=wsb: e.dma_start(
                    out=wsb[:, dc, :], in_=wd[dc * 128:(dc + 1) * 128, :]),
                    waits=waits, incs=[(sw, 16)])
            mark(f"p2w_{i}", sw)

    def rec_pass2_tile(t):
        q2 = stile.ap()[:, 0:4, :].rearrange("p a b -> p (a b)").bitcast(F32)
        q3 = stile.ap()[:, 4:8, :].rearrange("p a b -> p (a b)").bitcast(F32)
        for c in range(NCH):
            if t == 0:
                waits = [w(f"x1st_{nt - 1}_{NCH - 1}")]
            else:
                waits = [w(f"yst_{t - 1}_{c}")]
            B.rec("sync", lambda e, t=t, c=c: e.dma_start(
                out=sa[:, c, :], in_=x1_d[(t * NCH + c) * 128:(t * NCH + c + 1) * 128, :]),
                waits=waits, incs=[(sIN, 16)])
            mark(f"x1in_{t}_{c}", sIN)
        B.rec("vector", None, waits=[w(f"x1in_{t}_{NCH - 1}")])
        rec_ln_apply_transpose(t, sa, "ln1b",
                               mcol=lambda c: m1a.ap()[:, NCH * t + c:NCH * t + c + 1],
                               rcol=lambda c: r1a.ap()[:, NCH * t + c:NCH * t + c + 1])
        for fc in range(DC):
            par = fc % 2
            pQ = pA if par == 0 else pB
            pG = pC if par == 0 else pD
            qwaits = [w(f"ln1b_fm_{t}"), w("p2w_0"), w("bcols_done")]
            if ev.get(f"pq_free_{par}"):
                qwaits.append(ev[f"pq_free_{par}"])
            for dc in range(DC):
                B.rec("tensor", lambda e, fc=fc, dc=dc, pQ=pQ: e.matmul(
                    pQ[:], wq_sb[:, dc, fc * 128:(fc + 1) * 128], ln_fm[:, dc, :],
                    start=(dc == 0), stop=(dc == DC - 1)),
                    waits=qwaits if dc == 0 else [],
                    incs=[(sT, 1)] if dc == DC - 1 else [])
            mark(f"qmm_{t}_{fc}", sT)
            gwaits = [w("p2w_1")]
            if ev.get(f"pg_free_{par}"):
                gwaits.append(ev[f"pg_free_{par}"])
            for dc in range(DC):
                B.rec("tensor", lambda e, fc=fc, dc=dc, pG=pG: e.matmul(
                    pG[:], wg_sb[:, dc, fc * 128:(fc + 1) * 128], ln_fm[:, dc, :],
                    start=(dc == 0), stop=(dc == DC - 1)),
                    waits=gwaits if dc == 0 else [],
                    incs=[(sT, 1)] if dc == DC - 1 else [])
            mark(f"gmm_{t}_{fc}", sT)
            sh = par * 512
            B.rec("vector", lambda e, fc=fc, pQ=pQ, sh=sh: e.tensor_scalar(
                q2[:, sh:sh + 512], pQ[:], bcols.ap()[:, fc:fc + 1], 0.0,
                ALU.add, ALU.min),
                waits=[w(f"qmm_{t}_{fc}")])
            B.rec("vector", lambda e, fc=fc, pQ=pQ: e.tensor_scalar(
                qt[:, fc, :], pQ[:], bcols.ap()[:, fc:fc + 1], 0.0,
                ALU.add, ALU.max),
                incs=[(sV, 1)])
            mark(f"qpq_{t}_{fc}", sV)
            ewaits = [w(f"qpq_{t}_{fc}")]
            if fc >= 2:
                ewaits.append(w(f"attn_{t}_{fc - 2}"))
            B.rec("scalar", lambda e, sh=sh, par=par: e.activation(
                stile[:, par, :], q2[:, sh:sh + 512], AF.Exp),
                waits=ewaits, incs=[(sS, 1)])
            mark(f"qexp_{t}_{fc}", sS)
            twaits = [w(f"gmm_{t}_{fc}")]
            if ev.get(f"th_free_{par}"):
                twaits.append(ev[f"th_free_{par}"])
            B.rec("scalar", lambda e, fc=fc, pG=pG, par=par: e.activation(
                th[:, par, :], pG[:], AF.Tanh, scale=0.5,
                bias=bcols.ap()[:, 16 + fc:17 + fc]),
                waits=twaits, incs=[(sS, 1)])
            mark(f"th_{t}_{fc}", sS)
            ev[f"pg_free_{par}"] = (sS, sS.n)
            B.rec("tensor", lambda e, fc=fc, pQ=pQ: e.matmul(
                pQ[:], kvbd[:, fc, :], qt[:, fc, :], start=True, stop=False),
                waits=[w(f"qpq_{t}_{fc}"), w("kvbd")])
            B.rec("tensor", lambda e, fc=fc, pQ=pQ, par=par: e.matmul(
                pQ[:], kvbd[:, fc, :], stile[:, par, :], start=False, stop=True),
                waits=[w(f"qexp_{t}_{fc}")], incs=[(sT, 1)])
            mark(f"attn_{t}_{fc}", sT)
            B.rec("vector", lambda e, fc=fc, pQ=pQ, par=par: e.scalar_tensor_tensor(
                qt[:, fc, :], th[:, par, :], 1.0, pQ[:], ALU.add, ALU.mult),
                waits=[w(f"attn_{t}_{fc}"), w(f"th_{t}_{fc}")], incs=[(sV, 1)])
            mark(f"gt_{t}_{fc}", sV)
            ev[f"pq_free_{par}"] = (sV, sV.n)
            ev[f"th_free_{par}"] = (sV, sV.n)
        ev["pa_free"] = (sV, sV.n)
        for c in range(NCH):
            waits = [w(f"gt_{t}_{DC - 1}"), w("p2w_2")]
            if ev.get("pv_free"):
                waits.append(ev["pv_free"])
            for fc in range(DC):
                for gg, pX in ((0, pC), (1, pD)):
                    B.rec("tensor", lambda e, c=c, fc=fc, gg=gg, pX=pX: e.matmul(
                        pX[:], qt[:, fc, c * 128:(c + 1) * 128],
                        wo_sb[:, fc, gg * 512:(gg + 1) * 512],
                        start=(fc == 0), stop=False),
                        waits=waits if (fc == 0 and gg == 0) else [])
            for gg, pX in ((0, pC), (1, pD)):
                B.rec("tensor", lambda e, gg=gg, pX=pX: e.matmul(
                    pX[:], ones[0:1, :], brow_bo[0:1, gg * 512:(gg + 1) * 512],
                    start=False, stop=True),
                    incs=[(sT, 1)] if gg == 1 else [])
            mark(f"womm_{t}_{c}", sT)
            for gg, pX in ((0, pC), (1, pD)):
                B.rec("vector", lambda e, c=c, gg=gg, pX=pX: e.scalar_tensor_tensor(
                    sb[:, c, gg * 512:(gg + 1) * 512], pX[:], 0.5,
                    sa[:, c, gg * 512:(gg + 1) * 512], ALU.mult, ALU.add),
                    waits=[w(f"womm_{t}_{c}")],
                    incs=[(sV, 1)] if gg == 1 else [])
            mark(f"x2_{t}_{c}", sV)
            ev["pv_free"] = (sV, sV.n)
            rec_bn(sb, c, (bnb0, st40))
        rec_ln_finish(t, "ln2", (bnb0, st40))
        rec_ln_apply_transpose(t, sb, "ln2", g=st40)
        rec_ffn(t, "ffn2", "ln2", sb, sa, 2)
        for c in range(NCH):
            B.rec("gpsimd", lambda e, t=t, c=c: e.dma_start(
                out=y_d[(t * NCH + c) * 128:(t * NCH + c + 1) * 128, :], in_=sa[:, c, :]),
                waits=[w(f"ffn2_out_{t}_{c}")], incs=[(sOUT, 16)])
            mark(f"yst_{t}_{c}", sOUT)

    with contextlib.ExitStack() as p1:
        wk_sb = p1.enter_context(nc.sbuf_tensor("wk_sb", [128, DC, D], BF16))
        wv_sb = p1.enter_context(nc.sbuf_tensor("wv_sb", [128, DC, D], BF16))
        brows_kv = p1.enter_context(nc.sbuf_tensor("brows_kv", [1, 2 * D], BF16))
        scr2k = p1.enter_context(nc.sbuf_tensor("scr2k", [128, 2, D], F32))
        ktil = p1.enter_context(nc.sbuf_tensor("ktil", [128, 2, D], BF16))
        vtil = p1.enter_context(nc.sbuf_tensor("vtil", [128, 2, D], BF16))
        kvbf = p1.enter_context(nc.sbuf_tensor("kvbf", [64, 1024], BF16))
        pkv = p1.enter_context(nc.psum_tensor("pkv", [64, 1024], F32))
        rec_startup()
        for t in range(nt):
            rec_pass1_tile(t)
        rec_kv_finalize()
        B.flush()
    with contextlib.ExitStack() as p2:
        pE = p2.enter_context(nc.psum_tensor("pE", [128, 512], F32))
        pF = p2.enter_context(nc.psum_tensor("pF", [128, 512], F32))
        wq_sb = p2.enter_context(nc.sbuf_tensor("wq_sb", [128, DC, D], BF16))
        wg_sb = p2.enter_context(nc.sbuf_tensor("wg_sb", [128, DC, D], BF16))
        wo_sb = p2.enter_context(nc.sbuf_tensor("wo_sb", [128, DC, D], BF16))
        rec_phase2_weights()
        for t in range(nt):
            rec_pass2_tile(t)
        B.rec("gpsimd", lambda e: e.sem_inc(sG.h, 1),
              waits=[(sOUT, sOUT.n), (sIN, sIN.n)])
        B.flush()
    em.close()
    return nc


_PROG_CACHE = {}


def _get_prog(nt):
    if nt not in _PROG_CACHE:
        _PROG_CACHE[nt] = build_program(nt)
    return _PROG_CACHE[nt]


def kernel(L0_attn_x, L0_attn_Wq, L0_attn_bq, L0_attn_Wk, L0_attn_bk,
           L0_attn_Wv, L0_attn_bv, L0_attn_Wg, L0_attn_bg, L0_attn_Wo,
           L0_attn_bo, L0_ffn_expert_sel, L0_ffn_keys, L0_ffn_values):
    x = np.ascontiguousarray(np.asarray(L0_attn_x, dtype=np.float32))
    Bz, S, Dm = x.shape
    assert Dm == D and S % TT == 0
    nt = S // TT
    nc = _get_prog(nt)
    common = {
        "wq": np.ascontiguousarray(L0_attn_Wq, dtype=np.float32),
        "wk": np.ascontiguousarray(L0_attn_Wk, dtype=np.float32),
        "wv": np.ascontiguousarray(L0_attn_Wv, dtype=np.float32),
        "wg": np.ascontiguousarray(L0_attn_Wg, dtype=np.float32),
        "wo": np.ascontiguousarray(L0_attn_Wo, dtype=np.float32),
        "bq": np.ascontiguousarray(L0_attn_bq, dtype=np.float32),
        "bk": np.ascontiguousarray(L0_attn_bk, dtype=np.float32),
        "bv": np.ascontiguousarray(L0_attn_bv, dtype=np.float32),
        "bg": np.ascontiguousarray(L0_attn_bg, dtype=np.float32),
        "bo": np.ascontiguousarray(L0_attn_bo, dtype=np.float32),
        "es": np.ascontiguousarray(L0_ffn_expert_sel, dtype=np.float32),
        "keys": np.ascontiguousarray(L0_ffn_keys, dtype=np.float32),
        "vals": np.ascontiguousarray(L0_ffn_values, dtype=np.float32),
    }
    in_maps = [dict(common, x=np.ascontiguousarray(x[b])) for b in range(Bz)]
    res = run_bass_kernel_spmd(nc, in_maps, core_ids=list(range(Bz)))
    out = np.stack([np.asarray(r["y"], dtype=np.float32) for r in res.results], axis=0)
    return out



# version 19 (final)
# speedup vs baseline: 1.0501x; 1.0082x over previous
import contextlib

import numpy as np
import ml_dtypes

import concourse.bass as bass
import concourse.mybir as mybir
from concourse.bass_utils import run_bass_kernel_spmd

F32 = mybir.dt.float32
BF16 = mybir.dt.bfloat16
I32 = mybir.dt.int32
AF = mybir.ActivationFunctionType
ALU = mybir.AluOpType

D = 1024
DC = 8
E16 = 16
TT = 512
NCH = 4
ENGINES = ("sync", "tensor", "vector", "scalar", "gpsimd")


class Sem:
    __slots__ = ("h", "n", "name")

    def __init__(self, h, name):
        self.h = h
        self.n = 0
        self.name = name


class Builder:
    def __init__(self, nc):
        self.nc = nc
        self.ops = {e: [] for e in ENGINES}
        self.waited = {}

    def rec(self, eng, fn, waits=(), incs=()):
        w = [(s.h, int(v)) for (s, v) in waits]
        i = [(s.h, int(a)) for (s, a) in incs]
        assert fn is not None or not i
        for s, a in incs:
            s.n += a
        self.ops[eng].append((fn, w, i))

    def flush(self):
        blk_ops = self.ops
        self.ops = {e: [] for e in ENGINES}
        waited = self.waited
        with self.nc.Block() as block:
            for eng_name in ENGINES:
                entries = blk_ops[eng_name]
                if not entries:
                    continue

                def body(e, entries=entries, eng_name=eng_name):
                    for fn, waits, incs in entries:
                        for (h, v) in waits:
                            key = (eng_name, h.num)
                            if waited.get(key, -1) < v:
                                e.wait_ge(h, v)
                                waited[key] = v
                        if fn is None:
                            continue
                        ins = fn(e)
                        for (h, a) in incs:
                            if a == 16:
                                ins.then_inc(h, a)
                            else:
                                e.maybe_drain_then_inc((h, a), fusable=True)

                getattr(block, eng_name)(body)


def build_program(nt):
    S = nt * TT
    nc = bass.Bass("TRN2", target_bir_lowering=False, detect_race_conditions=False)

    x_d = nc.dram_tensor("x", [S, D], F32, kind="ExternalInput").ap()
    y_d = nc.dram_tensor("y", [S, D], F32, kind="ExternalOutput").ap()
    wq_d = nc.dram_tensor("wq", [D, D], F32, kind="ExternalInput").ap()
    wk_d = nc.dram_tensor("wk", [D, D], F32, kind="ExternalInput").ap()
    wv_d = nc.dram_tensor("wv", [D, D], F32, kind="ExternalInput").ap()
    wg_d = nc.dram_tensor("wg", [D, D], F32, kind="ExternalInput").ap()
    wo_d = nc.dram_tensor("wo", [D, D], F32, kind="ExternalInput").ap()
    bq_d = nc.dram_tensor("bq", [D], F32, kind="ExternalInput").ap()
    bk_d = nc.dram_tensor("bk", [D], F32, kind="ExternalInput").ap()
    bv_d = nc.dram_tensor("bv", [D], F32, kind="ExternalInput").ap()
    bg_d = nc.dram_tensor("bg", [D], F32, kind="ExternalInput").ap()
    bo_d = nc.dram_tensor("bo", [D], F32, kind="ExternalInput").ap()
    es_d = nc.dram_tensor("es", [E16, D], F32, kind="ExternalInput").ap()
    keys_d = nc.dram_tensor("keys", [E16, D, 128], F32, kind="ExternalInput").ap()
    vals_d = nc.dram_tensor("vals", [E16, 128, D], F32, kind="ExternalInput").ap()
    x1_d = nc.dram_tensor("x1s", [S, D], F32, kind="Internal").ap()
    ident_d = nc.inline_tensor(np.eye(128, dtype=ml_dtypes.bfloat16), name="ident_dr")

    B = Builder(nc)
    sIN = Sem(nc.alloc_semaphore("sIN"), "sIN")
    sOUT = Sem(nc.alloc_semaphore("sOUT"), "sOUT")
    sT = Sem(nc.alloc_semaphore("sT"), "sT")
    sV = Sem(nc.alloc_semaphore("sV"), "sV")
    sS = Sem(nc.alloc_semaphore("sS"), "sS")
    sG = Sem(nc.alloc_semaphore("sG"), "sG")
    sCFG = Sem(nc.alloc_semaphore("sCFG"), "sCFG")
    sWA = Sem(nc.alloc_semaphore("sWA"), "sWA")
    sWB = Sem(nc.alloc_semaphore("sWB"), "sWB")
    sWC = Sem(nc.alloc_semaphore("sWC"), "sWC")
    sWD = Sem(nc.alloc_semaphore("sWD"), "sWD")
    sWE = Sem(nc.alloc_semaphore("sWE"), "sWE")
    sWF = Sem(nc.alloc_semaphore("sWF"), "sWF")

    em = contextlib.ExitStack()
    E = em.enter_context
    ident = E(nc.sbuf_tensor("ident", [128, 128], BF16))
    ones = E(nc.sbuf_tensor("ones", [65, 128], BF16))
    brow_bo = E(nc.sbuf_tensor("brow_bo", [1, D], BF16))
    bcols = E(nc.sbuf_tensor("bcols", [128, 32], F32))
    keys_sb = E(nc.sbuf_tensor("keys_sb", [128, DC, 2048], BF16))
    vals_sb = E(nc.sbuf_tensor("vals_sb", [128, E16, D], BF16))
    es_sb = E(nc.sbuf_tensor("es_sb", [128, DC, E16], BF16))
    sa = E(nc.sbuf_tensor("sa", [128, NCH, D], F32))
    sb = E(nc.sbuf_tensor("sb", [128, NCH, D], F32))
    ln_tm = E(nc.sbuf_tensor("ln_tm", [128, 2, D], BF16))
    ln_fm = E(nc.sbuf_tensor("ln_fm", [128, DC, TT], BF16))
    sel_bf = E(nc.sbuf_tensor("sel_bf", [E16, TT], BF16))
    sel_row = E(nc.sbuf_tensor("sel_row", [65, 6 * TT], BF16))
    selb = E(nc.sbuf_tensor("selb", [128, 3, TT], BF16))
    stile = E(nc.sbuf_tensor("stile", [128, E16, TT], BF16))
    scr1 = E(nc.sbuf_tensor("scr1", [16, 512], F32))
    bnb0 = E(nc.sbuf_tensor("bnb0", [128, 8, 6], F32))
    st40 = E(nc.sbuf_tensor("st40", [128, 28], F32))
    bnb1 = E(nc.sbuf_tensor("bnb1", [128, 8, 6], F32))
    st41 = E(nc.sbuf_tensor("st41", [128, 28], F32))
    m1a = E(nc.sbuf_tensor("m1a", [128, NCH * nt], F32))
    r1a = E(nc.sbuf_tensor("r1a", [128, NCH * nt], F32))
    qt = E(nc.sbuf_tensor("qt", [128, DC, TT], BF16))
    th = E(nc.sbuf_tensor("th", [128, 2, TT], BF16))
    kvbd = E(nc.sbuf_tensor("kvbd", [128, DC, 128], BF16))
    pT1 = E(nc.psum_tensor("pT1", [128, 1024], BF16))
    pT2 = E(nc.psum_tensor("pT2", [128, 1024], BF16))
    pA = E(nc.psum_tensor("pA", [128, 512], F32))
    pB = E(nc.psum_tensor("pB", [128, 512], F32))
    pC = E(nc.psum_tensor("pC", [128, 512], F32))
    pD = E(nc.psum_tensor("pD", [128, 512], F32))

    ev = {}

    def mark(name, sem):
        ev[name] = (sem, sem.n)

    def w(name):
        return ev[name]

    def rec_startup():
        B.rec("scalar", lambda e: e.preload_act_table(AF.Exp))
        for c in range(NCH):
            B.rec("sync", lambda e, c=c: e.dma_start(out=sa[:, c, :], in_=x_d[c * 128:(c + 1) * 128, :]),
                  incs=[(sIN, 16)])
            mark(f"xin_0_{c}", sIN)
        B.rec("sync", lambda e: e.dma_start(out=ident[:], in_=ident_d.ap()), incs=[(sCFG, 16)])
        mark("ident_in", sCFG)
        B.rec("sync", lambda e: e.dma_start(out=bcols[:, 0:8], in_=bq_d.rearrange("(c p) -> p c", p=128), allow_slow_non_contiguous=True),
              incs=[(sCFG, 16)])
        B.rec("sync", lambda e: e.dma_start(out=bcols[:, 16:24], in_=bg_d.rearrange("(c p) -> p c", p=128), allow_slow_non_contiguous=True),
              incs=[(sCFG, 16)])
        B.rec("sync", lambda e: e.dma_start(out=stile.ap()[0:1, 0:4, :].rearrange('p a b -> p (a b)').bitcast(F32)[0:1, 0:D], in_=bo_d.unsqueeze(0)), incs=[(sCFG, 16)])
        mark("bo_in", sCFG)
        B.rec("vector", lambda e: e.memset(ones[:, :], 1.0))
        B.rec("vector", lambda e: e.tensor_scalar_mul(bcols[:, 8:16], bcols[:, 0:8], -1.0),
              waits=[w("bo_in")])
        B.rec("vector", lambda e: e.tensor_scalar_mul(bcols[:, 16:24], bcols[:, 16:24], 0.5))
        B.rec("vector", lambda e: e.tensor_scalar_mul(stile.ap()[0:1, 0:4, :].rearrange('p a b -> p (a b)').bitcast(F32)[0:1, 0:D], stile.ap()[0:1, 0:4, :].rearrange('p a b -> p (a b)').bitcast(F32)[0:1, 0:D], 2.0))
        B.rec("vector", lambda e: e.tensor_copy(brow_bo[0:1, 0:D], stile.ap()[0:1, 0:4, :].rearrange('p a b -> p (a b)').bitcast(F32)[0:1, 0:D]), incs=[(sV, 1)])
        mark("cfg_done", sV)
        mark("ones_done", sV)
        mark("bcols_done", sV)
        for dc in range(DC):
            B.rec("gpsimd", lambda e, dc=dc: e.dma_start(
                out=keys_sb[:, dc, :],
                in_=keys_d[:, dc * 128:(dc + 1) * 128, :].transpose([1, 0, 2])),
                incs=[(sWA, 16)])
        mark("keys_done", sWA)
        for dc in range(DC):
            B.rec("gpsimd", lambda e, dc=dc: e.dma_start(
                out=es_sb[:, dc, :],
                in_=es_d[:, dc * 128:(dc + 1) * 128].transpose([1, 0]),
                allow_slow_non_contiguous=True),
                incs=[(sWA, 16)])
        mark("es_done", sWA)
        for e_ in range(E16):
            B.rec("gpsimd", lambda e, e_=e_: e.dma_start(
                out=vals_sb[:, e_, :], in_=vals_d[e_]), incs=[(sWB, 16)])
        mark("vals_done", sWB)
        for name, wd, wsb in (("wk", wk_d, wk_sb), ("wv", wv_d, wv_sb)):
            for dc in range(DC):
                B.rec("gpsimd", lambda e, wd=wd, dc=dc, wsb=wsb: e.dma_start(
                    out=wsb[:, dc, :], in_=wd[dc * 128:(dc + 1) * 128, :]),
                    incs=[(sWC, 16)])
        B.rec("gpsimd", lambda e: e.dma_start(out=brows_kv[0:1, 0:D], in_=bk_d.unsqueeze(0)),
              incs=[(sWC, 16)])
        B.rec("gpsimd", lambda e: e.dma_start(out=brows_kv[0:1, D:2 * D], in_=bv_d.unsqueeze(0)),
              incs=[(sWC, 16)])
        mark("wkv_done", sWC)

    def rec_bn(src, c, bufs, waits=()):
        bnb, g = bufs
        B.rec("vector", lambda e, c=c: e.bn_stats(
            bnb[:, 2 * c, :], src[:, c, 0:512]), waits=list(waits))
        B.rec("vector", lambda e, c=c: e.bn_stats(
            bnb[:, 2 * c + 1, :], src[:, c, 512:1024]))

    def rec_ln_finish(t, which, bufs):
        bnb, g = bufs
        V = "vector"

        def RD(fn):
            B.rec(V, fn)
            B.rec(V, lambda e: e.drain())

        B.rec(V, lambda e: e.drain())
        for c in range(NCH):
            B.rec(V, lambda e, c=c: e.bn_aggr(g[:, 2 * c:2 * c + 2], bnb[:, 2 * c:2 * c + 2, :]))
        B.rec(V, lambda e: e.drain())
        mvv = g.ap()[:, 1:8:2]
        us = g.ap()[:, 12:16]
        Bs = g.ap()[:, 16:20]
        Cs = g.ap()[:, 20:24]
        ys = g.ap()[:, 24:28]
        rs = g.ap()[:, 8:12]
        RD(lambda e: e.tensor_scalar_add(us, mvv, 1e-5))
        RD(lambda e: e.tensor_scalar(Bs.bitcast(I32), us.bitcast(I32), 1, None, ALU.arith_shift_right))
        RD(lambda e: e.tensor_scalar(Cs.bitcast(I32), Bs.bitcast(I32), -1, 0x5F3759DF, ALU.mult, ALU.add))
        for it in range(3):
            yin = Cs if it % 2 == 0 else ys
            yout = ys if it % 2 == 0 else Cs
            last = it == 2
            RD(lambda e, yin=yin: e.tensor_tensor(Bs, yin, yin, ALU.mult))
            RD(lambda e: e.tensor_tensor(rs, Bs, us, ALU.mult))
            RD(lambda e: e.tensor_scalar(Bs, rs, -0.5, 1.5, ALU.mult, ALU.add))
            RD(lambda e, yin=yin, yout=(rs if last else yout): e.tensor_tensor(yout, yin, Bs, ALU.mult))
        if which == "ln1":
            B.rec(V, lambda e, t=t: e.tensor_copy(m1a[:, NCH * t:NCH * (t + 1)], g.ap()[:, 0:8:2]))
            B.rec(V, lambda e, t=t: e.tensor_copy(r1a[:, NCH * t:NCH * (t + 1)], rs))
            B.rec(V, lambda e: e.drain())

    def rec_ln_finish_pair(t, which, bufs, p):
        bnb, g = bufs
        V = "vector"

        def RD(fn):
            B.rec(V, fn)
            B.rec(V, lambda e: e.drain())

        B.rec(V, lambda e: e.drain())
        for c in (2 * p, 2 * p + 1):
            B.rec(V, lambda e, c=c: e.bn_aggr(g[:, 2 * c:2 * c + 2], bnb[:, 2 * c:2 * c + 2, :]))
        B.rec(V, lambda e: e.drain())
        o = 2 * p
        mvv = g.ap()[:, 1 + 4 * p:8:2][:, 0:2]
        us = g.ap()[:, 12 + o:14 + o]
        Bs = g.ap()[:, 16 + o:18 + o]
        Cs = g.ap()[:, 20 + o:22 + o]
        ys = g.ap()[:, 24 + o:26 + o]
        rs = g.ap()[:, 8 + o:10 + o]
        RD(lambda e: e.tensor_scalar_add(us, mvv, 1e-5))
        RD(lambda e: e.tensor_scalar(Bs.bitcast(I32), us.bitcast(I32), 1, None, ALU.arith_shift_right))
        RD(lambda e: e.tensor_scalar(Cs.bitcast(I32), Bs.bitcast(I32), -1, 0x5F3759DF, ALU.mult, ALU.add))
        for it in range(3):
            yin = Cs if it % 2 == 0 else ys
            yout = ys if it % 2 == 0 else Cs
            last = it == 2
            RD(lambda e, yin=yin: e.tensor_tensor(Bs, yin, yin, ALU.mult))
            RD(lambda e: e.tensor_tensor(rs, Bs, us, ALU.mult))
            RD(lambda e: e.tensor_scalar(Bs, rs, -0.5, 1.5, ALU.mult, ALU.add))
            RD(lambda e, yin=yin, yout=(rs if last else yout): e.tensor_tensor(yout, yin, Bs, ALU.mult))
        if which == "ln1":
            B.rec(V, lambda e, t=t, p=p: e.tensor_copy(
                m1a[:, NCH * t + 2 * p:NCH * t + 2 * p + 2], g.ap()[:, 4 * p:4 * p + 4:2]))
            B.rec(V, lambda e, t=t, p=p: e.tensor_copy(
                r1a[:, NCH * t + 2 * p:NCH * t + 2 * p + 2], rs))
            B.rec(V, lambda e: e.drain())

    def rec_ln_apply_T_pair(t, src, which, g, p):
        for c in (2 * p, 2 * p + 1):
            mc, rc = g.ap()[:, 2 * c:2 * c + 1], g.ap()[:, 8 + c:9 + c]
            awaits = []
            if ev.get(f"ln_tm_free_{c % 2}"):
                awaits.append(ev[f"ln_tm_free_{c % 2}"])
            B.rec("vector", lambda e, c=c, mc=mc, rc=rc: e.tensor_scalar(
                ln_tm[:, c % 2, :], src[:, c, :], mc, rc, ALU.subtract, ALU.mult),
                waits=awaits, incs=[(sV, 1)])
            mark(f"{which}_tm_{t}_{c}", sV)
        for c in (2 * p, 2 * p + 1):
            pT = pT1 if c % 2 == 0 else pT2
            waits = [w(f"{which}_tm_{t}_{c}"), w("ident_in")]
            if ev.get(f"Tevac_prev_{c % 2}"):
                waits.append(ev[f"Tevac_prev_{c % 2}"])
            if ev.get("pT_busy"):
                waits.append(ev["pT_busy"])
            for j in range(DC):
                B.rec("tensor", lambda e, c=c, j=j, pT=pT: e.transpose(
                    pT[:, j * 128:(j + 1) * 128], ln_tm[:, c % 2, j * 128:(j + 1) * 128], ident[:]),
                    waits=waits if j == 0 else [], incs=[(sT, 1)] if j == DC - 1 else [])
            mark(f"{which}_T_{t}_{c}", sT)
            ev[f"ln_tm_free_{c % 2}"] = (sT, sT.n)
            B.rec("scalar", lambda e, c=c, pT=pT: e.copy(
                ln_fm.ap()[:, :, c * 128:(c + 1) * 128],
                pT.ap().rearrange("p (j k) -> p j k", j=DC)),
                waits=[w(f"{which}_T_{t}_{c}")], incs=[(sS, 1)])
            mark(f"{which}_fm_{t}_{c}", sS)
            ev[f"Tevac_prev_{c % 2}"] = (sS, sS.n)
        if p == 1:
            mark(f"{which}_fm_{t}", sS)

    def rec_ln_apply_transpose(t, src, which, mcol=None, rcol=None, g=None):
        for c in range(NCH):
            if mcol is None:
                mc, rc = g.ap()[:, 2 * c:2 * c + 1], g.ap()[:, 8 + c:9 + c]
            else:
                mc, rc = mcol(c), rcol(c)
            awaits = []
            if ev.get(f"ln_tm_free_{c % 2}"):
                awaits.append(ev[f"ln_tm_free_{c % 2}"])
            B.rec("vector", lambda e, c=c, mc=mc, rc=rc: e.tensor_scalar(
                ln_tm[:, c % 2, :], src[:, c, :], mc, rc, ALU.subtract, ALU.mult),
                waits=awaits, incs=[(sV, 1)])
            mark(f"{which}_tm_{t}_{c}", sV)
            pT = pT1 if c % 2 == 0 else pT2
            waits = [w(f"{which}_tm_{t}_{c}"), w("ident_in")]
            if ev.get(f"Tevac_prev_{c % 2}"):
                waits.append(ev[f"Tevac_prev_{c % 2}"])
            if ev.get("pT_busy"):
                waits.append(ev["pT_busy"])
            for j in range(DC):
                B.rec("tensor", lambda e, c=c, j=j, pT=pT: e.transpose(
                    pT[:, j * 128:(j + 1) * 128], ln_tm[:, c % 2, j * 128:(j + 1) * 128], ident[:]),
                    waits=waits if j == 0 else [], incs=[(sT, 1)] if j == DC - 1 else [])
            mark(f"{which}_T_{t}_{c}", sT)
            ev[f"ln_tm_free_{c % 2}"] = (sT, sT.n)
            B.rec("scalar", lambda e, c=c, pT=pT: e.copy(
                ln_fm.ap()[:, :, c * 128:(c + 1) * 128],
                pT.ap().rearrange("p (j k) -> p j k", j=DC)),
                waits=[w(f"{which}_T_{t}_{c}")], incs=[(sS, 1)])
            mark(f"{which}_fm_{t}_{c}", sS)
            ev[f"Tevac_prev_{c % 2}"] = (sS, sS.n)
        mark(f"{which}_fm_{t}", sS)

    def rec_ffn(t, which, lnname, resid_src, out_dst, pass_no, post_evac=None):
        waits = [w(f"{lnname}_fm_{t}"), w("es_done"), w("ident_in")]
        if ev.get("pa_free"):
            waits.append(ev["pa_free"])
        for dc in range(DC):
            B.rec("tensor", lambda e, dc=dc: e.matmul(
                pA[0:16, :], es_sb[:, dc, :], ln_fm[:, dc, :],
                start=(dc == 0), stop=(dc == DC - 1)),
                waits=waits if dc == 0 else [], incs=[(sT, 1)] if dc == DC - 1 else [])
        mark(f"sel_mm_{which}_{t}", sT)
        B.rec("scalar", lambda e: e.activation(scr1[0:16, 0:TT], pA[0:16, :], AF.Tanh, scale=0.5),
              waits=[w(f"sel_mm_{which}_{t}")], incs=[(sS, 1)])
        mark(f"sel_th_{which}_{t}", sS)
        B.rec("vector", lambda e: e.tensor_scalar(
            sel_bf[:, :], scr1[0:16, 0:TT], 0.5, 0.5, ALU.mult, ALU.add),
            waits=[w(f"sel_th_{which}_{t}")], incs=[(sV, 1)])
        mark(f"sel_{which}_{t}", sV)
        srw = [w(f"sel_{which}_{t}")]
        if ev.get("sel_row_free"):
            srw.append(ev["sel_row_free"])
        B.rec("gpsimd", lambda e: e.dma_start(out=sel_row[0:1, 0:6 * TT], in_=sel_bf[0:6, :]),
              waits=srw, incs=[(sOUT, 16)])
        B.rec("gpsimd", lambda e: e.dma_start(out=sel_row[32:33, 0:6 * TT], in_=sel_bf[6:12, :]),
              incs=[(sOUT, 16)])
        B.rec("gpsimd", lambda e: e.dma_start(out=sel_row[64:65, 0:4 * TT], in_=sel_bf[12:16, :]),
              incs=[(sOUT, 16)])
        mark(f"selrow_{which}_{t}", sOUT)
        nb = 2 if pass_no == 1 else 4
        banks = (pA, pB) if pass_no == 1 else (pA, pB, pE, pF)
        for e_ in range(E16):
            par = e_ % 2
            spar = e_ % 3
            pSB = (pT1 if par == 0 else pT2).ap().bitcast(F32)[:, 0:512]
            sbs = selb.ap()[:, spar, :]
            waits = [w(f"selrow_{which}_{t}"), w("ones_done")]
            if ev.get(f"selb_evac_{par}"):
                waits.append(ev[f"selb_evac_{par}"])
            sp = 32 * (e_ // 6) if e_ < 12 else 64
            so = (e_ % 6) * TT if e_ < 12 else (e_ - 12) * TT
            B.rec("tensor", lambda e, pSB=pSB, sp=sp, so=so: e.matmul(
                pSB, ones[sp:sp + 1, :], sel_row[sp:sp + 1, so:so + TT],
                start=True, stop=True),
                waits=waits, incs=[(sT, 1)])
            mark(f"selb_mm_{which}_{t}_{e_}", sT)
            ewaits = [w(f"selb_mm_{which}_{t}_{e_}")]
            if e_ >= 3:
                ewaits.append(w(f"stile_{which}_{t}_{e_ - 3}"))
            B.rec("scalar", lambda e, pSB=pSB, sbs=sbs: e.copy(sbs, pSB),
                  waits=ewaits, incs=[(sS, 1)])
            mark(f"selb_{which}_{t}_{e_}", sS)
            ev[f"selb_evac_{par}"] = (sS, sS.n)
            pSC = banks[e_ % nb]
            swaits = []
            if e_ < 2:
                swaits.append(w(f"sel_th_{which}_{t}"))
            if pass_no == 2 and e_ in (2, 3) and ev.get("pef_free"):
                swaits.append(ev["pef_free"])
            if e_ >= nb:
                swaits.append(w(f"stile_{which}_{t}_{e_ - nb}"))
            if pass_no == 1 and t == 0 and e_ == 0:
                swaits.append(w("keys_done"))

            for dc in range(DC):
                B.rec("tensor", lambda e, e_=e_, dc=dc, pSC=pSC: e.matmul(
                    pSC[:], keys_sb[:, dc, e_ * 128:(e_ + 1) * 128], ln_fm[:, dc, :],
                    start=(dc == 0), stop=(dc == DC - 1)),
                    waits=swaits if dc == 0 else [], incs=[(sT, 1)] if dc == DC - 1 else [])
            mark(f"sc_mm_{which}_{t}_{e_}", sT)
            B.rec("vector", lambda e, e_=e_, pSC=pSC, sbs=sbs: e.scalar_tensor_tensor(
                stile[:, e_, :], pSC[:], 0.0, sbs, ALU.max, ALU.mult),
                waits=[w(f"sc_mm_{which}_{t}_{e_}"), w(f"selb_{which}_{t}_{e_}")],
                incs=[(sV, 1)])
            mark(f"stile_{which}_{t}_{e_}", sV)
        ev["pT_busy"] = (sS, ev[f"selb_{which}_{t}_{E16 - 1}"][1])
        ev["sel_row_free"] = (sT, ev[f"selb_mm_{which}_{t}_{E16 - 1}"][1])
        ev["pa_free"] = (sV, ev[f"stile_{which}_{t}_{E16 - 1}"][1])
        for c in range(NCH):
            if pass_no == 1 or c % 2 == 0:
                vb = (pC, pD)
            else:
                vb = (pE, pF)
            waits = [w(f"stile_{which}_{t}_{E16 - 1}")]
            if pass_no == 1 and t == 0 and c == 0:
                waits.append(w("vals_done"))
            if pass_no == 1 or c < 2:
                if ev.get("pv_free"):
                    waits.append(ev["pv_free"])
            else:
                waits.append(ev[f"pvv_free_{c % 2}"])
            for e_ in range(E16):
                for gg in range(2):
                    pV = vb[gg]
                    B.rec("tensor", lambda e, c=c, e_=e_, gg=gg, pV=pV: e.matmul(
                        pV[:], stile[:, e_, c * 128:(c + 1) * 128],
                        vals_sb[:, e_, gg * 512:(gg + 1) * 512],
                        start=(e_ == 0), stop=(e_ == E16 - 1)),
                        waits=waits if (e_ == 0 and gg == 0) else [],
                        incs=[(sT, 1)] if (e_ == E16 - 1 and gg == 1) else [])
            mark(f"ffnv_{which}_{t}_{c}", sT)
            ewaits = [w(f"ffnv_{which}_{t}_{c}")]
            for gg in range(2):
                pV = vb[gg]
                B.rec("vector", lambda e, c=c, gg=gg, pV=pV: e.scalar_tensor_tensor(
                    out_dst[:, c, gg * 512:(gg + 1) * 512], pV[:], 0.5,
                    resid_src[:, c, gg * 512:(gg + 1) * 512], ALU.mult, ALU.add),
                    waits=ewaits,
                    incs=[(sV, 1)] if gg == 1 else [])
            mark(f"{which}_out_{t}_{c}", sV)
            ev["pv_free"] = (sV, sV.n)
            ev[f"pvv_free_{c % 2}"] = (sV, sV.n)
            if post_evac is not None:
                post_evac(c)

    def rec_pass1_tile(t):
        if t == 0:
            for c in range(NCH):
                rec_bn(sa, c, (bnb0, st40), waits=[w(f"xin_0_{c}")])
            rec_ln_finish(0, "ln0", (bnb0, st40))
        rec_ln_apply_transpose(t, sa, "ln0", g=st40)
        def p1_post_evac(c):
            rec_bn(sb, c, (bnb1, st41))
            if c == 1:
                rec_ln_finish_pair(t, "ln1", (bnb1, st41), 0)
            if c == 2:
                rec_ln_apply_T_pair(t, sb, "ln1", st41, 0)
            if c == 3:
                rec_ln_finish_pair(t, "ln1", (bnb1, st41), 1)

        rec_ffn(t, "ffn1", "ln0", sa, sb, 1, post_evac=p1_post_evac)
        for c in range(NCH):
            B.rec("gpsimd", lambda e, t=t, c=c: e.dma_start(
                out=x1_d[(t * NCH + c) * 128:(t * NCH + c + 1) * 128, :], in_=sb[:, c, :]),
                waits=[w(f"ffn1_out_{t}_{c}")], incs=[(sOUT, 16)])
            mark(f"x1st_{t}_{c}", sOUT)
        if t > 0:
            B.rec("vector", None, waits=[w(f"x1st_{t - 1}_{NCH - 1}")])
        for c in range(NCH):
            if c == 2:
                rec_ln_apply_T_pair(t, sb, "ln1", st41, 1)
            waits = [w(f"ln1_fm_{t}_{c}"), w("wkv_done"),
                     w("ones_done"), ev["pa_free"], ev["pv_free"]]
            lnc = ln_fm.ap()[:, :, c * 128:(c + 1) * 128]
            for dc in range(DC):
                for g_, pX, wsb in ((0, pA, wk_sb), (1, pB, wk_sb), (2, pC, wv_sb), (3, pD, wv_sb)):
                    gg = g_ % 2
                    B.rec("tensor", lambda e, dc=dc, pX=pX, wsb=wsb, gg=gg, lnc=lnc: e.matmul(
                        pX[:], lnc[:, dc, :], wsb[:, dc, gg * 512:(gg + 1) * 512],
                        start=(dc == 0), stop=False),
                        waits=waits if (dc == 0 and g_ == 0) else [])
            for g_, pX, ro in ((0, pA, 0), (1, pB, 512), (2, pC, D), (3, pD, D + 512)):
                B.rec("tensor", lambda e, pX=pX, ro=ro: e.matmul(
                    pX[:], ones[0:1, :], brows_kv[0:1, ro:ro + 512], start=False, stop=True),
                    incs=[(sT, 1)] if g_ == 3 else [])
            mark(f"kvproj_{t}_{c}", sT)
            par = c % 2
            for gg, pX in ((0, pC), (1, pD)):
                B.rec("vector", lambda e, gg=gg, pX=pX, par=par: e.tensor_copy(
                    vtil[:, par, gg * 512:(gg + 1) * 512], pX[:]),
                    waits=[w(f"kvproj_{t}_{c}")] if gg == 0 else [])
            for gg, pX in ((0, pA), (1, pB)):
                B.rec("vector", lambda e, gg=gg, pX=pX, par=par: e.tensor_scalar_max(
                    ktil[:, par, gg * 512:(gg + 1) * 512], pX[:], 0.0))
                kw = []
                if t * NCH + c >= 2:
                    pt, pc = divmod(t * NCH + c - 2, NCH)
                    kw = [w(f"kexp_{pt}_{pc}")]
                B.rec("vector", lambda e, gg=gg, pX=pX, par=par: e.tensor_scalar_min(
                    scr2k[:, par, gg * 512:(gg + 1) * 512], pX[:], 0.0),
                    waits=kw if gg == 0 else [],
                    incs=[(sV, 1)] if gg == 1 else [])
            mark(f"kv_free_{t}_{c}", sV)
            ev["pa_free"] = (sV, sV.n)
            ev["pv_free"] = (sV, sV.n)
            B.rec("scalar", lambda e, par=par: e.activation(
                scr2k[:, par, :], scr2k[:, par, :], AF.Exp),
                waits=[w(f"kv_free_{t}_{c}")], incs=[(sS, 1)])
            mark(f"kexp_{t}_{c}", sS)
            B.rec("vector", lambda e, par=par: e.tensor_tensor(
                ktil[:, par, :], ktil[:, par, :], scr2k[:, par, :], ALU.add),
                waits=[w(f"kexp_{t}_{c}")], incs=[(sV, 1)])
            mark(f"kv_in_{t}_{c}", sV)

            def rec_kvmm(tc):
                tt_, cc_ = divmod(tc, NCH)
                par2 = cc_ % 2
                first = tc == 0
                last = tc == nt * NCH - 1
                for h in range(E16):
                    B.rec("tensor", lambda e, h=h, first=first, last=last, par2=par2: e.matmul(
                        pkv[0:64, h * 64:(h + 1) * 64], ktil[:, par2, h * 64:(h + 1) * 64],
                        vtil[:, par2, h * 64:(h + 1) * 64],
                        start=(first and h % 8 == 0), stop=(last and h % 8 == 7),
                        skip_group_check=True),
                        waits=[w(f"kv_in_{tt_}_{cc_}")] if h == 0 else [],
                        incs=[(sT, 1)] if h == E16 - 1 else [])
                mark(f"kvacc_{tt_}_{cc_}", sT)

            tc = t * NCH + c
            if tc >= 1:
                rec_kvmm(tc - 1)
            if tc == nt * NCH - 1:
                rec_kvmm(tc)
        if t + 1 < nt:
            for c in range(NCH):
                B.rec("sync", lambda e, t=t, c=c: e.dma_start(
                    out=sa[:, c, :], in_=x_d[((t + 1) * NCH + c) * 128:((t + 1) * NCH + c + 1) * 128, :]),
                    waits=[w(f"ffn1_out_{t}_{c}")], incs=[(sIN, 16)])
                mark(f"xin_{t + 1}_{c}", sIN)
            for c in range(NCH):
                rec_bn(sa, c, (bnb0, st40), waits=[w(f"xin_{t + 1}_{c}")])
            rec_ln_finish(t + 1, "ln0", (bnb0, st40))

    def rec_kv_finalize():
        B.rec("vector", lambda e: e.tensor_copy(kvbf[:, :], pkv[:, :]),
              waits=[w(f"kvacc_{nt - 1}_{NCH - 1}")])
        B.rec("vector", lambda e: e.memset(kvbd[:, :, :], 0.0), incs=[(sV, 1)])
        mark("kvbf", sV)
        for cc in range(DC):
            B.rec("gpsimd", lambda e, cc=cc: e.dma_start(
                out=kvbd[0:64, cc, 0:64], in_=kvbf[0:64, cc * 128:cc * 128 + 64]),
                waits=[w("kvbf")], incs=[(sOUT, 16)])
            B.rec("gpsimd", lambda e, cc=cc: e.dma_start(
                out=kvbd[64:128, cc, 64:128], in_=kvbf[0:64, cc * 128 + 64:cc * 128 + 128]),
                incs=[(sOUT, 16)])
        mark("kvbd", sOUT)

    def rec_phase2_weights():
        kvp = w(f"kvproj_{nt - 1}_{NCH - 1}")
        kva = w(f"kvacc_{nt - 1}_{NCH - 1}")
        for i, (wd, wsb, sw) in enumerate(((wq_d, wq_sb, sWD), (wg_d, wg_sb, sWE), (wo_d, wo_sb, sWF))):
            for dc in range(DC):
                waits = []
                if dc == 0:
                    waits = [kvp] if i < 2 else [kva, w("kvbd")]
                B.rec("gpsimd", lambda e, wd=wd, dc=dc, wsb=wsb: e.dma_start(
                    out=wsb[:, dc, :], in_=wd[dc * 128:(dc + 1) * 128, :]),
                    waits=waits, incs=[(sw, 16)])
            mark(f"p2w_{i}", sw)

    def rec_pass2_tile(t):
        q2 = stile.ap()[:, 0:4, :].rearrange("p a b -> p (a b)").bitcast(F32)
        q3 = stile.ap()[:, 4:8, :].rearrange("p a b -> p (a b)").bitcast(F32)
        for c in range(NCH):
            if t == 0:
                waits = [w(f"x1st_{nt - 1}_{NCH - 1}")]
            else:
                waits = [w(f"yst_{t - 1}_{c}")]
            B.rec("sync", lambda e, t=t, c=c: e.dma_start(
                out=sa[:, c, :], in_=x1_d[(t * NCH + c) * 128:(t * NCH + c + 1) * 128, :]),
                waits=waits, incs=[(sIN, 16)])
            mark(f"x1in_{t}_{c}", sIN)
        B.rec("vector", None, waits=[w(f"x1in_{t}_{NCH - 1}")])
        rec_ln_apply_transpose(t, sa, "ln1b",
                               mcol=lambda c: m1a.ap()[:, NCH * t + c:NCH * t + c + 1],
                               rcol=lambda c: r1a.ap()[:, NCH * t + c:NCH * t + c + 1])
        for fc in range(DC):
            par = fc % 2
            pQ = pA if par == 0 else pB
            pG = pC if par == 0 else pD
            qwaits = [w(f"ln1b_fm_{t}"), w("p2w_0"), w("bcols_done")]
            if ev.get(f"pq_free_{par}"):
                qwaits.append(ev[f"pq_free_{par}"])
            for dc in range(DC):
                B.rec("tensor", lambda e, fc=fc, dc=dc, pQ=pQ: e.matmul(
                    pQ[:], wq_sb[:, dc, fc * 128:(fc + 1) * 128], ln_fm[:, dc, :],
                    start=(dc == 0), stop=(dc == DC - 1)),
                    waits=qwaits if dc == 0 else [],
                    incs=[(sT, 1)] if dc == DC - 1 else [])
            mark(f"qmm_{t}_{fc}", sT)
            gwaits = [w("p2w_1")]
            if ev.get(f"pg_free_{par}"):
                gwaits.append(ev[f"pg_free_{par}"])
            for dc in range(DC):
                B.rec("tensor", lambda e, fc=fc, dc=dc, pG=pG: e.matmul(
                    pG[:], wg_sb[:, dc, fc * 128:(fc + 1) * 128], ln_fm[:, dc, :],
                    start=(dc == 0), stop=(dc == DC - 1)),
                    waits=gwaits if dc == 0 else [],
                    incs=[(sT, 1)] if dc == DC - 1 else [])
            mark(f"gmm_{t}_{fc}", sT)
            sh = par * 512
            B.rec("vector", lambda e, fc=fc, pQ=pQ, sh=sh: e.tensor_scalar(
                q2[:, sh:sh + 512], pQ[:], bcols.ap()[:, fc:fc + 1], 0.0,
                ALU.add, ALU.min),
                waits=[w(f"qmm_{t}_{fc}")])
            B.rec("vector", lambda e, fc=fc, pQ=pQ: e.tensor_scalar(
                qt[:, fc, :], pQ[:], bcols.ap()[:, fc:fc + 1], 0.0,
                ALU.add, ALU.max),
                incs=[(sV, 1)])
            mark(f"qpq_{t}_{fc}", sV)
            ewaits = [w(f"qpq_{t}_{fc}")]
            if fc >= 2:
                ewaits.append(w(f"attn_{t}_{fc - 2}"))
            B.rec("scalar", lambda e, sh=sh, par=par: e.activation(
                stile[:, par, :], q2[:, sh:sh + 512], AF.Exp),
                waits=ewaits, incs=[(sS, 1)])
            mark(f"qexp_{t}_{fc}", sS)
            twaits = [w(f"gmm_{t}_{fc}")]
            if ev.get(f"th_free_{par}"):
                twaits.append(ev[f"th_free_{par}"])
            B.rec("scalar", lambda e, fc=fc, pG=pG, par=par: e.activation(
                th[:, par, :], pG[:], AF.Tanh, scale=0.5,
                bias=bcols.ap()[:, 16 + fc:17 + fc]),
                waits=twaits, incs=[(sS, 1)])
            mark(f"th_{t}_{fc}", sS)
            ev[f"pg_free_{par}"] = (sS, sS.n)
            B.rec("tensor", lambda e, fc=fc, pQ=pQ: e.matmul(
                pQ[:], kvbd[:, fc, :], qt[:, fc, :], start=True, stop=False),
                waits=[w(f"qpq_{t}_{fc}"), w("kvbd")])
            B.rec("tensor", lambda e, fc=fc, pQ=pQ, par=par: e.matmul(
                pQ[:], kvbd[:, fc, :], stile[:, par, :], start=False, stop=True),
                waits=[w(f"qexp_{t}_{fc}")], incs=[(sT, 1)])
            mark(f"attn_{t}_{fc}", sT)
            B.rec("vector", lambda e, fc=fc, pQ=pQ, par=par: e.scalar_tensor_tensor(
                qt[:, fc, :], th[:, par, :], 1.0, pQ[:], ALU.add, ALU.mult),
                waits=[w(f"attn_{t}_{fc}"), w(f"th_{t}_{fc}")], incs=[(sV, 1)])
            mark(f"gt_{t}_{fc}", sV)
            ev[f"pq_free_{par}"] = (sV, sV.n)
            ev[f"th_free_{par}"] = (sV, sV.n)
        ev["pa_free"] = (sV, sV.n)
        for c in range(NCH):
            wb = (pC, pD) if c % 2 == 0 else (pE, pF)
            waits = [w(f"gt_{t}_{DC - 1}"), w("p2w_2")]
            if c >= 2:
                waits.append(w(f"x2_{t}_{c - 2}"))
            elif ev.get("pv_free"):
                waits.append(ev["pv_free"])
            for fc in range(DC):
                for gg in range(2):
                    B.rec("tensor", lambda e, c=c, fc=fc, gg=gg, pX=wb[gg]: e.matmul(
                        pX[:], qt[:, fc, c * 128:(c + 1) * 128],
                        wo_sb[:, fc, gg * 512:(gg + 1) * 512],
                        start=(fc == 0), stop=False),
                        waits=waits if (fc == 0 and gg == 0) else [])
            for gg in range(2):
                B.rec("tensor", lambda e, gg=gg, pX=wb[gg]: e.matmul(
                    pX[:], ones[0:1, :], brow_bo[0:1, gg * 512:(gg + 1) * 512],
                    start=False, stop=True),
                    incs=[(sT, 1)] if gg == 1 else [])
            mark(f"womm_{t}_{c}", sT)
            for gg in range(2):
                B.rec("vector", lambda e, c=c, gg=gg, pX=wb[gg]: e.scalar_tensor_tensor(
                    sb[:, c, gg * 512:(gg + 1) * 512], pX[:], 0.5,
                    sa[:, c, gg * 512:(gg + 1) * 512], ALU.mult, ALU.add),
                    waits=[w(f"womm_{t}_{c}")],
                    incs=[(sV, 1)] if gg == 1 else [])
            mark(f"x2_{t}_{c}", sV)
            ev["pv_free"] = (sV, sV.n)
            ev[f"pvv_free_{c % 2}"] = (sV, sV.n)
            rec_bn(sb, c, (bnb0, st40))
        ev["pef_free"] = w(f"x2_{t}_{NCH - 1}")
        rec_ln_finish(t, "ln2", (bnb0, st40))
        rec_ln_apply_transpose(t, sb, "ln2", g=st40)
        rec_ffn(t, "ffn2", "ln2", sb, sa, 2)
        for c in range(NCH):
            B.rec("gpsimd", lambda e, t=t, c=c: e.dma_start(
                out=y_d[(t * NCH + c) * 128:(t * NCH + c + 1) * 128, :], in_=sa[:, c, :]),
                waits=[w(f"ffn2_out_{t}_{c}")], incs=[(sOUT, 16)])
            mark(f"yst_{t}_{c}", sOUT)

    with contextlib.ExitStack() as p1:
        wk_sb = p1.enter_context(nc.sbuf_tensor("wk_sb", [128, DC, D], BF16))
        wv_sb = p1.enter_context(nc.sbuf_tensor("wv_sb", [128, DC, D], BF16))
        brows_kv = p1.enter_context(nc.sbuf_tensor("brows_kv", [1, 2 * D], BF16))
        scr2k = p1.enter_context(nc.sbuf_tensor("scr2k", [128, 2, D], F32))
        ktil = p1.enter_context(nc.sbuf_tensor("ktil", [128, 2, D], BF16))
        vtil = p1.enter_context(nc.sbuf_tensor("vtil", [128, 2, D], BF16))
        kvbf = p1.enter_context(nc.sbuf_tensor("kvbf", [64, 1024], BF16))
        pkv = p1.enter_context(nc.psum_tensor("pkv", [64, 1024], F32))
        rec_startup()
        for t in range(nt):
            rec_pass1_tile(t)
        rec_kv_finalize()
        B.flush()
    with contextlib.ExitStack() as p2:
        pE = p2.enter_context(nc.psum_tensor("pE", [128, 512], F32))
        pF = p2.enter_context(nc.psum_tensor("pF", [128, 512], F32))
        wq_sb = p2.enter_context(nc.sbuf_tensor("wq_sb", [128, DC, D], BF16))
        wg_sb = p2.enter_context(nc.sbuf_tensor("wg_sb", [128, DC, D], BF16))
        wo_sb = p2.enter_context(nc.sbuf_tensor("wo_sb", [128, DC, D], BF16))
        rec_phase2_weights()
        for t in range(nt):
            rec_pass2_tile(t)
        B.rec("gpsimd", lambda e: e.sem_inc(sG.h, 1),
              waits=[(sOUT, sOUT.n), (sIN, sIN.n)])
        B.flush()
    em.close()
    return nc


_PROG_CACHE = {}


def _get_prog(nt):
    if nt not in _PROG_CACHE:
        _PROG_CACHE[nt] = build_program(nt)
    return _PROG_CACHE[nt]


def kernel(L0_attn_x, L0_attn_Wq, L0_attn_bq, L0_attn_Wk, L0_attn_bk,
           L0_attn_Wv, L0_attn_bv, L0_attn_Wg, L0_attn_bg, L0_attn_Wo,
           L0_attn_bo, L0_ffn_expert_sel, L0_ffn_keys, L0_ffn_values):
    x = np.ascontiguousarray(np.asarray(L0_attn_x, dtype=np.float32))
    Bz, S, Dm = x.shape
    assert Dm == D and S % TT == 0
    nt = S // TT
    nc = _get_prog(nt)
    common = {
        "wq": np.ascontiguousarray(L0_attn_Wq, dtype=np.float32),
        "wk": np.ascontiguousarray(L0_attn_Wk, dtype=np.float32),
        "wv": np.ascontiguousarray(L0_attn_Wv, dtype=np.float32),
        "wg": np.ascontiguousarray(L0_attn_Wg, dtype=np.float32),
        "wo": np.ascontiguousarray(L0_attn_Wo, dtype=np.float32),
        "bq": np.ascontiguousarray(L0_attn_bq, dtype=np.float32),
        "bk": np.ascontiguousarray(L0_attn_bk, dtype=np.float32),
        "bv": np.ascontiguousarray(L0_attn_bv, dtype=np.float32),
        "bg": np.ascontiguousarray(L0_attn_bg, dtype=np.float32),
        "bo": np.ascontiguousarray(L0_attn_bo, dtype=np.float32),
        "es": np.ascontiguousarray(L0_ffn_expert_sel, dtype=np.float32),
        "keys": np.ascontiguousarray(L0_ffn_keys, dtype=np.float32),
        "vals": np.ascontiguousarray(L0_ffn_values, dtype=np.float32),
    }
    in_maps = [dict(common, x=np.ascontiguousarray(x[b])) for b in range(Bz)]
    res = run_bass_kernel_spmd(nc, in_maps, core_ids=list(range(Bz)))
    out = np.stack([np.asarray(r["y"], dtype=np.float32) for r in res.results], axis=0)
    return out



# version 20
# speedup vs baseline: 1.0723x; 1.0211x over previous
import contextlib

import numpy as np
import ml_dtypes

import concourse.bass as bass
import concourse.mybir as mybir
from concourse.bass_utils import run_bass_kernel_spmd

F32 = mybir.dt.float32
BF16 = mybir.dt.bfloat16
I32 = mybir.dt.int32
AF = mybir.ActivationFunctionType
ALU = mybir.AluOpType

D = 1024
DC = 8
E16 = 16
TT = 512
NCH = 4
ENGINES = ("sync", "tensor", "vector", "scalar", "gpsimd")


class Sem:
    __slots__ = ("h", "n", "name")

    def __init__(self, h, name):
        self.h = h
        self.n = 0
        self.name = name


class Builder:
    def __init__(self, nc):
        self.nc = nc
        self.ops = {e: [] for e in ENGINES}
        self.waited = {}

    def rec(self, eng, fn, waits=(), incs=()):
        w = [(s.h, int(v)) for (s, v) in waits]
        i = [(s.h, int(a)) for (s, a) in incs]
        assert fn is not None or not i
        for s, a in incs:
            s.n += a
        self.ops[eng].append((fn, w, i))

    def flush(self):
        blk_ops = self.ops
        self.ops = {e: [] for e in ENGINES}
        waited = self.waited
        with self.nc.Block() as block:
            for eng_name in ENGINES:
                entries = blk_ops[eng_name]
                if not entries:
                    continue

                def body(e, entries=entries, eng_name=eng_name):
                    for fn, waits, incs in entries:
                        for (h, v) in waits:
                            key = (eng_name, h.num)
                            if waited.get(key, -1) < v:
                                e.wait_ge(h, v)
                                waited[key] = v
                        if fn is None:
                            continue
                        ins = fn(e)
                        for (h, a) in incs:
                            if a == 16:
                                ins.then_inc(h, a)
                            else:
                                e.maybe_drain_then_inc((h, a), fusable=True)

                getattr(block, eng_name)(body)


def build_program(nt):
    S = nt * TT
    nc = bass.Bass("TRN2", target_bir_lowering=False, detect_race_conditions=False)

    x_d = nc.dram_tensor("x", [S, D], F32, kind="ExternalInput").ap()
    y_d = nc.dram_tensor("y", [S, D], F32, kind="ExternalOutput").ap()
    wq_d = nc.dram_tensor("wq", [D, D], F32, kind="ExternalInput").ap()
    wk_d = nc.dram_tensor("wk", [D, D], F32, kind="ExternalInput").ap()
    wv_d = nc.dram_tensor("wv", [D, D], F32, kind="ExternalInput").ap()
    wg_d = nc.dram_tensor("wg", [D, D], F32, kind="ExternalInput").ap()
    wo_d = nc.dram_tensor("wo", [D, D], F32, kind="ExternalInput").ap()
    bq_d = nc.dram_tensor("bq", [D], F32, kind="ExternalInput").ap()
    bk_d = nc.dram_tensor("bk", [D], F32, kind="ExternalInput").ap()
    bv_d = nc.dram_tensor("bv", [D], F32, kind="ExternalInput").ap()
    bg_d = nc.dram_tensor("bg", [D], F32, kind="ExternalInput").ap()
    bo_d = nc.dram_tensor("bo", [D], F32, kind="ExternalInput").ap()
    es_d = nc.dram_tensor("es", [E16, D], F32, kind="ExternalInput").ap()
    keys_d = nc.dram_tensor("keys", [E16, D, 128], F32, kind="ExternalInput").ap()
    vals_d = nc.dram_tensor("vals", [E16, 128, D], F32, kind="ExternalInput").ap()
    x1_d = nc.dram_tensor("x1s", [S, D], F32, kind="Internal").ap()
    ident_d = nc.inline_tensor(np.eye(128, dtype=ml_dtypes.bfloat16), name="ident_dr")

    B = Builder(nc)
    sIN = Sem(nc.alloc_semaphore("sIN"), "sIN")
    sOUT = Sem(nc.alloc_semaphore("sOUT"), "sOUT")
    sT = Sem(nc.alloc_semaphore("sT"), "sT")
    sV = Sem(nc.alloc_semaphore("sV"), "sV")
    sS = Sem(nc.alloc_semaphore("sS"), "sS")
    sG = Sem(nc.alloc_semaphore("sG"), "sG")
    sCFG = Sem(nc.alloc_semaphore("sCFG"), "sCFG")
    sWA = Sem(nc.alloc_semaphore("sWA"), "sWA")
    sWB = Sem(nc.alloc_semaphore("sWB"), "sWB")
    sWC = Sem(nc.alloc_semaphore("sWC"), "sWC")
    sWD = Sem(nc.alloc_semaphore("sWD"), "sWD")
    sWE = Sem(nc.alloc_semaphore("sWE"), "sWE")
    sWF = Sem(nc.alloc_semaphore("sWF"), "sWF")

    em = contextlib.ExitStack()
    E = em.enter_context
    ident = E(nc.sbuf_tensor("ident", [128, 128], BF16))
    ones = E(nc.sbuf_tensor("ones", [65, 128], BF16))
    brow_bo = E(nc.sbuf_tensor("brow_bo", [1, D], BF16))
    bcols = E(nc.sbuf_tensor("bcols", [128, 32], F32))
    keys_sb = E(nc.sbuf_tensor("keys_sb", [128, DC, 2048], BF16))
    vals_sb = E(nc.sbuf_tensor("vals_sb", [128, E16, D], BF16))
    es_sb = E(nc.sbuf_tensor("es_sb", [128, DC, E16], BF16))
    sa = E(nc.sbuf_tensor("sa", [128, NCH, D], F32))
    sb = E(nc.sbuf_tensor("sb", [128, NCH, D], F32))
    ln_tm = E(nc.sbuf_tensor("ln_tm", [128, 2, D], BF16))
    ln_fm = E(nc.sbuf_tensor("ln_fm", [128, DC, TT], BF16))
    sel_bf = E(nc.sbuf_tensor("sel_bf", [E16, TT], BF16))
    sel_row = E(nc.sbuf_tensor("sel_row", [65, 6 * TT], BF16))
    selb = E(nc.sbuf_tensor("selb", [128, 3, TT], BF16))
    stile = E(nc.sbuf_tensor("stile", [128, E16, TT], BF16))
    scr1 = E(nc.sbuf_tensor("scr1", [16, 512], F32))
    bnb0 = E(nc.sbuf_tensor("bnb0", [128, 8, 6], F32))
    st40 = E(nc.sbuf_tensor("st40", [128, 28], F32))
    bnb1 = E(nc.sbuf_tensor("bnb1", [128, 8, 6], F32))
    st41 = E(nc.sbuf_tensor("st41", [128, 28], F32))
    m1a = E(nc.sbuf_tensor("m1a", [128, NCH * nt], F32))
    r1a = E(nc.sbuf_tensor("r1a", [128, NCH * nt], F32))
    qt = E(nc.sbuf_tensor("qt", [128, DC, TT], BF16))
    th = E(nc.sbuf_tensor("th", [128, 2, TT], BF16))
    kvbd = E(nc.sbuf_tensor("kvbd", [128, DC, 128], BF16))
    pT1 = E(nc.psum_tensor("pT1", [128, 1024], BF16))
    pT2 = E(nc.psum_tensor("pT2", [128, 1024], BF16))
    pA = E(nc.psum_tensor("pA", [128, 512], F32))
    pB = E(nc.psum_tensor("pB", [128, 512], F32))
    pC = E(nc.psum_tensor("pC", [128, 512], F32))
    pD = E(nc.psum_tensor("pD", [128, 512], F32))

    ev = {}

    def mark(name, sem):
        ev[name] = (sem, sem.n)

    def w(name):
        return ev[name]

    def rec_startup():
        B.rec("scalar", lambda e: e.preload_act_table(AF.Exp))
        for c in range(NCH):
            B.rec("sync", lambda e, c=c: e.dma_start(out=sa[:, c, :], in_=x_d[c * 128:(c + 1) * 128, :]),
                  incs=[(sIN, 16)])
            mark(f"xin_0_{c}", sIN)
        B.rec("sync", lambda e: e.dma_start(out=ident[:], in_=ident_d.ap()), incs=[(sCFG, 16)])
        mark("ident_in", sCFG)
        B.rec("sync", lambda e: e.dma_start(out=bcols[:, 0:8], in_=bq_d.rearrange("(c p) -> p c", p=128), allow_slow_non_contiguous=True),
              incs=[(sCFG, 16)])
        B.rec("sync", lambda e: e.dma_start(out=bcols[:, 16:24], in_=bg_d.rearrange("(c p) -> p c", p=128), allow_slow_non_contiguous=True),
              incs=[(sCFG, 16)])
        B.rec("sync", lambda e: e.dma_start(out=stile.ap()[0:1, 0:4, :].rearrange('p a b -> p (a b)').bitcast(F32)[0:1, 0:D], in_=bo_d.unsqueeze(0)), incs=[(sCFG, 16)])
        mark("bo_in", sCFG)
        B.rec("vector", lambda e: e.memset(ones[:, :], 1.0))
        B.rec("vector", lambda e: e.tensor_scalar_mul(bcols[:, 8:16], bcols[:, 0:8], -1.0),
              waits=[w("bo_in")])
        B.rec("vector", lambda e: e.tensor_scalar_mul(bcols[:, 16:24], bcols[:, 16:24], 0.5))
        B.rec("vector", lambda e: e.tensor_scalar_mul(stile.ap()[0:1, 0:4, :].rearrange('p a b -> p (a b)').bitcast(F32)[0:1, 0:D], stile.ap()[0:1, 0:4, :].rearrange('p a b -> p (a b)').bitcast(F32)[0:1, 0:D], 2.0))
        B.rec("vector", lambda e: e.tensor_copy(brow_bo[0:1, 0:D], stile.ap()[0:1, 0:4, :].rearrange('p a b -> p (a b)').bitcast(F32)[0:1, 0:D]), incs=[(sV, 1)])
        mark("cfg_done", sV)
        mark("ones_done", sV)
        mark("bcols_done", sV)
        for dc in range(DC):
            B.rec("gpsimd", lambda e, dc=dc: e.dma_start(
                out=keys_sb[:, dc, :],
                in_=keys_d[:, dc * 128:(dc + 1) * 128, :].transpose([1, 0, 2])),
                incs=[(sWA, 16)])
        mark("keys_done", sWA)
        for dc in range(DC):
            B.rec("gpsimd", lambda e, dc=dc: e.dma_start(
                out=es_sb[:, dc, :],
                in_=es_d[:, dc * 128:(dc + 1) * 128].transpose([1, 0]),
                allow_slow_non_contiguous=True),
                incs=[(sWA, 16)])
        mark("es_done", sWA)
        for e_ in range(E16):
            B.rec("gpsimd", lambda e, e_=e_: e.dma_start(
                out=vals_sb[:, e_, :], in_=vals_d[e_]), incs=[(sWB, 16)])
        mark("vals_done", sWB)
        for name, wd, wsb in (("wk", wk_d, wk_sb), ("wv", wv_d, wv_sb)):
            for dc in range(DC):
                B.rec("gpsimd", lambda e, wd=wd, dc=dc, wsb=wsb: e.dma_start(
                    out=wsb[:, dc, :], in_=wd[dc * 128:(dc + 1) * 128, :]),
                    incs=[(sWC, 16)])
        B.rec("gpsimd", lambda e: e.dma_start(out=brows_kv[0:1, 0:D], in_=bk_d.unsqueeze(0)),
              incs=[(sWC, 16)])
        B.rec("gpsimd", lambda e: e.dma_start(out=brows_kv[0:1, D:2 * D], in_=bv_d.unsqueeze(0)),
              incs=[(sWC, 16)])
        mark("wkv_done", sWC)

    def rec_bn(src, c, bufs, waits=()):
        bnb, g = bufs
        B.rec("vector", lambda e, c=c: e.bn_stats(
            bnb[:, 2 * c, :], src[:, c, 0:512]), waits=list(waits))
        B.rec("vector", lambda e, c=c: e.bn_stats(
            bnb[:, 2 * c + 1, :], src[:, c, 512:1024]))

    def rec_ln_finish(t, which, bufs):
        bnb, g = bufs
        V = "vector"

        def RD(fn):
            B.rec(V, fn)
            B.rec(V, lambda e: e.drain())

        B.rec(V, lambda e: e.drain())
        for c in range(NCH):
            B.rec(V, lambda e, c=c: e.bn_aggr(g[:, 2 * c:2 * c + 2], bnb[:, 2 * c:2 * c + 2, :]))
        B.rec(V, lambda e: e.drain())
        mvv = g.ap()[:, 1:8:2]
        us = g.ap()[:, 12:16]
        Bs = g.ap()[:, 16:20]
        Cs = g.ap()[:, 20:24]
        ys = g.ap()[:, 24:28]
        rs = g.ap()[:, 8:12]
        RD(lambda e: e.tensor_scalar_add(us, mvv, 1e-5))
        RD(lambda e: e.tensor_scalar(Bs.bitcast(I32), us.bitcast(I32), 1, None, ALU.arith_shift_right))
        RD(lambda e: e.tensor_scalar(Cs.bitcast(I32), Bs.bitcast(I32), -1, 0x5F3759DF, ALU.mult, ALU.add))
        for it in range(3):
            yin = Cs if it % 2 == 0 else ys
            yout = ys if it % 2 == 0 else Cs
            last = it == 2
            RD(lambda e, yin=yin: e.tensor_tensor(Bs, yin, yin, ALU.mult))
            RD(lambda e: e.tensor_tensor(rs, Bs, us, ALU.mult))
            RD(lambda e: e.tensor_scalar(Bs, rs, -0.5, 1.5, ALU.mult, ALU.add))
            RD(lambda e, yin=yin, yout=(rs if last else yout): e.tensor_tensor(yout, yin, Bs, ALU.mult))
        if which == "ln1":
            B.rec(V, lambda e, t=t: e.tensor_copy(m1a[:, NCH * t:NCH * (t + 1)], g.ap()[:, 0:8:2]))
            B.rec(V, lambda e, t=t: e.tensor_copy(r1a[:, NCH * t:NCH * (t + 1)], rs))
            B.rec(V, lambda e: e.drain())

    def rec_ln_finish_pair(t, which, bufs, p):
        bnb, g = bufs
        V = "vector"

        def RD(fn):
            B.rec(V, fn)
            B.rec(V, lambda e: e.drain())

        B.rec(V, lambda e: e.drain())
        for c in (2 * p, 2 * p + 1):
            B.rec(V, lambda e, c=c: e.bn_aggr(g[:, 2 * c:2 * c + 2], bnb[:, 2 * c:2 * c + 2, :]))
        B.rec(V, lambda e: e.drain())
        o = 2 * p
        mvv = g.ap()[:, 1 + 4 * p:8:2][:, 0:2]
        us = g.ap()[:, 12 + o:14 + o]
        Bs = g.ap()[:, 16 + o:18 + o]
        Cs = g.ap()[:, 20 + o:22 + o]
        ys = g.ap()[:, 24 + o:26 + o]
        rs = g.ap()[:, 8 + o:10 + o]
        RD(lambda e: e.tensor_scalar_add(us, mvv, 1e-5))
        RD(lambda e: e.tensor_scalar(Bs.bitcast(I32), us.bitcast(I32), 1, None, ALU.arith_shift_right))
        RD(lambda e: e.tensor_scalar(Cs.bitcast(I32), Bs.bitcast(I32), -1, 0x5F3759DF, ALU.mult, ALU.add))
        for it in range(3):
            yin = Cs if it % 2 == 0 else ys
            yout = ys if it % 2 == 0 else Cs
            last = it == 2
            RD(lambda e, yin=yin: e.tensor_tensor(Bs, yin, yin, ALU.mult))
            RD(lambda e: e.tensor_tensor(rs, Bs, us, ALU.mult))
            RD(lambda e: e.tensor_scalar(Bs, rs, -0.5, 1.5, ALU.mult, ALU.add))
            RD(lambda e, yin=yin, yout=(rs if last else yout): e.tensor_tensor(yout, yin, Bs, ALU.mult))
        if which == "ln1":
            B.rec(V, lambda e, t=t, p=p: e.tensor_copy(
                m1a[:, NCH * t + 2 * p:NCH * t + 2 * p + 2], g.ap()[:, 4 * p:4 * p + 4:2]))
            B.rec(V, lambda e, t=t, p=p: e.tensor_copy(
                r1a[:, NCH * t + 2 * p:NCH * t + 2 * p + 2], rs))
            B.rec(V, lambda e: e.drain())

    def rec_ln_apply_T_pair(t, src, which, g, p):
        for c in (2 * p, 2 * p + 1):
            mc, rc = g.ap()[:, 2 * c:2 * c + 1], g.ap()[:, 8 + c:9 + c]
            awaits = []
            if ev.get(f"ln_tm_free_{c % 2}"):
                awaits.append(ev[f"ln_tm_free_{c % 2}"])
            B.rec("vector", lambda e, c=c, mc=mc, rc=rc: e.tensor_scalar(
                ln_tm[:, c % 2, :], src[:, c, :], mc, rc, ALU.subtract, ALU.mult),
                waits=awaits, incs=[(sV, 1)])
            mark(f"{which}_tm_{t}_{c}", sV)
        for c in (2 * p, 2 * p + 1):
            pT = pT1 if c % 2 == 0 else pT2
            waits = [w(f"{which}_tm_{t}_{c}"), w("ident_in")]
            if ev.get(f"Tevac_prev_{c % 2}"):
                waits.append(ev[f"Tevac_prev_{c % 2}"])
            if ev.get("pT_busy"):
                waits.append(ev["pT_busy"])
            for j in range(DC):
                B.rec("tensor", lambda e, c=c, j=j, pT=pT: e.transpose(
                    pT[:, j * 128:(j + 1) * 128], ln_tm[:, c % 2, j * 128:(j + 1) * 128], ident[:]),
                    waits=waits if j == 0 else [], incs=[(sT, 1)] if j == DC - 1 else [])
            mark(f"{which}_T_{t}_{c}", sT)
            ev[f"ln_tm_free_{c % 2}"] = (sT, sT.n)
            B.rec("scalar", lambda e, c=c, pT=pT: e.copy(
                ln_fm.ap()[:, :, c * 128:(c + 1) * 128],
                pT.ap().rearrange("p (j k) -> p j k", j=DC)),
                waits=[w(f"{which}_T_{t}_{c}")], incs=[(sS, 1)])
            mark(f"{which}_fm_{t}_{c}", sS)
            ev[f"Tevac_prev_{c % 2}"] = (sS, sS.n)
        if p == 1:
            mark(f"{which}_fm_{t}", sS)

    def rec_ln_apply_transpose(t, src, which, mcol=None, rcol=None, g=None):
        for c in range(NCH):
            if mcol is None:
                mc, rc = g.ap()[:, 2 * c:2 * c + 1], g.ap()[:, 8 + c:9 + c]
            else:
                mc, rc = mcol(c), rcol(c)
            awaits = []
            if ev.get(f"ln_tm_free_{c % 2}"):
                awaits.append(ev[f"ln_tm_free_{c % 2}"])
            B.rec("vector", lambda e, c=c, mc=mc, rc=rc: e.tensor_scalar(
                ln_tm[:, c % 2, :], src[:, c, :], mc, rc, ALU.subtract, ALU.mult),
                waits=awaits, incs=[(sV, 1)])
            mark(f"{which}_tm_{t}_{c}", sV)
            pT = pT1 if c % 2 == 0 else pT2
            waits = [w(f"{which}_tm_{t}_{c}"), w("ident_in")]
            if ev.get(f"Tevac_prev_{c % 2}"):
                waits.append(ev[f"Tevac_prev_{c % 2}"])
            if ev.get("pT_busy"):
                waits.append(ev["pT_busy"])
            for j in range(DC):
                B.rec("tensor", lambda e, c=c, j=j, pT=pT: e.transpose(
                    pT[:, j * 128:(j + 1) * 128], ln_tm[:, c % 2, j * 128:(j + 1) * 128], ident[:]),
                    waits=waits if j == 0 else [], incs=[(sT, 1)] if j == DC - 1 else [])
            mark(f"{which}_T_{t}_{c}", sT)
            ev[f"ln_tm_free_{c % 2}"] = (sT, sT.n)
            B.rec("scalar", lambda e, c=c, pT=pT: e.copy(
                ln_fm.ap()[:, :, c * 128:(c + 1) * 128],
                pT.ap().rearrange("p (j k) -> p j k", j=DC)),
                waits=[w(f"{which}_T_{t}_{c}")], incs=[(sS, 1)])
            mark(f"{which}_fm_{t}_{c}", sS)
            ev[f"Tevac_prev_{c % 2}"] = (sS, sS.n)
        mark(f"{which}_fm_{t}", sS)

    def rec_ffn(t, which, lnname, resid_src, out_dst, pass_no, post_evac=None):
        waits = [w(f"{lnname}_fm_{t}"), w("es_done"), w("ident_in")]
        if ev.get("pa_free"):
            waits.append(ev["pa_free"])
        for dc in range(DC):
            B.rec("tensor", lambda e, dc=dc: e.matmul(
                pA[0:16, :], es_sb[:, dc, :], ln_fm[:, dc, :],
                start=(dc == 0), stop=(dc == DC - 1)),
                waits=waits if dc == 0 else [], incs=[(sT, 1)] if dc == DC - 1 else [])
        mark(f"sel_mm_{which}_{t}", sT)
        B.rec("scalar", lambda e: e.activation(scr1[0:16, 0:TT], pA[0:16, :], AF.Tanh, scale=0.5),
              waits=[w(f"sel_mm_{which}_{t}")], incs=[(sS, 1)])
        mark(f"sel_th_{which}_{t}", sS)
        B.rec("vector", lambda e: e.tensor_scalar(
            sel_bf[:, :], scr1[0:16, 0:TT], 0.5, 0.5, ALU.mult, ALU.add),
            waits=[w(f"sel_th_{which}_{t}")], incs=[(sV, 1)])
        mark(f"sel_{which}_{t}", sV)
        srw = [w(f"sel_{which}_{t}")]
        if ev.get("sel_row_free"):
            srw.append(ev["sel_row_free"])
        B.rec("gpsimd", lambda e: e.dma_start(out=sel_row[0:1, 0:6 * TT], in_=sel_bf[0:6, :]),
              waits=srw, incs=[(sOUT, 16)])
        B.rec("gpsimd", lambda e: e.dma_start(out=sel_row[32:33, 0:6 * TT], in_=sel_bf[6:12, :]),
              incs=[(sOUT, 16)])
        B.rec("gpsimd", lambda e: e.dma_start(out=sel_row[64:65, 0:4 * TT], in_=sel_bf[12:16, :]),
              incs=[(sOUT, 16)])
        mark(f"selrow_{which}_{t}", sOUT)
        nb = 2 if pass_no == 1 else 4
        banks = (pA, pB) if pass_no == 1 else (pA, pB, pE, pF)
        for e_ in range(E16):
            par = e_ % 2
            spar = e_ % 3
            pSB = (pT1 if par == 0 else pT2).ap().bitcast(F32)[:, 0:512]
            sbs = selb.ap()[:, spar, :]
            waits = [w(f"selrow_{which}_{t}"), w("ones_done")]
            if ev.get(f"selb_evac_{par}"):
                waits.append(ev[f"selb_evac_{par}"])
            sp = 32 * (e_ // 6) if e_ < 12 else 64
            so = (e_ % 6) * TT if e_ < 12 else (e_ - 12) * TT
            pSC = banks[e_ % nb]

            def rec_selb_blk():
                B.rec("tensor", lambda e, pSB=pSB, sp=sp, so=so: e.matmul(
                    pSB, ones[sp:sp + 1, :], sel_row[sp:sp + 1, so:so + TT],
                    start=True, stop=True),
                    waits=waits, incs=[(sT, 1)])
                mark(f"selb_mm_{which}_{t}_{e_}", sT)
                ewaits = [w(f"selb_mm_{which}_{t}_{e_}")]
                if e_ >= 3:
                    ewaits.append(w(f"stile_{which}_{t}_{e_ - 3}"))
                B.rec("scalar", lambda e, pSB=pSB, sbs=sbs: e.copy(sbs, pSB),
                      waits=ewaits, incs=[(sS, 1)])
                mark(f"selb_{which}_{t}_{e_}", sS)
                ev[f"selb_evac_{par}"] = (sS, sS.n)

            def rec_scmm_blk():
                swaits = []
                if e_ < 2:
                    swaits.append(w(f"sel_th_{which}_{t}"))
                if pass_no == 2 and e_ in (2, 3) and ev.get("pef_free"):
                    swaits.append(ev["pef_free"])
                if e_ >= nb:
                    swaits.append(w(f"stile_{which}_{t}_{e_ - nb}"))
                if pass_no == 1 and t == 0 and e_ == 0:
                    swaits.append(w("keys_done"))
                for dc in range(DC):
                    B.rec("tensor", lambda e, e_=e_, dc=dc, pSC=pSC: e.matmul(
                        pSC[:], keys_sb[:, dc, e_ * 128:(e_ + 1) * 128], ln_fm[:, dc, :],
                        start=(dc == 0), stop=(dc == DC - 1)),
                        waits=swaits if dc == 0 else [], incs=[(sT, 1)] if dc == DC - 1 else [])
                mark(f"sc_mm_{which}_{t}_{e_}", sT)

            if e_ == 0:
                rec_scmm_blk()
                rec_selb_blk()
            else:
                rec_selb_blk()
                rec_scmm_blk()
            B.rec("vector", lambda e, e_=e_, pSC=pSC, sbs=sbs: e.scalar_tensor_tensor(
                stile[:, e_, :], pSC[:], 0.0, sbs, ALU.max, ALU.mult),
                waits=[w(f"sc_mm_{which}_{t}_{e_}"), w(f"selb_{which}_{t}_{e_}")],
                incs=[(sV, 1)])
            mark(f"stile_{which}_{t}_{e_}", sV)
        ev["pT_busy"] = (sS, ev[f"selb_{which}_{t}_{E16 - 1}"][1])
        ev["sel_row_free"] = (sT, ev[f"selb_mm_{which}_{t}_{E16 - 1}"][1])
        ev["pa_free"] = (sV, ev[f"stile_{which}_{t}_{E16 - 1}"][1])
        for c in range(NCH):
            if pass_no == 1 or c % 2 == 0:
                vb = (pC, pD)
            else:
                vb = (pE, pF)
            waits = [w(f"stile_{which}_{t}_{E16 - 1}")]
            if pass_no == 1 and t == 0 and c == 0:
                waits.append(w("vals_done"))
            if pass_no == 1 or c < 2:
                if ev.get("pv_free"):
                    waits.append(ev["pv_free"])
            else:
                waits.append(ev[f"pvv_free_{c % 2}"])
            for e_ in range(E16):
                for gg in range(2):
                    pV = vb[gg]
                    B.rec("tensor", lambda e, c=c, e_=e_, gg=gg, pV=pV: e.matmul(
                        pV[:], stile[:, e_, c * 128:(c + 1) * 128],
                        vals_sb[:, e_, gg * 512:(gg + 1) * 512],
                        start=(e_ == 0), stop=(e_ == E16 - 1)),
                        waits=waits if (e_ == 0 and gg == 0) else [],
                        incs=[(sT, 1)] if (e_ == E16 - 1 and gg == 1) else [])
            mark(f"ffnv_{which}_{t}_{c}", sT)
            ewaits = [w(f"ffnv_{which}_{t}_{c}")]
            for gg in range(2):
                pV = vb[gg]
                B.rec("vector", lambda e, c=c, gg=gg, pV=pV: e.scalar_tensor_tensor(
                    out_dst[:, c, gg * 512:(gg + 1) * 512], pV[:], 0.5,
                    resid_src[:, c, gg * 512:(gg + 1) * 512], ALU.mult, ALU.add),
                    waits=ewaits,
                    incs=[(sV, 1)] if gg == 1 else [])
            mark(f"{which}_out_{t}_{c}", sV)
            ev["pv_free"] = (sV, sV.n)
            ev[f"pvv_free_{c % 2}"] = (sV, sV.n)
            if post_evac is not None:
                post_evac(c)

    def rec_pass1_tile(t):
        if t == 0:
            for c in range(NCH):
                rec_bn(sa, c, (bnb0, st40), waits=[w(f"xin_0_{c}")])
            rec_ln_finish(0, "ln0", (bnb0, st40))
        rec_ln_apply_transpose(t, sa, "ln0", g=st40)
        def p1_post_evac(c):
            rec_bn(sb, c, (bnb1, st41))
            if c == 1:
                rec_ln_finish_pair(t, "ln1", (bnb1, st41), 0)
            if c == 2:
                rec_ln_apply_T_pair(t, sb, "ln1", st41, 0)
            if c == 3:
                rec_ln_finish_pair(t, "ln1", (bnb1, st41), 1)

        rec_ffn(t, "ffn1", "ln0", sa, sb, 1, post_evac=p1_post_evac)
        for c in range(NCH):
            B.rec("gpsimd", lambda e, t=t, c=c: e.dma_start(
                out=x1_d[(t * NCH + c) * 128:(t * NCH + c + 1) * 128, :], in_=sb[:, c, :]),
                waits=[w(f"ffn1_out_{t}_{c}")], incs=[(sOUT, 16)])
            mark(f"x1st_{t}_{c}", sOUT)
        if t > 0:
            B.rec("vector", None, waits=[w(f"x1st_{t - 1}_{NCH - 1}")])
        for c in range(NCH):
            if c == 2:
                rec_ln_apply_T_pair(t, sb, "ln1", st41, 1)
            waits = [w(f"ln1_fm_{t}_{c}"), w("wkv_done"),
                     w("ones_done"), ev["pa_free"], ev["pv_free"]]
            lnc = ln_fm.ap()[:, :, c * 128:(c + 1) * 128]
            for dc in range(DC):
                for g_, pX, wsb in ((0, pA, wk_sb), (1, pB, wk_sb), (2, pC, wv_sb), (3, pD, wv_sb)):
                    gg = g_ % 2
                    B.rec("tensor", lambda e, dc=dc, pX=pX, wsb=wsb, gg=gg, lnc=lnc: e.matmul(
                        pX[:], lnc[:, dc, :], wsb[:, dc, gg * 512:(gg + 1) * 512],
                        start=(dc == 0), stop=False),
                        waits=waits if (dc == 0 and g_ == 0) else [])
            for g_, pX, ro in ((0, pA, 0), (1, pB, 512), (2, pC, D), (3, pD, D + 512)):
                B.rec("tensor", lambda e, pX=pX, ro=ro: e.matmul(
                    pX[:], ones[0:1, :], brows_kv[0:1, ro:ro + 512], start=False, stop=True),
                    incs=[(sT, 1)] if g_ == 3 else [])
            mark(f"kvproj_{t}_{c}", sT)
            par = c % 2
            for gg, pX in ((0, pC), (1, pD)):
                B.rec("vector", lambda e, gg=gg, pX=pX, par=par: e.tensor_copy(
                    vtil[:, par, gg * 512:(gg + 1) * 512], pX[:]),
                    waits=[w(f"kvproj_{t}_{c}")] if gg == 0 else [])
            for gg, pX in ((0, pA), (1, pB)):
                B.rec("vector", lambda e, gg=gg, pX=pX, par=par: e.tensor_scalar_max(
                    ktil[:, par, gg * 512:(gg + 1) * 512], pX[:], 0.0))
                kw = []
                if t * NCH + c >= 2:
                    pt, pc = divmod(t * NCH + c - 2, NCH)
                    kw = [w(f"kexp_{pt}_{pc}")]
                B.rec("vector", lambda e, gg=gg, pX=pX, par=par: e.tensor_scalar_min(
                    scr2k[:, par, gg * 512:(gg + 1) * 512], pX[:], 0.0),
                    waits=kw if gg == 0 else [],
                    incs=[(sV, 1)] if gg == 1 else [])
            mark(f"kv_free_{t}_{c}", sV)
            ev["pa_free"] = (sV, sV.n)
            ev["pv_free"] = (sV, sV.n)
            B.rec("scalar", lambda e, par=par: e.activation(
                scr2k[:, par, :], scr2k[:, par, :], AF.Exp),
                waits=[w(f"kv_free_{t}_{c}")], incs=[(sS, 1)])
            mark(f"kexp_{t}_{c}", sS)
            B.rec("vector", lambda e, par=par: e.tensor_tensor(
                ktil[:, par, :], ktil[:, par, :], scr2k[:, par, :], ALU.add),
                waits=[w(f"kexp_{t}_{c}")], incs=[(sV, 1)])
            mark(f"kv_in_{t}_{c}", sV)

            def rec_kvmm(tc):
                tt_, cc_ = divmod(tc, NCH)
                par2 = cc_ % 2
                first = tc == 0
                last = tc == nt * NCH - 1
                for h in range(E16):
                    B.rec("tensor", lambda e, h=h, first=first, last=last, par2=par2: e.matmul(
                        pkv[0:64, h * 64:(h + 1) * 64], ktil[:, par2, h * 64:(h + 1) * 64],
                        vtil[:, par2, h * 64:(h + 1) * 64],
                        start=(first and h % 8 == 0), stop=(last and h % 8 == 7),
                        skip_group_check=True),
                        waits=[w(f"kv_in_{tt_}_{cc_}")] if h == 0 else [],
                        incs=[(sT, 1)] if h == E16 - 1 else [])
                mark(f"kvacc_{tt_}_{cc_}", sT)

            tc = t * NCH + c
            if tc >= 1:
                rec_kvmm(tc - 1)
            if tc == nt * NCH - 1:
                rec_kvmm(tc)
        if t + 1 < nt:
            for c in range(NCH):
                B.rec("sync", lambda e, t=t, c=c: e.dma_start(
                    out=sa[:, c, :], in_=x_d[((t + 1) * NCH + c) * 128:((t + 1) * NCH + c + 1) * 128, :]),
                    waits=[w(f"ffn1_out_{t}_{c}")], incs=[(sIN, 16)])
                mark(f"xin_{t + 1}_{c}", sIN)
            for c in range(NCH):
                rec_bn(sa, c, (bnb0, st40), waits=[w(f"xin_{t + 1}_{c}")])
            rec_ln_finish(t + 1, "ln0", (bnb0, st40))

    def rec_kv_finalize():
        B.rec("vector", lambda e: e.tensor_copy(kvbf[:, :], pkv[:, :]),
              waits=[w(f"kvacc_{nt - 1}_{NCH - 1}")])
        B.rec("vector", lambda e: e.memset(kvbd[:, :, :], 0.0), incs=[(sV, 1)])
        mark("kvbf", sV)
        for cc in range(DC):
            B.rec("gpsimd", lambda e, cc=cc: e.dma_start(
                out=kvbd[0:64, cc, 0:64], in_=kvbf[0:64, cc * 128:cc * 128 + 64]),
                waits=[w("kvbf")], incs=[(sOUT, 16)])
            B.rec("gpsimd", lambda e, cc=cc: e.dma_start(
                out=kvbd[64:128, cc, 64:128], in_=kvbf[0:64, cc * 128 + 64:cc * 128 + 128]),
                incs=[(sOUT, 16)])
        mark("kvbd", sOUT)

    def rec_phase2_weights():
        kvp = w(f"kvproj_{nt - 1}_{NCH - 1}")
        kva = w(f"kvacc_{nt - 1}_{NCH - 1}")
        for i, (wd, wsb, sw) in enumerate(((wq_d, wq_sb, sWD), (wg_d, wg_sb, sWE), (wo_d, wo_sb, sWF))):
            for dc in range(DC):
                waits = []
                if dc == 0:
                    waits = [kvp] if i < 2 else [kva, w("kvbd")]
                B.rec("gpsimd", lambda e, wd=wd, dc=dc, wsb=wsb: e.dma_start(
                    out=wsb[:, dc, :], in_=wd[dc * 128:(dc + 1) * 128, :]),
                    waits=waits, incs=[(sw, 16)])
            mark(f"p2w_{i}", sw)

    def rec_pass2_tile(t):
        q2 = stile.ap()[:, 0:4, :].rearrange("p a b -> p (a b)").bitcast(F32)
        q3 = stile.ap()[:, 4:8, :].rearrange("p a b -> p (a b)").bitcast(F32)
        for c in range(NCH):
            if t == 0:
                waits = [w(f"x1st_{nt - 1}_{NCH - 1}")]
            else:
                waits = [w(f"yst_{t - 1}_{c}")]
            B.rec("sync", lambda e, t=t, c=c: e.dma_start(
                out=sa[:, c, :], in_=x1_d[(t * NCH + c) * 128:(t * NCH + c + 1) * 128, :]),
                waits=waits, incs=[(sIN, 16)])
            mark(f"x1in_{t}_{c}", sIN)
        B.rec("vector", None, waits=[w(f"x1in_{t}_{NCH - 1}")])
        rec_ln_apply_transpose(t, sa, "ln1b",
                               mcol=lambda c: m1a.ap()[:, NCH * t + c:NCH * t + c + 1],
                               rcol=lambda c: r1a.ap()[:, NCH * t + c:NCH * t + c + 1])
        for fc in range(DC):
            par = fc % 2
            pQ = pA if par == 0 else pB
            pG = pC if par == 0 else pD
            qwaits = [w(f"ln1b_fm_{t}"), w("p2w_0"), w("bcols_done")]
            if ev.get(f"pq_free_{par}"):
                qwaits.append(ev[f"pq_free_{par}"])
            for dc in range(DC):
                B.rec("tensor", lambda e, fc=fc, dc=dc, pQ=pQ: e.matmul(
                    pQ[:], wq_sb[:, dc, fc * 128:(fc + 1) * 128], ln_fm[:, dc, :],
                    start=(dc == 0), stop=(dc == DC - 1)),
                    waits=qwaits if dc == 0 else [],
                    incs=[(sT, 1)] if dc == DC - 1 else [])
            mark(f"qmm_{t}_{fc}", sT)
            gwaits = [w("p2w_1")]
            if ev.get(f"pg_free_{par}"):
                gwaits.append(ev[f"pg_free_{par}"])
            for dc in range(DC):
                B.rec("tensor", lambda e, fc=fc, dc=dc, pG=pG: e.matmul(
                    pG[:], wg_sb[:, dc, fc * 128:(fc + 1) * 128], ln_fm[:, dc, :],
                    start=(dc == 0), stop=(dc == DC - 1)),
                    waits=gwaits if dc == 0 else [],
                    incs=[(sT, 1)] if dc == DC - 1 else [])
            mark(f"gmm_{t}_{fc}", sT)
            sh = par * 512
            B.rec("vector", lambda e, fc=fc, pQ=pQ, sh=sh: e.tensor_scalar(
                q2[:, sh:sh + 512], pQ[:], bcols.ap()[:, fc:fc + 1], 0.0,
                ALU.add, ALU.min),
                waits=[w(f"qmm_{t}_{fc}")])
            B.rec("vector", lambda e, fc=fc, pQ=pQ: e.tensor_scalar(
                qt[:, fc, :], pQ[:], bcols.ap()[:, fc:fc + 1], 0.0,
                ALU.add, ALU.max),
                incs=[(sV, 1)])
            mark(f"qpq_{t}_{fc}", sV)
            ewaits = [w(f"qpq_{t}_{fc}")]
            if fc >= 2:
                ewaits.append(w(f"attn_{t}_{fc - 2}"))
            B.rec("scalar", lambda e, sh=sh, par=par: e.activation(
                stile[:, par, :], q2[:, sh:sh + 512], AF.Exp),
                waits=ewaits, incs=[(sS, 1)])
            mark(f"qexp_{t}_{fc}", sS)
            twaits = [w(f"gmm_{t}_{fc}")]
            if ev.get(f"th_free_{par}"):
                twaits.append(ev[f"th_free_{par}"])
            B.rec("scalar", lambda e, fc=fc, pG=pG, par=par: e.activation(
                th[:, par, :], pG[:], AF.Tanh, scale=0.5,
                bias=bcols.ap()[:, 16 + fc:17 + fc]),
                waits=twaits, incs=[(sS, 1)])
            mark(f"th_{t}_{fc}", sS)
            ev[f"pg_free_{par}"] = (sS, sS.n)
            B.rec("tensor", lambda e, fc=fc, pQ=pQ: e.matmul(
                pQ[:], kvbd[:, fc, :], qt[:, fc, :], start=True, stop=False),
                waits=[w(f"qpq_{t}_{fc}"), w("kvbd")])
            B.rec("tensor", lambda e, fc=fc, pQ=pQ, par=par: e.matmul(
                pQ[:], kvbd[:, fc, :], stile[:, par, :], start=False, stop=True),
                waits=[w(f"qexp_{t}_{fc}")], incs=[(sT, 1)])
            mark(f"attn_{t}_{fc}", sT)
            B.rec("vector", lambda e, fc=fc, pQ=pQ, par=par: e.scalar_tensor_tensor(
                qt[:, fc, :], th[:, par, :], 1.0, pQ[:], ALU.add, ALU.mult),
                waits=[w(f"attn_{t}_{fc}"), w(f"th_{t}_{fc}")], incs=[(sV, 1)])
            mark(f"gt_{t}_{fc}", sV)
            ev[f"pq_free_{par}"] = (sV, sV.n)
            ev[f"th_free_{par}"] = (sV, sV.n)
        ev["pa_free"] = (sV, sV.n)
        for c in range(NCH):
            wb = (pC, pD) if c % 2 == 0 else (pE, pF)
            waits = [w(f"gt_{t}_{DC - 1}"), w("p2w_2")]
            if c >= 2:
                waits.append(w(f"x2_{t}_{c - 2}"))
            elif ev.get("pv_free"):
                waits.append(ev["pv_free"])
            for fc in range(DC):
                for gg in range(2):
                    B.rec("tensor", lambda e, c=c, fc=fc, gg=gg, pX=wb[gg]: e.matmul(
                        pX[:], qt[:, fc, c * 128:(c + 1) * 128],
                        wo_sb[:, fc, gg * 512:(gg + 1) * 512],
                        start=(fc == 0), stop=False),
                        waits=waits if (fc == 0 and gg == 0) else [])
            for gg in range(2):
                B.rec("tensor", lambda e, gg=gg, pX=wb[gg]: e.matmul(
                    pX[:], ones[0:1, :], brow_bo[0:1, gg * 512:(gg + 1) * 512],
                    start=False, stop=True),
                    incs=[(sT, 1)] if gg == 1 else [])
            mark(f"womm_{t}_{c}", sT)
            for gg in range(2):
                B.rec("vector", lambda e, c=c, gg=gg, pX=wb[gg]: e.scalar_tensor_tensor(
                    sb[:, c, gg * 512:(gg + 1) * 512], pX[:], 0.5,
                    sa[:, c, gg * 512:(gg + 1) * 512], ALU.mult, ALU.add),
                    waits=[w(f"womm_{t}_{c}")],
                    incs=[(sV, 1)] if gg == 1 else [])
            mark(f"x2_{t}_{c}", sV)
            ev["pv_free"] = (sV, sV.n)
            ev[f"pvv_free_{c % 2}"] = (sV, sV.n)
            rec_bn(sb, c, (bnb0, st40))
        ev["pef_free"] = w(f"x2_{t}_{NCH - 1}")
        rec_ln_finish(t, "ln2", (bnb0, st40))
        rec_ln_apply_transpose(t, sb, "ln2", g=st40)
        rec_ffn(t, "ffn2", "ln2", sb, sa, 2)
        for c in range(NCH):
            B.rec("gpsimd", lambda e, t=t, c=c: e.dma_start(
                out=y_d[(t * NCH + c) * 128:(t * NCH + c + 1) * 128, :], in_=sa[:, c, :]),
                waits=[w(f"ffn2_out_{t}_{c}")], incs=[(sOUT, 16)])
            mark(f"yst_{t}_{c}", sOUT)

    with contextlib.ExitStack() as p1:
        wk_sb = p1.enter_context(nc.sbuf_tensor("wk_sb", [128, DC, D], BF16))
        wv_sb = p1.enter_context(nc.sbuf_tensor("wv_sb", [128, DC, D], BF16))
        brows_kv = p1.enter_context(nc.sbuf_tensor("brows_kv", [1, 2 * D], BF16))
        scr2k = p1.enter_context(nc.sbuf_tensor("scr2k", [128, 2, D], F32))
        ktil = p1.enter_context(nc.sbuf_tensor("ktil", [128, 2, D], BF16))
        vtil = p1.enter_context(nc.sbuf_tensor("vtil", [128, 2, D], BF16))
        kvbf = p1.enter_context(nc.sbuf_tensor("kvbf", [64, 1024], BF16))
        pkv = p1.enter_context(nc.psum_tensor("pkv", [64, 1024], F32))
        rec_startup()
        for t in range(nt):
            rec_pass1_tile(t)
        rec_kv_finalize()
        B.flush()
    with contextlib.ExitStack() as p2:
        pE = p2.enter_context(nc.psum_tensor("pE", [128, 512], F32))
        pF = p2.enter_context(nc.psum_tensor("pF", [128, 512], F32))
        wq_sb = p2.enter_context(nc.sbuf_tensor("wq_sb", [128, DC, D], BF16))
        wg_sb = p2.enter_context(nc.sbuf_tensor("wg_sb", [128, DC, D], BF16))
        wo_sb = p2.enter_context(nc.sbuf_tensor("wo_sb", [128, DC, D], BF16))
        rec_phase2_weights()
        for t in range(nt):
            rec_pass2_tile(t)
        B.rec("gpsimd", lambda e: e.sem_inc(sG.h, 1),
              waits=[(sOUT, sOUT.n), (sIN, sIN.n)])
        B.flush()
    em.close()
    return nc


_PROG_CACHE = {}


def _get_prog(nt):
    if nt not in _PROG_CACHE:
        _PROG_CACHE[nt] = build_program(nt)
    return _PROG_CACHE[nt]


def kernel(L0_attn_x, L0_attn_Wq, L0_attn_bq, L0_attn_Wk, L0_attn_bk,
           L0_attn_Wv, L0_attn_bv, L0_attn_Wg, L0_attn_bg, L0_attn_Wo,
           L0_attn_bo, L0_ffn_expert_sel, L0_ffn_keys, L0_ffn_values):
    x = np.ascontiguousarray(np.asarray(L0_attn_x, dtype=np.float32))
    Bz, S, Dm = x.shape
    assert Dm == D and S % TT == 0
    nt = S // TT
    nc = _get_prog(nt)
    common = {
        "wq": np.ascontiguousarray(L0_attn_Wq, dtype=np.float32),
        "wk": np.ascontiguousarray(L0_attn_Wk, dtype=np.float32),
        "wv": np.ascontiguousarray(L0_attn_Wv, dtype=np.float32),
        "wg": np.ascontiguousarray(L0_attn_Wg, dtype=np.float32),
        "wo": np.ascontiguousarray(L0_attn_Wo, dtype=np.float32),
        "bq": np.ascontiguousarray(L0_attn_bq, dtype=np.float32),
        "bk": np.ascontiguousarray(L0_attn_bk, dtype=np.float32),
        "bv": np.ascontiguousarray(L0_attn_bv, dtype=np.float32),
        "bg": np.ascontiguousarray(L0_attn_bg, dtype=np.float32),
        "bo": np.ascontiguousarray(L0_attn_bo, dtype=np.float32),
        "es": np.ascontiguousarray(L0_ffn_expert_sel, dtype=np.float32),
        "keys": np.ascontiguousarray(L0_ffn_keys, dtype=np.float32),
        "vals": np.ascontiguousarray(L0_ffn_values, dtype=np.float32),
    }
    in_maps = [dict(common, x=np.ascontiguousarray(x[b])) for b in range(Bz)]
    res = run_bass_kernel_spmd(nc, in_maps, core_ids=list(range(Bz)))
    out = np.stack([np.asarray(r["y"], dtype=np.float32) for r in res.results], axis=0)
    return out

